# version 20
# baseline (speedup 1.0000x reference)
import numpy as np
import ml_dtypes
from types import SimpleNamespace
import concourse.bass as bass
import concourse.mybir as mybir
from concourse.bass_utils import run_bass_kernel_spmd

F32 = mybir.dt.float32
BF16 = mybir.dt.bfloat16
I32 = mybir.dt.int32
AF = mybir.ActivationFunctionType
ALU = mybir.AluOpType
AX = mybir.AxisListType

B, T, D = 2, 8192, 2048
NDC = D // 128
CH = 512
SC = 256
E1, E2 = 2560, 6656
OUTER_SLOTS = [0, 1, 2, 3, 4, 13, 14, 15]
MIDDLE_SLOTS = [5, 6, 7, 8, 9, 10, 11, 12]
KV_RANGE = {0: T, 1: E2}
LN_N = {0: 4096, 1: 3328}
ATTN_MULT = 1.0 / float(np.sqrt(128.0))
NEG_BIG = -1e30
EPS = 1e-5
PACK = 112
ENGS = ("sync", "tensor", "scalar", "vector", "gpsimd")
ATT_J = 8


class Sem:
    def __init__(self, h):
        self.h = h
        self.n = 0

    def inc(self, k):
        self.n += k
        return self.n


class Plan:
    def __init__(self):
        self.q = {e: [] for e in ENGS}

    def em(self, e, fn):
        self.q[e].append(("op", fn))

    def br(self, e, key, it0, it1):
        self.q[e].append(("br", key, it0, it1))

    def splice_role(self, pa, pb):
        for e in ENGS:
            self.q[e].append(("br", "role", pa.q[e], pb.q[e]))


def _play(e, items, regs):
    for it in items:
        if it[0] == "op":
            it[1](e)
        else:
            _, key, it0, it1 = it
            with e.If_eq(regs[key], 0):
                _play(e, it0, regs)
            with e.If_ne(regs[key], 0):
                _play(e, it1, regs)


def dma(C, P, eng, sem, o, i):
    c = sem.inc(16)
    P.em(eng, lambda e, o=o, i=i, s=sem.h: e.dma_start(out=o, in_=i).then_inc(s, 16))
    return c


def dma_br(C, P, eng, sem, o0, o1, i):
    c = sem.inc(16)
    it0 = [("op", lambda e, o=o0, i=i, s=sem.h: e.dma_start(out=o, in_=i).then_inc(s, 16))]
    it1 = [("op", lambda e, o=o1, i=i, s=sem.h: e.dma_start(out=o, in_=i).then_inc(s, 16))]
    P.br(eng, "memb", it0, it1)
    return c


def wt(C, P, eng, sem, val=None):
    v = sem.n if val is None else val
    if v > 0:
        P.em(eng, lambda e, s=sem.h, v=v: e.wait_ge(s, v))


def op(C, P, eng, fn, sem=None, k=1):
    if sem is None:
        P.em(eng, lambda e, f=fn: f(e))
        return None
    c = sem.inc(k)
    P.em(eng, lambda e, f=fn, s=sem.h, k=k: f(e).then_inc(s, k))
    return c


def vop(C, P, fn, sem=None):
    wt(C, P, "vector", C.VS)
    return op(C, P, "vector", fn, sem if sem is not None else C.VS)


def aop(C, P, fn, chain=True):
    if chain:
        wt(C, P, "scalar", C.AS)
    return op(C, P, "scalar", fn, C.AS)


def v2op(C, P, fn):
    wt(C, P, "vector", C.VS2)
    return op(C, P, "vector", fn, C.VS2)


def a2op(C, P, fn, chain=True):
    if chain:
        wt(C, P, "scalar", C.AS2)
    return op(C, P, "scalar", fn, C.AS2)


def make_context():
    nc = bass.Bass("TRN2", target_bir_lowering=False)
    C = SimpleNamespace()
    C.nc = nc
    C.P = Plan()

    def din(name, shape, dtype=F32):
        a = nc.dram_tensor(name, list(shape), dtype, kind="ExternalInput").ap()
        setattr(C, name, a)

    din("x_ln", (4096, D))
    din("x_res", (2048, D))
    din("wqkv", (D, 3072))
    din("bqkv", (1, 3072))
    din("wproj_half", (1024, D))
    din("wbase_half", (1024, D))
    din("bproj", (1, D))
    din("bbase", (1, D))
    din("a_s", (D, 16))
    din("b_s", (16, D))
    din("a_r", (6, D, 16))
    din("b_r", (6, 16, D))
    din("wrouter", (D, 6))
    din("brouter", (1, 6))
    din("rmt", (1, 4), I32)

    C.out = nc.dram_tensor("out", [2048, D], F32, kind="ExternalOutput").ap()
    if DEBUG_LEVEL:
        C.dbg = nc.dram_tensor("dbg", [2048, D], BF16, kind="ExternalOutput").ap()

    C.MY_WQKV = nc.dram_tensor("MY_WQKV", [D, 3072], BF16, kind="Internal").ap()
    C.D_AALL = nc.dram_tensor("D_AALL", [128, NDC * PACK], BF16, kind="Internal").ap()
    C.D_BALL = nc.dram_tensor("D_BALL", [PACK, D], BF16, kind="Internal").ap()
    C.D_WROUT = nc.dram_tensor("D_WROUT", [128, NDC * 8], BF16, kind="Internal").ap()
    C.D_BIAS = nc.dram_tensor("D_BIAS", [4, D], BF16, kind="Internal").ap()
    C.SH_XNT = nc.dram_tensor("SH_XNT", [D, T], BF16, kind="Internal", addr_space="Shared").ap()
    C.SH_O = nc.dram_tensor("SH_O", [4096, D], BF16, kind="Internal", addr_space="Shared").ap()
    C.SH_WP = nc.dram_tensor("SH_WP", [D, D], BF16, kind="Internal", addr_space="Shared").ap()
    C.SH_WB = nc.dram_tensor("SH_WB", [D, D], BF16, kind="Internal", addr_space="Shared").ap()
    C.SH_FLAGS = nc.dram_tensor("SH_FLAGS", [2, 4, 16], I32, kind="Internal", addr_space="Shared").ap()

    C.ident_bf = nc.inline_tensor(np.eye(128, dtype=ml_dtypes.bfloat16), "c_identbf").ap()
    C.ident_f32 = nc.inline_tensor(np.eye(128, dtype=np.float32), "c_identf").ap()
    C.ones_col = nc.inline_tensor(np.ones((128, 1), dtype=ml_dtypes.bfloat16), "c_onescol").ap()
    C.ones_row = nc.inline_tensor(np.ones((1, 128), dtype=ml_dtypes.bfloat16), "c_onesrow").ap()
    mk = np.zeros((4, 128, CH), dtype=ml_dtypes.bfloat16)
    for r in range(4):
        for k in range(128):
            lo = 128 * r + k
            if lo < CH:
                mk[r, k, lo:] = 1.0
    C.masks_c = nc.inline_tensor(np.ascontiguousarray(mk), "c_masks").ap()
    sel = np.zeros((8, PACK), dtype=ml_dtypes.bfloat16)
    for ei in range(7):
        sel[ei, 16 * ei:16 * ei + 16] = 1.0
    C.sel_c = nc.inline_tensor(sel, "c_sel").ap()

    C._sem_guards = []
    C._semnames = []

    def S(name):
        g = nc.semaphore(name)
        h = g.__enter__()
        C._sem_guards.append(g)
        s = Sem(h)
        setattr(C, name, s)
        C._semnames.append(name)

    for name in ("LD", "LDW", "STQ", "GQ", "PL", "VS", "AS", "TS", "EXPS",
                 "AVS", "NRM", "KVE", "BAR", "ARM",
                 "LD2", "ST2", "VS2", "AS2", "T2"):
        S(name)

    C._sb_guards = {}

    def sbuf(name, shape, dtype, offset=None):
        if offset is None:
            g = nc.sbuf_tensor(name, list(shape), dtype)
            h = g.__enter__()
            C._sb_guards[name] = g
        else:
            h = nc.alloc_sbuf_tensor_at(name, list(shape), dtype, offset=offset)
        setattr(C, name, h)
        return h

    def sbfree(name):
        g = C._sb_guards.pop(name, None)
        if g is not None:
            g.__exit__(None, None, None)

    C.sbuf = sbuf
    C.sbfree = sbfree
    sbuf("rmt_sb", (1, 4), I32)
    sbuf("eps_sb", (128, 1), F32)
    sbuf("idb_sb", (128, 128), BF16)
    sbuf("idf_sb", (128, 128), F32)
    sbuf("onc_sb", (128, 1), BF16)
    sbuf("onr_sb", (1, 128), BF16)
    sbuf("masks_sb", (128, 4 * CH), BF16)
    sbuf("bqkvc_sb", (128, 16), F32)
    sbuf("bvrow_sb", (1, 1024), BF16)
    sbuf("kn_sb", (1, 16), F32)
    sbuf("tiny_sb", (128, 16), F32)
    sbuf("bnst0_sb", (128, 24), F32)
    sbuf("tinyb_sb", (1, 8), BF16)
    sbuf("gz_sb", (1, 1), I32)
    sbuf("pf_sb", (1, 16), I32)

    C.PS6 = [nc.alloc_psum_tensor(f"ps6_{i}", [128, 512], F32) for i in range(6)]
    C.PAV = [nc.alloc_psum_tensor(f"pav{i}", [128, 512], F32) for i in range(2)]
    C.PBF = [C.PS6[4].bitcast(BF16), C.PS6[5].bitcast(BF16)]
    return C


def alloc_phase1(C):
    C.sbuf("K_sb", (128, 4 * T), BF16)
    C.sbuf("V_sb", (128, 64 * 516), BF16)
    C.sbuf("xs_sb", (128, 2 * NDC * SC), BF16)
    C.sbuf("wkv_sb", (128, NDC * 1024), BF16)
    C.sbuf("Q_sb", (128, 2 * 4 * CH), BF16)
    C.sbuf("Pt_sb", (128, 4 * CH), BF16)
    C.sbuf("Pm_sb", (128, 2 * CH), BF16)
    C.sbuf("Ost_sb", (128, 2048), BF16)
    C.sbuf("negb_sb", (128, 8), F32)
    C.sbuf("wq_sb", (128, NDC * 512), BF16, offset=C.nc.lookup_mloc(C.wkv_sb).addr)
    C.sbuf("sqq_sb", (128, 2 * CH), BF16)
    kof = C.nc.lookup_mloc(C.K_sb).addr
    vof = C.nc.lookup_mloc(C.V_sb).addr
    C.sbuf("x0_sb", (128, 2 * D), F32, offset=kof)
    C.sbuf("xn_sb", (128, D), BF16, offset=kof + 16384)
    C.sbuf("sq0_sb", (128, D), BF16, offset=kof + 16384 + 4096)
    C.sbuf("xnt_sb", (128, 2 * NDC * CH), BF16, offset=kof + 24576)
    C.sbuf("stg_sb", (128, 2048), F32, offset=vof)
    C.sbuf("stgb_sb", (128, 2048), BF16, offset=vof + 8192)


def plan_prolog(C):
    P = C.P
    C.LD.inc(16)
    C.c_rmt = C.LD.n

    dma(C, P, "sync", C.LD, C.idb_sb.ap(), C.ident_bf)
    dma(C, P, "sync", C.LD, C.idf_sb.ap(), C.ident_f32)
    dma(C, P, "sync", C.LD, C.onc_sb.ap(), C.ones_col)
    dma(C, P, "sync", C.LD, C.onr_sb.ap(), C.ones_row)
    for r in range(4):
        dma(C, P, "sync", C.LD, C.masks_sb.ap()[:, r * CH:(r + 1) * CH], C.masks_c[r])

    c_msets = vop(C, P, lambda e: e.memset(C.eps_sb.ap(), EPS))

    stg = C.stg_sb.ap()
    stgb = C.stgb_sb.ap()
    stgA = stg[:, 0:NDC * PACK].rearrange("p (c k) -> p c k", k=PACK)
    for ei in range(6):
        dma(C, P, "sync", C.LD, stgA[:, :, 16 * ei:16 * ei + 16],
            C.a_r[ei].rearrange("(c p) r -> p c r", p=128))
    c1 = dma(C, P, "sync", C.LD, stgA[:, :, 96:112], C.a_s.rearrange("(c p) r -> p c r", p=128))
    wt(C, P, "scalar", C.LD, c1)
    cA = aop(C, P, lambda e: e.copy(stgb[:, 0:NDC * PACK], stg[:, 0:NDC * PACK]))
    wt(C, P, "scalar", C.AS, cA)
    dma(C, P, "scalar", C.STQ, C.D_AALL, stgb[:, 0:NDC * PACK])
    wt(C, P, "sync", C.AS, cA)
    for ei in range(6):
        dma(C, P, "sync", C.LD, stg[16 * ei:16 * ei + 16, 0:D], C.b_r[ei])
    c1 = dma(C, P, "sync", C.LD, stg[96:112, 0:D], C.b_s)
    wt(C, P, "scalar", C.LD, c1)
    cB = aop(C, P, lambda e: e.copy(stgb[0:PACK, 0:D], stg[0:PACK, 0:D]))
    wt(C, P, "scalar", C.AS, cB)
    dma(C, P, "scalar", C.STQ, C.D_BALL, stgb[0:PACK, 0:D])
    wt(C, P, "sync", C.AS, cB)
    cz = vop(C, P, lambda e: e.memset(stgb[:, 0:NDC * 8], 0.0))
    stgR = stg[:, 0:NDC * 6].rearrange("p (c r) -> p c r", r=6)
    c1 = dma(C, P, "sync", C.LD, stgR, C.wrouter.rearrange("(c p) r -> p c r", p=128))
    wt(C, P, "scalar", C.LD, c1)
    wt(C, P, "scalar", C.VS, cz)
    cR = aop(C, P, lambda e: e.copy(
        stgb[:, 0:NDC * 8].rearrange("p (c r) -> p c r", r=8)[:, :, 0:6], stgR))
    wt(C, P, "scalar", C.AS, cR)
    dma(C, P, "scalar", C.STQ, C.D_WROUT, stgb[:, 0:NDC * 8])
    wt(C, P, "sync", C.AS, cR)
    c1 = dma(C, P, "sync", C.LD, stg[0:1, 0:D], C.bproj)
    wt(C, P, "scalar", C.LD, c1)
    cb = aop(C, P, lambda e: e.copy(stgb[0:1, 0:D], stg[0:1, 0:D]))
    wt(C, P, "scalar", C.AS, cb)
    cs1 = dma(C, P, "scalar", C.STQ, C.D_BIAS[0:1, :], stgb[0:1, 0:D])
    wt(C, P, "sync", C.AS, cb)
    c1 = dma(C, P, "sync", C.LD, stg[0:1, 0:D], C.bbase)
    wt(C, P, "scalar", C.LD, c1)
    wt(C, P, "scalar", C.STQ, cs1)
    cb = aop(C, P, lambda e: e.copy(stgb[0:1, 0:D], stg[0:1, 0:D]))
    wt(C, P, "scalar", C.AS, cb)
    cs1 = dma(C, P, "scalar", C.STQ, C.D_BIAS[1:2, :], stgb[0:1, 0:D])
    wt(C, P, "sync", C.AS, cb)
    c1 = dma(C, P, "sync", C.LD, stg[0:1, 0:6], C.brouter)
    wt(C, P, "vector", C.STQ, cs1)
    cbm = vop(C, P, lambda e: e.memset(stgb[0:1, 0:8], NEG_BIG))
    wt(C, P, "scalar", C.LD, c1)
    wt(C, P, "scalar", C.VS, cbm)
    wt(C, P, "scalar", C.STQ, cs1)
    cb = aop(C, P, lambda e: e.copy(stgb[0:1, 0:6], stg[0:1, 0:6]))
    wt(C, P, "scalar", C.AS, cb)
    cs1 = dma(C, P, "scalar", C.STQ, C.D_BIAS[2:3, 0:8], stgb[0:1, 0:8])
    wt(C, P, "sync", C.AS, cb)
    c1 = dma(C, P, "sync", C.LD, stg[0:1, 0:1024], C.bqkv[0:1, 2048:3072])
    wt(C, P, "scalar", C.LD, c1)
    aop(C, P, lambda e: e.copy(C.bvrow_sb.ap(), stg[0:1, 0:1024]))
    C.c_smalls_stq = C.STQ.n
    wt(C, P, "sync", C.AS, C.AS.n)
    c1 = dma(C, P, "sync", C.LD, stg[0:1, 0:2048], C.bqkv[0:1, 0:2048])
    wt(C, P, "tensor", C.LD, c1)
    for c in range(16):
        op(C, P, "tensor", lambda e, c=c: e.transpose(
            C.PS6[0].ap()[:, c:c + 1], stg[0:1, 128 * c:128 * c + 128],
            C.idf_sb.ap()[0:1, 0:1]), C.TS if c == 15 else None)
    c_biasT = C.TS.n
    wt(C, P, "scalar", C.TS, c_biasT)
    cqb = aop(C, P, lambda e: e.copy(C.bqkvc_sb.ap(), C.PS6[0].ap()[:, 0:16]))
    wt(C, P, "vector", C.AS, cqb)
    vop(C, P, lambda e: e.tensor_scalar_mul(C.bqkvc_sb.ap()[:, 0:8], C.bqkvc_sb.ap()[:, 0:8], ATTN_MULT))
    C.c_bias_ps_free = cqb

    jobs = []
    for c in range(NDC):
        for part in range(3):
            src = C.wqkv[128 * c:128 * c + 128, 1024 * part:1024 * part + 1024]
            dst = C.MY_WQKV[128 * c:128 * c + 128, 1024 * part:1024 * part + 1024]
            jobs.append((src, dst, None))
    for r8 in range(8):
        for halfc in range(2):
            cs = slice(1024 * halfc, 1024 * halfc + 1024)
            jobs.append((C.wproj_half[128 * r8:128 * r8 + 128, cs],
                         C.SH_WP[128 * r8:128 * r8 + 128, cs],
                         C.SH_WP[1024 + 128 * r8:1024 + 128 * r8 + 128, cs]))
            jobs.append((C.wbase_half[128 * r8:128 * r8 + 128, cs],
                         C.SH_WB[128 * r8:128 * r8 + 128, cs],
                         C.SH_WB[1024 + 128 * r8:1024 + 128 * r8 + 128, cs]))

    c_stg_done = C.AS.n
    prev_store = {0: 0, 1: 0}
    for j, (src, d0, d1) in enumerate(jobs):
        sl = j % 2
        sslc = stg[:, 1024 * sl:1024 * sl + 1024]
        bslc = C.stgb_sb.ap()[:, 1024 * sl:1024 * sl + 1024]
        if j < 2:
            wt(C, P, "sync", C.AS, c_stg_done)
            wt(C, P, "sync", C.TS, c_biasT)
            wt(C, P, "sync", C.STQ, C.c_smalls_stq)
        wt(C, P, "sync", C.STQ, prev_store[sl])
        cl = dma(C, P, "sync", C.LDW, sslc, src)
        wt(C, P, "scalar", C.LDW, cl)
        wt(C, P, "scalar", C.STQ, prev_store[sl])
        if j < 2:
            wt(C, P, "scalar", C.STQ, C.c_smalls_stq)
        ccast = aop(C, P, lambda e, o=bslc, i=sslc: e.copy(o, i))
        wt(C, P, "scalar", C.AS, ccast)
        if d1 is None:
            cst = dma(C, P, "scalar", C.STQ, d0, bslc)
        else:
            cst = dma_br(C, P, "scalar", C.STQ, d0, d1, bslc)
        prev_store[sl] = cst
    C.c_wconv = C.STQ.n


def barrier(C, P, slot, gq_waits):
    for s, cval in gq_waits:
        wt(C, P, "gpsimd", s, cval)
    cf = C.GQ.inc(16)
    i0 = [("op", lambda e, s=C.GQ.h, sl=slot: e.dma_start(
        out=C.SH_FLAGS[0:1, sl, 0:1], in_=C.rmt_sb.ap()[0:1, 2:3]).then_inc(s, 16))]
    i1 = [("op", lambda e, s=C.GQ.h, sl=slot: e.dma_start(
        out=C.SH_FLAGS[1:2, sl, 0:1], in_=C.rmt_sb.ap()[0:1, 2:3]).then_inc(s, 16))]
    P.br("gpsimd", "memb", i0, i1)
    wt(C, P, "gpsimd", C.GQ, cf)

    def _poll(e, sl=slot):
        R = C.GREGS

        def cond():
            with e.If_eq(R["memb"], 0):
                e.dma_start(out=C.pf_sb.ap()[0:1, 0:1], in_=C.SH_FLAGS[1:2, sl, 0:1]).then_inc(C.PL.h, 16)
            with e.If_ne(R["memb"], 0):
                e.dma_start(out=C.pf_sb.ap()[0:1, 0:1], in_=C.SH_FLAGS[0:1, sl, 0:1]).then_inc(C.PL.h, 16)
            e.reg_alu(R["plcnt"], R["plcnt"], 16, ALU.add)
            e.wait_ge(C.PL.h, R["plcnt"])
            e.reg_load(R["pf"], C.pf_sb.ap()[0:1, 0:1])
            e.reg_alu(R["cond"], R["pf"], R["tok"], ALU.not_equal)
            return R["cond"]

        with e.While(cond):
            pass

    P.em("gpsimd", _poll)
    return op(C, P, "gpsimd", lambda e: e.memset(C.gz_sb.ap(), 0), C.BAR, 1)


def plan_phase0(C, P, role):
    nln = LN_N[role]
    nblk = nln // 128
    x0 = C.x0_sb.ap()
    xn = C.xn_sb.ap()
    sq0 = C.sq0_sb.ap()
    st = C.tiny_sb.ap()

    vlast = {0: 0, 1: 0}
    alast = {0: 0, 1: 0}
    evlast = {0: 0, 1: 0}
    stlast = {0: 0, 1: 0}
    cload = []
    tlast = 0
    c_ss_prev = 0
    xntg = C.xnt_sb.ap().rearrange("p (g c t) -> p g c t", g=2, t=CH)
    dstr = C.SH_XNT.rearrange("(c p) t -> p c t", p=128)
    for i in range(nblk):
        par = i % 2
        if i == 0:
            cload.append(dma(C, P, "sync", C.LD, x0[:, 0:D], C.x_ln[0:128, :]))
        if i + 1 < nblk:
            npar = (i + 1) % 2
            wt(C, P, "sync", C.VS, vlast[npar])
            cload.append(dma(C, P, "sync", C.LD, x0[:, npar * D:npar * D + D],
                             C.x_ln[128 * (i + 1):128 * (i + 1) + 128, :]))
        xv = x0[:, par * D:par * D + D]
        wt(C, P, "vector", C.LD, cload[i])
        bnst = C.bnst0_sb.ap()
        for q in range(4):
            vop(C, P, lambda e, xv=xv, q=q: e.bn_stats(
                bnst[:, 6 * q:6 * q + 6], xv[:, 512 * q:512 * q + 512]))
        cvar = vop(C, P, lambda e: e.bn_aggr(st[:, 0:2], bnst))
        alast[par] = 0
        wt(C, P, "scalar", C.VS, cvar)
        aop(C, P, lambda e: e.activation(st[:, 2:3], st[:, 1:2], AF.Ln, bias=C.eps_sb.ap()[:, 0:1]))
        crs = aop(C, P, lambda e: e.activation(st[:, 3:4], st[:, 2:3], AF.Exp, scale=-0.5))
        wt(C, P, "vector", C.AS, crs)
        vop(C, P, lambda e: e.tensor_tensor(st[:, 2:3], st[:, 0:1], st[:, 3:4], ALU.mult))
        wt(C, P, "vector", C.TS, tlast)
        cxn = vop(C, P, lambda e, xv=xv: e.tensor_scalar(
            xn, xv, st[:, 3:4], st[:, 2:3], ALU.mult, ALU.subtract))
        vlast[par] = cxn
        gslot = (i // 4) % 2
        toff = (i % 4) * 128
        for half in range(2):
            pb = C.PBF[half].ap()
            wt(C, P, "tensor", C.VS, cxn)
            wt(C, P, "tensor", C.AS, evlast[half])
            for cc in range(8):
                c = half * 8 + cc
                op(C, P, "tensor", lambda e, c=c, cc=cc, pb=pb: e.transpose(
                    pb[:, 128 * cc:128 * cc + 128], xn[:, 128 * c:128 * c + 128], C.idb_sb.ap()),
                   C.TS if cc == 7 else None)
            ct = C.TS.n
            wt(C, P, "scalar", C.TS, ct)
            wt(C, P, "scalar", C.STQ, stlast[gslot])
            ce = aop(C, P, lambda e, pb=pb, half=half, gslot=gslot, toff=toff: e.copy(
                xntg[:, gslot, 8 * half:8 * half + 8, toff:toff + 128],
                pb.rearrange("p (c t) -> p c t", t=128)))
            evlast[half] = ce
        tlast = C.TS.n
        if i % 4 == 3 or i == nblk - 1:
            tg0 = (i // 4) * CH
            tglen = (i % 4 + 1) * 128
            wt(C, P, "scalar", C.AS, C.AS.n)
            cst = C.STQ.inc(16)
            i0 = [("op", lambda e, tg0=tg0, tglen=tglen, gslot=gslot, s=C.STQ.h: e.dma_start(
                out=dstr[:, :, tg0:tg0 + tglen],
                in_=xntg[:, gslot, :, 0:tglen]).then_inc(s, 16))]
            i1 = [("op", lambda e, tg0=tg0, tglen=tglen, gslot=gslot, nln=nln, s=C.STQ.h: e.dma_start(
                out=dstr[:, :, nln + tg0:nln + tg0 + tglen],
                in_=xntg[:, gslot, :, 0:tglen]).then_inc(s, 16))]
            P.br("scalar", "memb", i0, i1)
            stlast[gslot] = cst


def plan_phase1(C, P, role, limit=None):
    slots = OUTER_SLOTS if role == 0 else MIDDLE_SLOTS
    kvr = KV_RANGE[role]
    nsc = kvr // SC
    xsr = C.xs_sb.ap().rearrange("p (s c t) -> p s c t", s=2, t=SC)
    shx = C.SH_XNT.rearrange("(c p) t -> p c t", p=128)
    Kap = C.K_sb.ap()
    Vap = C.V_sb.ap()
    st = C.tiny_sb.ap()
    PS = C.PS6
    PAV = C.PAV

    barrier(C, P, 0, [(C.STQ, C.STQ.n)])
    bar0 = C.BAR.n

    wt(C, P, "vector", C.BAR, bar0)
    vset = vop(C, P, lambda e: e.memset(
        Vap.rearrange("p (k h d) -> p k h d", h=4, d=129)[:, :, :, 128:129], 1.0))

    srot = [0]
    exp_of_bank = {i: 0 for i in range(6)}
    nrm_last = 0
    ost_stored = 0
    pt_used = {i: 0 for i in range(4)}
    pm_used = {0: 0, 1: 0}
    xs_free = 0
    sqq_ts = {0: 0, 1: 0}
    kv_ev = 0
    kn_vread = 0

    for g in range(2):
        wt(C, P, "sync", C.STQ, C.c_wconv)
        wt(C, P, "sync", C.BAR, bar0)
        wt(C, P, "sync", C.TS, xs_free)
        ck = 0
        for c in range(NDC):
            ck = dma(C, P, "sync", C.LD, C.wkv_sb.ap()[:, 1024 * c:1024 * c + 512],
                     C.MY_WQKV[128 * c:128 * c + 128, 1024 + 512 * g:1024 + 512 * g + 512])
            ck = dma(C, P, "sync", C.LD, C.wkv_sb.ap()[:, 1024 * c + 512:1024 * c + 1024],
                     C.MY_WQKV[128 * c:128 * c + 128, 2048 + 512 * g:2048 + 512 * g + 512])
        c_wkv = ck

        kv_ts = {0: xs_free, 1: xs_free}
        knts = {0: 0, 1: 0}
        vkn = 0
        for si in range(nsc):
            par = si % 2
            wt(C, P, "sync", C.TS, kv_ts[par])
            base = si * SC
            cl = 0
            for qq in range(4):
                cl = dma(C, P, "sync", C.LD, xsr[:, par, 4 * qq:4 * qq + 4, :],
                         shx[:, 4 * qq:4 * qq + 4, base:base + SC])
            wt(C, P, "tensor", C.LD, cl)
            if g == 0 and si == 0:
                wt(C, P, "tensor", C.AS, C.c_bias_ps_free)
            if si == 0:
                wt(C, P, "tensor", C.LD, c_wkv)
                wt(C, P, "tensor", C.VS, kn_vread)
                wt(C, P, "tensor", C.NRM, nrm_last)
                wt(C, P, "tensor", C.EXPS, max(exp_of_bank.values()))
            wt(C, P, "tensor", C.KVE, kv_ev)
            for h in range(4):
                for c in range(NDC):
                    op(C, P, "tensor", lambda e, h=h, c=c, par=par: e.matmul(
                        PS[h].ap()[:, 0:SC],
                        C.wkv_sb.ap()[:, 1024 * c + 128 * h:1024 * c + 128 * h + 128],
                        C.xs_sb.ap()[:, (par * NDC + c) * SC:(par * NDC + c) * SC + SC],
                        start=(c == 0), stop=(c == NDC - 1)), None)
            for tb in range(2):
                for c in range(NDC):
                    op(C, P, "tensor", lambda e, tb=tb, c=c, par=par: e.matmul(
                        PAV[tb].ap()[:, 0:512],
                        C.xs_sb.ap()[:, (par * NDC + c) * SC + 128 * tb:(par * NDC + c) * SC + 128 * tb + 128],
                        C.wkv_sb.ap()[:, 1024 * c + 512:1024 * c + 1024],
                        start=(c == 0), stop=False), None)
                op(C, P, "tensor", lambda e, tb=tb, g=g: e.matmul(
                    PAV[tb].ap()[:, 0:512], C.onr_sb.ap()[0:1, :],
                    C.bvrow_sb.ap()[0:1, 512 * g:512 * g + 512],
                    start=False, stop=True), C.TS if tb == 1 else None)
            kv_ts[par] = C.TS.n
            wt(C, P, "scalar", C.TS, kv_ts[par])
            for h in range(4):
                aop(C, P, lambda e, h=h, base=base, g=g: e.activation(
                    Kap[:, h * T + base:h * T + base + SC], PS[h].ap()[:, 0:SC],
                    AF.Identity, bias=C.bqkvc_sb.ap()[:, 8 + 4 * g + h:8 + 4 * g + h + 1]), chain=False)
            c_kev = C.AS.n
            for tb in range(2):
                kb = base // 128 + tb
                for h in range(4):
                    op(C, P, "scalar", lambda e, kb=kb, h=h, tb=tb: e.copy(
                        Vap[:, 516 * kb + 129 * h:516 * kb + 129 * h + 128],
                        PAV[tb].ap()[:, 128 * h:128 * h + 128]),
                        C.KVE if (tb == 1 and h == 3) else None)
            kv_ev = C.KVE.n
            wt(C, P, "vector", C.AS, c_kev)
            for h in range(4):
                hp = h % 2
                wt(C, P, "vector", C.TS, sqq_ts[hp])
                csq = vop(C, P, lambda e, h=h, base=base, hp=hp: e.tensor_mul(
                    C.sqq_sb.ap()[:, hp * CH:hp * CH + SC],
                    Kap[:, h * T + base:h * T + base + SC],
                    Kap[:, h * T + base:h * T + base + SC]))
                wt(C, P, "tensor", C.VS, csq)
                wt(C, P, "tensor", C.VS, knts[si % 2])
                op(C, P, "tensor", lambda e, si=si, hp=hp: e.matmul(
                    PS[4 + si % 2].ap()[0:1, 0:SC], C.onc_sb.ap()[:, 0:1],
                    C.sqq_sb.ap()[:, hp * CH:hp * CH + SC],
                    start=True, stop=True), C.TS)
                cts = C.TS.n
                sqq_ts[hp] = cts
                wt(C, P, "vector", C.TS, cts)
                vop(C, P, lambda e, si=si: e.reduce_max(
                    st[0:1, 4:5], PS[4 + si % 2].ap()[0:1, 0:SC], axis=AX.X))
                if si == 0:
                    vkn = vop(C, P, lambda e, h=h: e.tensor_copy(
                        C.kn_sb.ap()[0:1, h:h + 1], st[0:1, 4:5]))
                else:
                    vkn = vop(C, P, lambda e, h=h: e.tensor_tensor(
                        C.kn_sb.ap()[0:1, h:h + 1], C.kn_sb.ap()[0:1, h:h + 1],
                        st[0:1, 4:5], ALU.max))
                knts[si % 2] = vkn
        kn_vread = vkn
        if limit in ("kvonly", "kv1g"):
            xs_free = max(kv_ts.values())
            continue
        wt(C, P, "sync", C.TS, max(kv_ts.values()))
        ck = 0
        for c in range(NDC):
            ck = dma(C, P, "sync", C.LD, C.wq_sb.ap()[:, 512 * c:512 * c + 512],
                     C.MY_WQKV[128 * c:128 * c + 128, 512 * g:512 * g + 512])
        c_wq = ck
        wt(C, P, "scalar", C.VS, vkn)
        aop(C, P, lambda e: e.activation(C.kn_sb.ap()[0:1, 8:12], C.kn_sb.ap()[0:1, 0:4], AF.Ln))
        ckn = aop(C, P, lambda e: e.activation(C.kn_sb.ap()[0:1, 4:8], C.kn_sb.ap()[0:1, 8:12], AF.Exp, scale=0.5))
        if limit in ("kv", "kn_novec", "kn_nolnexp"):
            continue

        q_consumed = max(kv_ts.values())
        avset = [(C.PS6[4], C.PS6[5]), (C.PAV[0], C.PAV[1])]
        nrm_of_set = {0: 0, 1: 0}
        ghead = [0]
        bank_extra = {i: [] for i in range(4)}

        def s_slot():
            sb = srot[0] % 4
            srot[0] += 1
            return sb

        for j in range(ATT_J):
            sl = slots[j]
            qbase = sl * CH
            nkt = 4 * (sl + 1)
            qpar = j % 2
            for half in range(2):
                wt(C, P, "sync", C.TS, q_consumed)
                for qq in range(4):
                    cl = dma(C, P, "sync", C.LD, xsr[:, half, 4 * qq:4 * qq + 4, :],
                             shx[:, 4 * qq:4 * qq + 4, qbase + SC * half:qbase + SC * half + SC])
            c_xq = cl
            wt(C, P, "tensor", C.LD, c_xq)
            if j == 0:
                wt(C, P, "tensor", C.LD, c_wq)
                wt(C, P, "tensor", C.VS, kn_vread)
                wt(C, P, "tensor", C.NRM, nrm_last)
            for h in range(4):
                wt(C, P, "tensor", C.EXPS, exp_of_bank[h])
                for sem, cnt in bank_extra[h]:
                    wt(C, P, "tensor", sem, cnt)
                bank_extra[h] = []
                for half in range(2):
                    for c in range(NDC):
                        last = (half == 1 and c == NDC - 1)
                        op(C, P, "tensor", lambda e, h=h, half=half, c=c: e.matmul(
                            C.PS6[h].ap()[:, half * SC:half * SC + SC],
                            C.wq_sb.ap()[:, 512 * c + 128 * h:512 * c + 128 * h + 128],
                            C.xs_sb.ap()[:, (half * NDC + c) * SC:(half * NDC + c) * SC + SC],
                            start=(c == 0), stop=(c == NDC - 1)),
                            C.TS if last else None)
                cqmm_h = C.TS.n
                wt(C, P, "scalar", C.TS, cqmm_h)
                aop(C, P, lambda e, h=h, qpar=qpar, g=g: e.activation(
                    C.Q_sb.ap()[:, (4 * qpar + h) * CH:(4 * qpar + h) * CH + CH],
                    C.PS6[h].ap()[:, 0:CH], AF.Identity,
                    bias=C.bqkvc_sb.ap()[:, 4 * g + h:4 * g + h + 1], scale=ATTN_MULT),
                    chain=False)
            cqmm = C.TS.n
            q_consumed = cqmm
            cqev = C.AS.n
            if g == 0 and j == 0:
                wt(C, P, "tensor", C.VS, vset)

            def emit_bounds_a(h):
                hp = h % 2
                wt(C, P, "vector", C.AS, cqev)
                wt(C, P, "vector", C.TS, sqq_ts[hp])
                qsl = C.Q_sb.ap()[:, (4 * qpar + h) * CH:(4 * qpar + h) * CH + CH]
                csq = vop(C, P, lambda e, qsl=qsl, hp=hp: e.tensor_mul(
                    C.sqq_sb.ap()[:, hp * CH:hp * CH + CH], qsl, qsl))
                sb = s_slot()
                wt(C, P, "tensor", C.EXPS, exp_of_bank[sb])
                for sem, cnt in bank_extra[sb]:
                    wt(C, P, "tensor", sem, cnt)
                bank_extra[sb] = []
                wt(C, P, "tensor", C.VS, csq)
                op(C, P, "tensor", lambda e, sb=sb, hp=hp: e.matmul(
                    C.PS6[sb].ap()[0:1, 0:CH], C.onc_sb.ap()[:, 0:1],
                    C.sqq_sb.ap()[:, hp * CH:hp * CH + CH],
                    start=True, stop=True), C.TS)
                cts = C.TS.n
                sqq_ts[hp] = cts
                wt(C, P, "vector", C.TS, cts)
                vop(C, P, lambda e, sb=sb: e.reduce_max(
                    st[0:1, 0:1], C.PS6[sb].ap()[0:1, 0:CH], axis=AX.X))
                wt(C, P, "scalar", C.VS)
                wt(C, P, "scalar", C.AS, ckn)
                aop(C, P, lambda e: e.activation(st[0:1, 1:2], st[0:1, 0:1], AF.Ln))
                caq = aop(C, P, lambda e: e.activation(st[0:1, 2:3], st[0:1, 1:2], AF.Exp, scale=0.5))
                wt(C, P, "vector", C.AS, caq)
                vop(C, P, lambda e, h=h: e.tensor_tensor(
                    st[0:1, 3:4], st[0:1, 2:3], C.kn_sb.ap()[0:1, 4 + h:5 + h], ALU.mult))
                vop(C, P, lambda e: e.tensor_scalar(
                    st[0:1, 0:1], st[0:1, 3:4], -1.0, -1.0, ALU.mult, ALU.add))
                vnb = vop(C, P, lambda e, h=h: e.tensor_copy(
                    C.tinyb_sb.ap()[0:1, h:h + 1], st[0:1, 0:1]))
                bank_extra[sb].append((C.VS, C.VS.n))
                return vnb

            def emit_bounds_b(h, vnb):
                sb2 = s_slot()
                wt(C, P, "tensor", C.EXPS, exp_of_bank[sb2])
                for sem, cnt in bank_extra[sb2]:
                    wt(C, P, "tensor", sem, cnt)
                bank_extra[sb2] = []
                wt(C, P, "tensor", C.VS, vnb)
                op(C, P, "tensor", lambda e, sb2=sb2, h=h: e.matmul(
                    C.PS6[sb2].ap()[:, 256 + h:256 + h + 1], C.onr_sb.ap()[0:1, :],
                    C.tinyb_sb.ap()[0:1, h:h + 1], start=True, stop=True), C.TS)
                bts = C.TS.n
                wt(C, P, "scalar", C.TS, bts)
                cnb = aop(C, P, lambda e, qpar=qpar, h=h, sb2=sb2: e.copy(
                    C.negb_sb.ap()[:, 4 * qpar + h:4 * qpar + h + 1],
                    C.PS6[sb2].ap()[:, 256 + h:256 + h + 1]))
                bank_extra[sb2].append((C.AS, cnb))
                return cnb

            cnb_of = {}
            cnb_of[0] = emit_bounds_b(0, emit_bounds_a(0))
            if limit == "tap45":
                wt(C, P, "gpsimd", C.AS, cnb_of[0])
                wt(C, P, "gpsimd", C.KVE, C.KVE.n)
                dma(C, P, "gpsimd", C.GQ, C.dbg[0:128, 0:2048], C.Q_sb.ap()[:, 0:2048])
                dma(C, P, "gpsimd", C.GQ, C.dbg[128:256, 0:16],
                    C.negb_sb.ap().bitcast(BF16)[:, 0:16])
                dma(C, P, "gpsimd", C.GQ, C.dbg[256:384, 0:2048], C.K_sb.ap()[:, 0:2048])
                dma(C, P, "gpsimd", C.GQ, C.dbg[384:512, 0:2048], C.V_sb.ap()[:, 0:2048])
                wt(C, P, "gpsimd", C.GQ, C.GQ.n)
                return

            for h in range(4):
                gh = ghead[0]
                ghead[0] += 1
                sset = gh % 2
                bk = avset[sset]
                wt(C, P, "scalar", C.AS, cnb_of[h])
                state = {}

                def emit_S(kt, h=h):
                    sb = s_slot()
                    wt(C, P, "tensor", C.EXPS, exp_of_bank[sb])
                    for sem, cnt in bank_extra[sb]:
                        wt(C, P, "tensor", sem, cnt)
                    bank_extra[sb] = []
                    if kt == 0:
                        wt(C, P, "tensor", C.NRM, nrm_of_set[sset])
                    op(C, P, "tensor", lambda e, sb=sb, h=h, kt=kt, qpar=qpar: e.matmul(
                        C.PS6[sb].ap()[:, 0:CH],
                        Kap[:, h * T + kt * 128:h * T + kt * 128 + 128],
                        C.Q_sb.ap()[:, (4 * qpar + h) * CH:(4 * qpar + h) * CH + CH],
                        start=True, stop=True), C.TS)
                    state[kt] = (sb, C.TS.n)

                def emit_tail(kt, h=h, sset=sset, bk=bk):
                    sb, cs = state.pop(kt)
                    diag = (kt >= nkt - 4)
                    rel = kt - (nkt - 4)
                    ppar = kt % 4
                    wt(C, P, "scalar", C.TS, cs)
                    wt(C, P, "scalar", C.AVS, pt_used[ppar])
                    ce = op(C, P, "scalar", lambda e, sb=sb, ppar=ppar, qpar=qpar, h=h: e.activation(
                        C.Pt_sb.ap()[:, ppar * CH:ppar * CH + CH], C.PS6[sb].ap()[:, 0:CH],
                        AF.Exp, bias=C.negb_sb.ap()[:, 4 * qpar + h:4 * qpar + h + 1]), C.EXPS)
                    exp_of_bank[sb] = ce
                    src_p = C.Pt_sb.ap()[:, ppar * CH:ppar * CH + CH]
                    if diag:
                        wt(C, P, "vector", C.EXPS, ce)
                        wt(C, P, "vector", C.AVS, pm_used[rel % 2])
                        cm = vop(C, P, lambda e, rel=rel, src_p=src_p: e.tensor_mul(
                            C.Pm_sb.ap()[:, (rel % 2) * CH:(rel % 2) * CH + CH],
                            src_p, C.masks_sb.ap()[:, rel * CH:rel * CH + CH]))
                        src_p = C.Pm_sb.ap()[:, (rel % 2) * CH:(rel % 2) * CH + CH]
                        wt(C, P, "tensor", C.VS, cm)
                    else:
                        wt(C, P, "tensor", C.EXPS, ce)
                    for b4 in range(4):
                        op(C, P, "tensor", lambda e, b4=b4, kt=kt, h=h, src_p=src_p, bk=bk: e.matmul(
                            bk[b4 // 2].ap()[:, 129 * (b4 % 2):129 * (b4 % 2) + 129],
                            src_p[:, 128 * b4:128 * b4 + 128],
                            Vap[:, 516 * kt + 129 * h:516 * kt + 129 * h + 129],
                            start=(kt == 0 and b4 % 2 == 0), stop=(kt == nkt - 1)),
                            C.AVS if b4 == 3 else None)
                    pt_used[ppar] = C.AVS.n
                    if diag:
                        pm_used[rel % 2] = C.AVS.n

                LOOK = 2
                vnb_next = [None]
                for kt in range(nkt + LOOK):
                    if kt < nkt:
                        emit_S(kt)
                    if kt == 0 and h < 3:
                        vnb_next[0] = emit_bounds_a(h + 1)
                    if kt - LOOK >= 0:
                        emit_tail(kt - LOOK)
                    if kt == 2 and h < 3:
                        cnb_of[h + 1] = emit_bounds_b(h + 1, vnb_next[0])
                cav = C.AVS.n
                wt(C, P, "vector", C.AVS, cav)
                wt(C, P, "vector", C.STQ, ost_stored)
                for b4 in range(4):
                    pbank = bk[b4 // 2].ap()[:, 129 * (b4 % 2):129 * (b4 % 2) + 129]
                    vop(C, P, lambda e, pbank=pbank: e.tensor_scalar(
                        st[:, 4:5], pbank[:, 128:129], 1e-30, None, ALU.add))
                    vop(C, P, lambda e: e.reciprocal(st[:, 5:6], st[:, 4:5]))
                    lastnrm = (b4 == 3)
                    wt(C, P, "vector", C.VS)
                    op(C, P, "vector", lambda e, b4=b4, h=h, pbank=pbank: e.tensor_scalar_mul(
                        C.Ost_sb.ap()[:, 512 * b4 + 128 * h:512 * b4 + 128 * h + 128],
                        pbank[:, 0:128], st[:, 5:6]),
                        C.NRM if lastnrm else C.VS)
                nrm_of_set[sset] = C.NRM.n
                nrm_last = C.NRM.n
                wt(C, P, "vector", C.NRM, nrm_last)
            wt(C, P, "scalar", C.NRM, C.NRM.n)
            cst = C.STQ.inc(16)
            i0 = [("op", lambda e, j=j, g=g, s=C.STQ.h: e.dma_start(
                out=C.SH_O[512 * j:512 * j + 512, 512 * g:512 * g + 512].rearrange(
                    "(b p) f -> p b f", p=128),
                in_=C.Ost_sb.ap().rearrange("p (b f) -> p b f", f=512)).then_inc(s, 16))]
            i1 = [("op", lambda e, j=j, g=g, s=C.STQ.h: e.dma_start(
                out=C.SH_O[512 * j:512 * j + 512, 1024 + 512 * g:1024 + 512 * g + 512].rearrange(
                    "(b p) f -> p b f", p=128),
                in_=C.Ost_sb.ap().rearrange("p (b f) -> p b f", f=512)).then_inc(s, 16))]
            P.br("scalar", "memb", i0, i1)
            ost_stored = cst

        xs_free = q_consumed

    wt(C, P, "tensor", C.NRM, C.NRM.n)
    op(C, P, "tensor", lambda e: e.transpose(
        C.PBF[0].ap()[:, 0:128], C.idb_sb.ap(), C.idb_sb.ap()), C.ARM, 1)
    wt(C, P, "scalar", C.AS)
    wt(C, P, "scalar", C.STQ, C.STQ.n)
    op(C, P, "scalar", lambda e: e.copy(C.tinyb_sb.ap()[0:1, 4:5], C.tinyb_sb.ap()[0:1, 5:6]), C.ARM, 1)
    wt(C, P, "vector", C.VS)
    op(C, P, "vector", lambda e: e.tensor_copy(st[0:1, 6:7], st[0:1, 7:8]), C.ARM, 1)
    barrier(C, P, 1, [(C.STQ, C.STQ.n)])


def plan_phase2(C, P):
    st = C.tiny_sb.ap()
    wproj = C.wproj_sb.ap()
    wbase = C.wbase_sb.ap()
    wt(C, P, "sync", C.ARM, 3)
    wt(C, P, "sync", C.BAR, C.BAR.n)
    wt(C, P, "sync", C.STQ, C.c_smalls_stq)
    dma(C, P, "sync", C.LD2, C.aall_sb.ap(), C.D_AALL)
    dma(C, P, "sync", C.LD2, C.ball_sb.ap(), C.D_BALL)
    dma(C, P, "sync", C.LD2, C.wrout_sb.ap(), C.D_WROUT)
    dma(C, P, "sync", C.LD2, C.bias2_sb.ap()[0:1, :], C.D_BIAS[0:1, :])
    dma(C, P, "sync", C.LD2, C.bias2_sb.ap()[64:65, :], C.D_BIAS[1:2, :])
    dma(C, P, "sync", C.LD2, C.bias2_sb.ap()[32:33, 0:8], C.D_BIAS[2:3, 0:8])
    dma(C, P, "sync", C.LD2, C.ones3_sb.ap()[32:33, :], C.ones_row)
    dma(C, P, "sync", C.LD2, C.ones3_sb.ap()[64:65, :], C.ones_row)
    dma(C, P, "sync", C.LD2, C.sel_sb.ap(), C.sel_c)
    c_smalls = dma(C, P, "sync", C.LD2, C.w7_sb.ap()[6:7, :], C.ones_row)
    cw = 0
    for c in range(NDC):
        cw = dma(C, P, "sync", C.LD2, wproj[:, 2048 * c:2048 * c + 2048],
                 C.SH_WP[128 * c:128 * c + 128, :])
    c_wp = cw
    for c in range(NDC):
        cw = dma(C, P, "sync", C.LD2, wbase[:, 2048 * c:2048 * c + 2048],
                 C.SH_WB[128 * c:128 * c + 128, :])
    c_wb = cw

    xres = C.xres_sb.ap()
    osb = C.o_sb.ap()[:, 0:D]
    ot = C.ot_sb.ap()
    big = C.big_sb.ap()
    x2 = C.x2_sb.ap()
    xn2 = C.xn2_sb.ap()
    xn2t = C.xn2t_sb.ap()
    outb = C.out_sb.ap()
    g8 = C.g8_sb.ap()

    bnst = C.bnst_sb.ap()

    def ln_stats(src, vw):
        for s, v in vw:
            wt(C, P, "vector", s, v)
        for q in range(4):
            v2op(C, P, lambda e, q=q: e.bn_stats(bnst[:, 6 * q:6 * q + 6],
                                                 src[:, 512 * q:512 * q + 512]))
        cv = v2op(C, P, lambda e: e.bn_aggr(st[:, 0:2], bnst))
        wt(C, P, "scalar", C.VS2, cv)
        a2op(C, P, lambda e: e.activation(st[:, 2:3], st[:, 1:2], AF.Ln, bias=C.eps_sb.ap()[:, 0:1]))
        crs = a2op(C, P, lambda e: e.activation(st[:, 3:4], st[:, 2:3], AF.Exp, scale=-0.5))
        wt(C, P, "vector", C.AS2, crs)
        v2op(C, P, lambda e: e.tensor_tensor(st[:, 2:3], st[:, 0:1], st[:, 3:4], ALU.mult))

    S = {
        "c_xload": {}, "c_oload": {}, "c_otr": {}, "tr_ev": {0: 0, 1: 0},
        "otr_ev": {0: 0, 1: 0}, "xn2_ev": {0: 0, 1: 0},
        "projev": {0: 0, 1: 0}, "ln4_norm": 0, "c_ffn": {},
        "cLN2": {}, "cLN4": {}, "c_add": {}, "cxn2": {}, "c_xn2tr": {},
        "c_rout": {}, "c_hmm": {}, "c_hwf_ev": 0, "cw2": 0, "cge": 0,
        "chw": {}, "c_ost": {}, "c_xntr_done": 0,
    }

    def seg_oload(i):
        if i > 0:
            wt(C, P, "sync", C.T2, S["c_otr"][i - 1])
        S["c_oload"][i] = C.LD2.inc(16)
        i0 = [("op", lambda e, i=i, s=C.LD2.h: e.dma_start(
            out=osb, in_=C.SH_O[128 * i:128 * i + 128, :]).then_inc(s, 16))]
        i1 = [("op", lambda e, i=i, s=C.LD2.h: e.dma_start(
            out=osb, in_=C.SH_O[2048 + 128 * i:2048 + 128 * i + 128, :]).then_inc(s, 16))]
        P.br("sync", "memb", i0, i1)

    def seg_xload(i):
        if i > 0:
            wt(C, P, "sync", C.VS2, S["c_add"].get(i - 1, 0))
        S["c_xload"][i] = dma(C, P, "sync", C.LD2, xres, C.x_res[128 * i:128 * i + 128, :])

    def seg_otr(i):
        wt(C, P, "tensor", C.LD2, S["c_oload"][i])
        if i == 0:
            wt(C, P, "tensor", C.LD2, c_wp)
            wt(C, P, "tensor", C.LD2, c_smalls)
        for half in range(2):
            pb = C.PBF[half].ap()
            wt(C, P, "tensor", C.AS2, S["tr_ev"][half])
            for cc in range(8):
                c = half * 8 + cc
                op(C, P, "tensor", lambda e, c=c, cc=cc, pb=pb: e.transpose(
                    pb[:, 128 * cc:128 * cc + 128], osb[:, 128 * c:128 * c + 128],
                    C.idb_sb.ap()), C.T2 if cc == 7 else None)
            ct = C.T2.n
            wt(C, P, "scalar", C.T2, ct)
            ev = a2op(C, P, lambda e, pb=pb, half=half: e.copy(
                ot[:, 1024 * half:1024 * half + 1024], pb), chain=False)
            S["tr_ev"][half] = ev
            S["otr_ev"][half] = ev
        S["c_otr"][i] = C.T2.n

    def seg_xn2tr_tr(i):
        wt(C, P, "tensor", C.VS2, S["cxn2"][i])
        for half in range(2):
            pb = C.PBF[half].ap()
            wt(C, P, "tensor", C.AS2, S["tr_ev"][half])
            for cc in range(8):
                c = half * 8 + cc
                op(C, P, "tensor", lambda e, c=c, cc=cc, pb=pb: e.transpose(
                    pb[:, 128 * cc:128 * cc + 128], xn2[:, 128 * c:128 * c + 128],
                    C.idb_sb.ap()), C.T2 if (half, cc) == (1, 7) else None)
        S["c_xntr_done"] = C.T2.n
        S["c_xn2tr"][i] = C.T2.n

    def seg_xn2tr_ev(gate):
        for half in range(2):
            pb = C.PBF[half].ap()
            wt(C, P, "scalar", C.T2, gate)
            ev = a2op(C, P, lambda e, pb=pb, half=half: e.copy(
                xn2t[:, 1024 * half:1024 * half + 1024], pb), chain=False)
            S["tr_ev"][half] = ev
            S["xn2_ev"][half] = ev

    def seg_proj(i):
        for r in range(2):
            wt(C, P, "tensor", C.AS2, S["projev"][1] if r == 0 else S["projev"][0])
            for c in range(NDC):
                if r == 0 and c in (0, 8):
                    wt(C, P, "tensor", C.AS2, S["otr_ev"][c // 8])
                for fi in range(2):
                    fc = 2 * r + fi
                    op(C, P, "tensor", lambda e, c=c, fc=fc, fi=fi: e.matmul(
                        C.PS6[fi].ap()[:, 0:512],
                        ot[:, 128 * c:128 * c + 128],
                        wproj[:, 2048 * c + 512 * fc:2048 * c + 512 * fc + 512],
                        start=(c == 0), stop=False), None)
            for fi in range(2):
                fc = 2 * r + fi
                op(C, P, "tensor", lambda e, fc=fc, fi=fi: e.matmul(
                    C.PS6[fi].ap()[:, 0:512], C.onr_sb.ap()[0:1, :],
                    C.bias2_sb.ap()[0:1, 512 * fc:512 * fc + 512],
                    start=False, stop=True), C.T2 if fi == 1 else None)
            cpr = C.T2.n
            if r == 0 and i >= 1:
                seg_xn2tr_tr(i - 1)
            wt(C, P, "scalar", C.T2, cpr)
            if r == 0:
                wt(C, P, "scalar", C.VS2, S["cLN2"].get(i - 1, 0))
            for fi in range(2):
                fc = 2 * r + fi
                ev = a2op(C, P, lambda e, fc=fc, fi=fi: e.copy(
                    big[:, 512 * fc:512 * fc + 512], C.PS6[fi].ap()[:, 0:512]), chain=False)
            S["projev"][r] = ev
        if i >= 1:
            seg_xn2tr_ev(S["c_xn2tr"][i - 1])
        if i + 1 <= 15:
            seg_otr(i + 1)
        if i + 2 <= 15:
            seg_oload(i + 2)

    def seg_ln23(i):
        sl = i % 2
        pe = S["projev"][1]
        ln_stats(big, [(C.AS2, pe)])
        wt(C, P, "vector", C.LD2, S["c_xload"][i])
        v2op(C, P, lambda e, sl=sl: e.tensor_scalar(
            x2[:, sl * D:sl * D + D], big, st[:, 3:4], st[:, 2:3], ALU.mult, ALU.subtract))
        S["cLN2"][i] = C.VS2.n
        S["c_add"][i] = v2op(C, P, lambda e, sl=sl: e.tensor_add(
            x2[:, sl * D:sl * D + D], x2[:, sl * D:sl * D + D], xres))
        ln_stats(x2[:, sl * D:sl * D + D], [])
        wt(C, P, "vector", C.T2, S["c_xn2tr"].get(i - 1, 0))
        S["cxn2"][i] = v2op(C, P, lambda e, sl=sl: e.tensor_scalar(
            xn2, x2[:, sl * D:sl * D + D], st[:, 3:4], st[:, 2:3], ALU.mult, ALU.subtract))

    def seg_rh(i):
        if i == 0:
            wt(C, P, "tensor", C.LD2, c_wb)
        if i > 0:
            wt(C, P, "tensor", C.AS2, S["c_hwf_ev"])
            wt(C, P, "tensor", C.VS2, S["ln4_norm"])
        for c in range(NDC):
            if c in (0, 8):
                wt(C, P, "tensor", C.AS2, S["xn2_ev"][c // 8])
            op(C, P, "tensor", lambda e, c=c: e.matmul(
                C.PAV[1].ap()[:, 0:8], xn2t[:, 128 * c:128 * c + 128],
                C.wrout_sb.ap()[:, 8 * c:8 * c + 8],
                start=(c == 0), stop=False), None)
        op(C, P, "tensor", lambda e: e.matmul(
            C.PAV[1].ap()[:, 0:8], C.ones3_sb.ap()[32:33, :], C.bias2_sb.ap()[32:33, 0:8],
            start=False, stop=True), C.T2)
        S["c_rout"][i] = C.T2.n
        for c in range(NDC):
            op(C, P, "tensor", lambda e, c=c: e.matmul(
                C.PAV[0].ap()[0:PACK, 0:128],
                C.aall_sb.ap()[:, PACK * c:PACK * c + PACK], xn2t[:, 128 * c:128 * c + 128],
                start=(c == 0), stop=(c == NDC - 1)), C.T2 if c == NDC - 1 else None)
        S["c_hmm"][i] = C.T2.n

    def seg_softmax_v(i):
        wt(C, P, "vector", C.T2, S["c_rout"][i])
        v2op(C, P, lambda e: e.reduce_max(st[:, 4:5], C.PAV[1].ap()[:, 0:8], axis=AX.X))
        cgm = v2op(C, P, lambda e: e.tensor_scalar_mul(st[:, 7:8], st[:, 4:5], -1.0))
        wt(C, P, "scalar", C.VS2, cgm)
        S["cge"] = a2op(C, P, lambda e: e.activation(g8, C.PAV[1].ap()[:, 0:8], AF.Exp, bias=st[:, 7:8]))
        wt(C, P, "vector", C.AS2, S["cge"])
        v2op(C, P, lambda e: e.reduce_sum(st[:, 5:6], g8, axis=AX.X))
        v2op(C, P, lambda e: e.reciprocal(st[:, 6:7], st[:, 5:6]))
        v2op(C, P, lambda e: e.tensor_scalar_mul(C.e2_sb.ap(), g8, st[:, 6:7]))
        v2op(C, P, lambda e: e.reduce_max(st[:, 4:5], C.e2_sb.ap(), axis=AX.X))
        v2op(C, P, lambda e: e.tensor_scalar(C.e1_sb.ap(), C.e2_sb.ap(), st[:, 4:5], None, ALU.is_equal))
        v2op(C, P, lambda e: e.scalar_tensor_tensor(g8, C.e1_sb.ap(), NEG_BIG, C.e2_sb.ap(), ALU.mult, ALU.add))
        v2op(C, P, lambda e: e.reduce_max(st[:, 5:6], g8, axis=AX.X))
        v2op(C, P, lambda e: e.tensor_scalar(C.w2_sb.ap(), g8, st[:, 5:6], None, ALU.is_equal))
        v2op(C, P, lambda e: e.tensor_add(g8, C.e1_sb.ap(), C.w2_sb.ap()))
        S["cw2"] = v2op(C, P, lambda e: e.tensor_mul(C.w2_sb.ap(), C.e2_sb.ap(), g8))

    def seg_ffn(i):
        FB = [C.PS6[2], C.PS6[3], C.PAV[0], C.PAV[1]]
        wt(C, P, "tensor", C.VS2, S["ln4_norm"])
        for c in range(NDC):
            for fi in range(2):
                op(C, P, "tensor", lambda e, c=c, fi=fi, bk=FB[fi]: e.matmul(
                    bk.ap()[:, 0:512],
                    xn2t[:, 128 * c:128 * c + 128],
                    wbase[:, 2048 * c + 512 * fi:2048 * c + 512 * fi + 512],
                    start=(c == 0), stop=False), None)
        wt(C, P, "tensor", C.VS2, S["cw2"])
        wt(C, P, "tensor", C.AS2, S["cge"])
        op(C, P, "tensor", lambda e: e.transpose(
            C.PAV[1].ap()[0:8, 128:256], C.w2_sb.ap(), C.idf_sb.ap()), C.T2)
        cwt = C.T2.n
        wt(C, P, "scalar", C.T2, cwt)
        cw7 = a2op(C, P, lambda e: e.copy(C.w7_sb.ap()[0:6, :], C.PAV[1].ap()[0:6, 128:256]))
        wt(C, P, "tensor", C.AS2, cw7)
        op(C, P, "tensor", lambda e: e.matmul(
            C.PAV[0].ap()[0:PACK, 128:256], C.sel_sb.ap()[0:7, :], C.w7_sb.ap()[0:7, :],
            start=True, stop=True), C.T2)
        c_wfull = C.T2.n
        wt(C, P, "scalar", C.T2, c_wfull)
        a2op(C, P, lambda e: e.copy(C.h_sb.ap(), C.PAV[0].ap()[0:PACK, 0:128]), chain=False)
        S["c_hwf_ev"] = a2op(C, P, lambda e: e.copy(
            C.wf_sb.ap(), C.PAV[0].ap()[0:PACK, 128:256]), chain=False)
        wt(C, P, "vector", C.AS2, S["c_hwf_ev"])
        S["chw"][i] = v2op(C, P, lambda e: e.tensor_mul(C.hw_sb.ap(), C.h_sb.ap(), C.wf_sb.ap()))
        if i + 1 <= 15:
            seg_ln23(i + 1)
        wt(C, P, "tensor", C.AS2, S["c_hwf_ev"])
        for c in range(NDC):
            for fi in range(2, 4):
                op(C, P, "tensor", lambda e, c=c, fi=fi, bk=FB[fi]: e.matmul(
                    bk.ap()[:, 0:512],
                    xn2t[:, 128 * c:128 * c + 128],
                    wbase[:, 2048 * c + 512 * fi:2048 * c + 512 * fi + 512],
                    start=(c == 0), stop=False), None)
        wt(C, P, "tensor", C.VS2, S["chw"][i])
        for fi in range(4):
            op(C, P, "tensor", lambda e, fi=fi, bk=FB[fi]: e.matmul(
                bk.ap()[:, 0:512], C.hw_sb.ap(),
                C.ball_sb.ap()[:, 512 * fi:512 * fi + 512],
                start=False, stop=False), None)
        for fi in range(4):
            op(C, P, "tensor", lambda e, fi=fi, bk=FB[fi]: e.matmul(
                bk.ap()[:, 0:512], C.ones3_sb.ap()[64:65, :],
                C.bias2_sb.ap()[64:65, 512 * fi:512 * fi + 512],
                start=False, stop=True), C.T2 if fi == 3 else None)
        S["c_ffn"][i] = C.T2.n

    def seg_ln4_out(i):
        FB = [C.PS6[2], C.PS6[3], C.PAV[0], C.PAV[1]]
        sl = i % 2
        cf = S["c_ffn"][i]
        wt(C, P, "vector", C.T2, cf)
        for fi in range(4):
            v2op(C, P, lambda e, fi=fi, bk=FB[fi]: e.bn_stats(
                bnst[:, 6 * fi:6 * fi + 6], bk.ap()[:, 0:512]))
        cv = v2op(C, P, lambda e: e.bn_aggr(st[:, 0:2], bnst))
        wt(C, P, "scalar", C.VS2, cv)
        a2op(C, P, lambda e: e.activation(st[:, 2:3], st[:, 1:2], AF.Ln, bias=C.eps_sb.ap()[:, 0:1]))
        crs = a2op(C, P, lambda e: e.activation(st[:, 3:4], st[:, 2:3], AF.Exp, scale=-0.5))
        wt(C, P, "vector", C.AS2, crs)
        v2op(C, P, lambda e: e.tensor_tensor(st[:, 2:3], st[:, 0:1], st[:, 3:4], ALU.mult))
        wt(C, P, "vector", C.ST2, S["c_ost"].get(i - 1, 0))
        cn = 0
        for fi in (2, 3, 0, 1):
            cn = v2op(C, P, lambda e, fi=fi, bk=FB[fi]: e.tensor_scalar(
                outb[:, 512 * fi:512 * fi + 512], bk.ap()[:, 0:512],
                st[:, 3:4], st[:, 2:3], ALU.mult, ALU.subtract))
        S["ln4_norm"] = cn
        S["cLN4"][i] = cn
        cout = v2op(C, P, lambda e, sl=sl: e.tensor_add(outb, outb, x2[:, sl * D:sl * D + D]))
        wt(C, P, "gpsimd", C.VS2, cout)
        S["c_ost"][i] = dma(C, P, "gpsimd", C.ST2, C.out[128 * i:128 * i + 128, :], outb)

    seg_xload(0)
    seg_oload(0)
    seg_otr(0)
    seg_oload(1)
    for k in range(-1, 16):
        if k + 1 <= 15:
            seg_proj(k + 1)
        if k == -1:
            seg_ln23(0)
        if k >= 0:
            if k == 15:
                seg_xn2tr_tr(k)
                seg_xn2tr_ev(S["c_xntr_done"])
            seg_rh(k)
            seg_softmax_v(k)
            seg_ffn(k)
            seg_ln4_out(k)
        if k + 2 <= 15:
            seg_xload(k + 2)

    wt(C, P, "gpsimd", C.ST2, C.ST2.n)


def finish_debug(C):
    _close_block(C)


def _close_block(C):
    with C.nc.Block() as block:
        def run(ename):
            def f(e):
                if ename == "sync":
                    e.dma_start(out=C.rmt_sb.ap(), in_=C.rmt).then_inc(C.LD.h, 16)
                e.wait_ge(C.LD.h, C.c_rmt)
                r_role = e.alloc_register()
                e.reg_load(r_role, C.rmt_sb.ap()[0:1, 0:1])
                r_memb = e.alloc_register()
                e.reg_load(r_memb, C.rmt_sb.ap()[0:1, 1:2])
                regs = {"role": r_role, "memb": r_memb}
                if ename == "gpsimd":
                    C.GREGS["memb"] = r_memb
                    r_tok = e.alloc_register()
                    e.reg_load(r_tok, C.rmt_sb.ap()[0:1, 2:3])
                    r_pf = e.alloc_register()
                    r_plcnt = e.alloc_register()
                    r_cond = e.alloc_register()
                    e.reg_mov(r_plcnt, 0)
                    C.GREGS.update(tok=r_tok, pf=r_pf, plcnt=r_plcnt, cond=r_cond)
                _play(e, C.P.q[ename], regs)
            return f
        block.sync(run("sync"))
        block.tensor(run("tensor"))
        block.scalar(run("scalar"))
        block.vector(run("vector"))
        block.gpsimd(run("gpsimd"))
    for g in C._sem_guards:
        g.__exit__(None, None, None)


def build():
    C = make_context()
    alloc_phase1(C)
    C.GREGS = {}
    plan_prolog(C)

    semnames = C._semnames
    dl = DEBUG_LEVEL
    if dl == 1:
        finish_debug(C)
        return C
    limit = {2: "phase0", 25: "bar0", 26: "kvonly", 261: "kv1g", 27: "wq",
             28: "kn_novec", 29: "kn_nolnexp", 3: "kv",
             42: "bar0", 43: "kv", 44: None, 45: "tap45"}.get(dl, None)
    snap = {k: getattr(C, k).n for k in semnames}
    Pa = Plan()
    plan_phase0(C, Pa, 0)
    if limit != "phase0":
        plan_phase1(C, Pa, 0, limit=limit)
    cnt_a = {k: getattr(C, k).n for k in semnames}
    for k in semnames:
        getattr(C, k).n = snap[k]
    Pb = Plan()
    plan_phase0(C, Pb, 1)
    if limit != "phase0":
        plan_phase1(C, Pb, 1, limit=limit)
    cnt_b = {k: getattr(C, k).n for k in semnames}
    if limit is None:
        assert cnt_a["ARM"] == cnt_b["ARM"] == snap["ARM"] + 3, (cnt_a["ARM"], cnt_b["ARM"])
        assert cnt_a["BAR"] == cnt_b["BAR"], (cnt_a["BAR"], cnt_b["BAR"])
    if dl == 45:
        C.P.splice_role(Pa, Pb)
        finish_debug(C)
        return C
    for k in semnames:
        getattr(C, k).n = max(cnt_a[k], cnt_b[k])
    C.P.splice_role(Pa, Pb)
    P = C.P
    if dl == 42:
        wt(C, P, "gpsimd", C.BAR, min(cnt_a["BAR"], cnt_b["BAR"]))
        i0 = [("op", lambda e, s=C.GQ.h: e.dma_start(
            out=C.dbg.rearrange("(c p) t -> p c t", p=128),
            in_=C.SH_XNT.rearrange("(c p) t -> p c t", p=128)[:, :, 0:2048]).then_inc(s, 16))]
        i1 = [("op", lambda e, s=C.GQ.h: e.dma_start(
            out=C.dbg.rearrange("(c p) t -> p c t", p=128),
            in_=C.SH_XNT.rearrange("(c p) t -> p c t", p=128)[:, :, 4096:4096 + 2048]).then_inc(s, 16))]
        C.GQ.inc(16)
        P.br("gpsimd", "memb", i0, i1)
        wt(C, P, "gpsimd", C.GQ, C.GQ.n)
        finish_debug(C)
        return C
    if dl == 43:
        wt(C, P, "gpsimd", C.BAR, min(cnt_a["BAR"], cnt_b["BAR"]))
        dma(C, P, "gpsimd", C.GQ, C.dbg[0:128, 0:2048], C.K_sb.ap()[:, 0:2048])
        dma(C, P, "gpsimd", C.GQ, C.dbg[128:256, 0:2048], C.V_sb.ap()[:, 0:2048])
        dma(C, P, "gpsimd", C.GQ, C.dbg[384:385, 0:16], C.kn_sb.ap().bitcast(BF16)[0:1, 0:32][0:1, 0:16])
        wt(C, P, "gpsimd", C.GQ, C.GQ.n)
        finish_debug(C)
        return C
    if dl == 44:
        wt(C, P, "gpsimd", C.BAR, min(cnt_a["BAR"], cnt_b["BAR"]))
        i0 = [("op", lambda e, s=C.GQ.h: e.dma_start(
            out=C.dbg, in_=C.SH_O[0:2048, :]).then_inc(s, 16))]
        i1 = [("op", lambda e, s=C.GQ.h: e.dma_start(
            out=C.dbg, in_=C.SH_O[2048:4096, :]).then_inc(s, 16))]
        C.GQ.inc(16)
        P.br("gpsimd", "memb", i0, i1)
        wt(C, P, "gpsimd", C.GQ, C.GQ.n)
        finish_debug(C)
        return C
    if dl in (2, 25, 26, 261, 27, 28, 29, 3, 4):
        finish_debug(C)
        return C

    for name in ("sqq_sb", "negb_sb", "Ost_sb", "Pm_sb", "Pt_sb", "Q_sb",
                 "wkv_sb", "xs_sb", "V_sb", "K_sb"):
        C.sbfree(name)
    C.sbuf("wproj_sb", (128, NDC * 2048), BF16)
    C.sbuf("wbase_sb", (128, NDC * 2048), BF16)
    C.sbuf("xres_sb", (128, D), F32)
    C.sbuf("o_sb", (128, D), BF16)
    C.sbuf("ot_sb", (128, D), BF16)
    C.sbuf("big_sb", (128, D), F32)
    C.sbuf("x2_sb", (128, 2 * D), F32)
    C.sbuf("xn2_sb", (128, D), BF16)
    C.sbuf("out_sb", (128, D), F32)
    C.sbuf("g8_sb", (128, 8), F32)
    C.sbuf("e1_sb", (128, 8), F32)
    C.sbuf("e2_sb", (128, 8), F32)
    C.sbuf("w2_sb", (128, 8), F32)
    C.sbuf("h_sb", (PACK, 128), BF16)
    C.sbuf("wf_sb", (PACK, 128), BF16)
    C.sbuf("hw_sb", (PACK, 128), BF16)
    C.sbuf("aall_sb", (128, NDC * PACK), BF16)
    C.sbuf("ball_sb", (PACK, D), BF16)
    C.sbuf("wrout_sb", (128, NDC * 8), BF16)
    C.sbuf("sel_sb", (8, PACK), BF16)
    C.sbuf("w7_sb", (8, 128), BF16)
    C.sbuf("bias2_sb", (128, D), BF16)
    C.sbuf("ones3_sb", (128, 128), BF16)
    C.sbuf("bnst_sb", (128, 24), F32)
    C.sbuf("xn2t_sb", (128, D), BF16)

    plan_phase2(C, C.P)
    _close_block(C)
    return C


_NC_CACHE = [None]
DEBUG_LEVEL = 0


def kernel(**L):
    xin = np.asarray(L["L0_attn_x"])
    w_qkv = np.asarray(L["L0_attn_w_qkv"])
    b_qkv = np.asarray(L["L0_attn_b_qkv"])
    w_proj = np.asarray(L["L0_attn_w_proj"])
    b_proj = np.asarray(L["L0_attn_b_proj"])
    W_base = np.asarray(L["L0_ffn_W_base"])
    b_base = np.asarray(L["L0_ffn_b_base"])
    A_s = np.asarray(L["L0_ffn_A_s"])
    B_s = np.asarray(L["L0_ffn_B_s"])
    A_r = np.asarray(L["L0_ffn_A_r"])
    B_r = np.asarray(L["L0_ffn_B_r"])
    W_router = np.asarray(L["L0_ffn_W_router"])
    b_router = np.asarray(L["L0_ffn_b_router"])

    if _NC_CACHE[0] is None:
        _NC_CACHE[0] = build()
    C = _NC_CACHE[0]

    token = int(np.random.randint(1, 2**31 - 1))
    in_maps = []
    rows_per_core = []
    for c in range(8):
        p, m = c // 2, c % 2
        b, role = p // 2, p % 2
        n = LN_N[role]
        xl = np.zeros((4096, D), dtype=np.float32)
        xl[:n] = xin[b, m * n:m * n + n]
        if role == 0:
            rows = np.r_[0:2048] if m == 0 else np.r_[2048:E1, E2:T]
        else:
            rows = np.r_[E1:E1 + 2048] if m == 0 else np.r_[E1 + 2048:E2]
        rows_per_core.append((b, rows))
        hcols = np.r_[1024 * m:1024 * m + 1024]
        wq = np.concatenate([w_qkv[:, hcols], w_qkv[:, 2048 + hcols], w_qkv[:, 4096 + hcols]], axis=1)
        bq = np.concatenate([b_qkv[hcols], b_qkv[2048 + hcols], b_qkv[4096 + hcols]])[None, :]
        in_maps.append({
            "x_ln": np.ascontiguousarray(xl),
            "x_res": np.ascontiguousarray(xin[b, rows]),
            "wqkv": np.ascontiguousarray(wq),
            "bqkv": np.ascontiguousarray(bq),
            "wproj_half": np.ascontiguousarray(w_proj[1024 * m:1024 * m + 1024]),
            "wbase_half": np.ascontiguousarray(W_base[1024 * m:1024 * m + 1024]),
            "bproj": np.ascontiguousarray(b_proj[None, :]),
            "bbase": np.ascontiguousarray(b_base[None, :]),
            "a_s": np.ascontiguousarray(A_s),
            "b_s": np.ascontiguousarray(B_s),
            "a_r": np.ascontiguousarray(A_r),
            "b_r": np.ascontiguousarray(B_r),
            "wrouter": np.ascontiguousarray(W_router),
            "brouter": np.ascontiguousarray(b_router[None, :]),
            "rmt": np.array([[role, m, token, 0]], dtype=np.int32),
        })

    res = run_bass_kernel_spmd(C.nc, in_maps, core_ids=list(range(8)))
    outv = np.zeros((B, T, D), dtype=np.float32)
    for c in range(8):
        b, rows = rows_per_core[c]
        outv[b, rows] = res.results[c]["out"]
    if DEBUG_LEVEL:
        kernel.dbg = [np.asarray(res.results[c]["dbg"]) for c in range(8)]
    return outv


# version 21
# speedup vs baseline: 2.5369x; 2.5369x over previous
import numpy as np
import ml_dtypes
from types import SimpleNamespace
import concourse.bass as bass
import concourse.mybir as mybir
from concourse.bass_utils import run_bass_kernel_spmd

F32 = mybir.dt.float32
BF16 = mybir.dt.bfloat16
I32 = mybir.dt.int32
AF = mybir.ActivationFunctionType
ALU = mybir.AluOpType
AX = mybir.AxisListType

B, T, D = 2, 8192, 2048
NDC = D // 128
CH = 512
SC = 256
E1, E2 = 2560, 6656
OUTER_SLOTS = [0, 1, 2, 3, 4, 13, 14, 15]
MIDDLE_SLOTS = [5, 6, 7, 8, 9, 10, 11, 12]
KV_RANGE = {0: T, 1: E2}
LN_N = {0: 4096, 1: 3328}
ATTN_MULT = 1.0 / float(np.sqrt(128.0))
NEG_BIG = -1e30
EPS = 1e-5
PACK = 112
ENGS = ("sync", "tensor", "scalar", "vector", "gpsimd")
ATT_J = 8


class Sem:
    def __init__(self, h):
        self.h = h
        self.n = 0

    def inc(self, k):
        self.n += k
        return self.n


class Plan:
    def __init__(self):
        self.q = {e: [] for e in ENGS}

    def em(self, e, fn):
        self.q[e].append(("op", fn))

    def br(self, e, key, it0, it1):
        self.q[e].append(("br", key, it0, it1))

    def splice_role(self, pa, pb):
        for e in ENGS:
            self.q[e].append(("br", "role", pa.q[e], pb.q[e]))


def _play(e, items, regs):
    for it in items:
        if it[0] == "op":
            it[1](e)
        else:
            _, key, it0, it1 = it
            with e.If_eq(regs[key], 0):
                _play(e, it0, regs)
            with e.If_ne(regs[key], 0):
                _play(e, it1, regs)


def dma(C, P, eng, sem, o, i):
    c = sem.inc(16)
    P.em(eng, lambda e, o=o, i=i, s=sem.h: e.dma_start(out=o, in_=i).then_inc(s, 16))
    return c


def dma_br(C, P, eng, sem, o0, o1, i):
    c = sem.inc(16)
    it0 = [("op", lambda e, o=o0, i=i, s=sem.h: e.dma_start(out=o, in_=i).then_inc(s, 16))]
    it1 = [("op", lambda e, o=o1, i=i, s=sem.h: e.dma_start(out=o, in_=i).then_inc(s, 16))]
    P.br(eng, "memb", it0, it1)
    return c


def wt(C, P, eng, sem, val=None):
    v = sem.n if val is None else val
    if v > 0:
        P.em(eng, lambda e, s=sem.h, v=v: e.wait_ge(s, v))


def op(C, P, eng, fn, sem=None, k=1):
    if sem is None:
        P.em(eng, lambda e, f=fn: f(e))
        return None
    c = sem.inc(k)
    P.em(eng, lambda e, f=fn, s=sem.h, k=k: f(e).then_inc(s, k))
    return c


def vop(C, P, fn, sem=None):
    wt(C, P, "vector", C.VS)
    return op(C, P, "vector", fn, sem if sem is not None else C.VS)


def aop(C, P, fn, chain=True):
    if chain:
        wt(C, P, "scalar", C.AS)
    return op(C, P, "scalar", fn, C.AS)


def v2op(C, P, fn):
    wt(C, P, "vector", C.VS2)
    return op(C, P, "vector", fn, C.VS2)


def a2op(C, P, fn, chain=True):
    if chain:
        wt(C, P, "scalar", C.AS2)
    return op(C, P, "scalar", fn, C.AS2)


def make_context():
    nc = bass.Bass("TRN2", target_bir_lowering=False)
    C = SimpleNamespace()
    C.nc = nc
    C.P = Plan()

    def din(name, shape, dtype=F32):
        a = nc.dram_tensor(name, list(shape), dtype, kind="ExternalInput").ap()
        setattr(C, name, a)

    din("x_ln", (4096, D))
    din("x_res", (2048, D))
    din("wqkv", (D, 3072))
    din("bqkv", (1, 3072))
    din("wproj_half", (1024, D))
    din("wbase_half", (1024, D))
    din("bproj", (1, D))
    din("bbase", (1, D))
    din("a_s", (D, 16))
    din("b_s", (16, D))
    din("a_r", (6, D, 16))
    din("b_r", (6, 16, D))
    din("wrouter", (D, 6))
    din("brouter", (1, 6))
    din("rmt", (1, 4), I32)

    C.out = nc.dram_tensor("out", [2048, D], F32, kind="ExternalOutput").ap()
    if DEBUG_LEVEL:
        C.dbg = nc.dram_tensor("dbg", [2048, D], BF16, kind="ExternalOutput").ap()

    C.MY_WQKV = nc.dram_tensor("MY_WQKV", [D, 3072], BF16, kind="Internal").ap()
    C.D_AALL = nc.dram_tensor("D_AALL", [128, NDC * PACK], BF16, kind="Internal").ap()
    C.D_BALL = nc.dram_tensor("D_BALL", [PACK, D], BF16, kind="Internal").ap()
    C.D_WROUT = nc.dram_tensor("D_WROUT", [128, NDC * 8], BF16, kind="Internal").ap()
    C.D_BIAS = nc.dram_tensor("D_BIAS", [4, D], BF16, kind="Internal").ap()
    C.SH_XNT = nc.dram_tensor("SH_XNT", [D, T], BF16, kind="Internal", addr_space="Shared").ap()
    C.SH_O = nc.dram_tensor("SH_O", [4096, D], BF16, kind="Internal", addr_space="Shared").ap()
    C.SH_WP = nc.dram_tensor("SH_WP", [D, D], BF16, kind="Internal", addr_space="Shared").ap()
    C.SH_WB = nc.dram_tensor("SH_WB", [D, D], BF16, kind="Internal", addr_space="Shared").ap()
    C.SH_FLAGS = nc.dram_tensor("SH_FLAGS", [2, 4, 16], I32, kind="Internal", addr_space="Shared").ap()

    C.ident_bf = nc.inline_tensor(np.eye(128, dtype=ml_dtypes.bfloat16), "c_identbf").ap()
    C.ident_f32 = nc.inline_tensor(np.eye(128, dtype=np.float32), "c_identf").ap()
    C.ones_col = nc.inline_tensor(np.ones((128, 1), dtype=ml_dtypes.bfloat16), "c_onescol").ap()
    C.ones_row = nc.inline_tensor(np.ones((1, 128), dtype=ml_dtypes.bfloat16), "c_onesrow").ap()
    mk = np.zeros((4, 128, CH), dtype=ml_dtypes.bfloat16)
    for r in range(4):
        for k in range(128):
            lo = 128 * r + k
            if lo < CH:
                mk[r, k, lo:] = 1.0
    C.masks_c = nc.inline_tensor(np.ascontiguousarray(mk), "c_masks").ap()
    sel = np.zeros((8, PACK), dtype=ml_dtypes.bfloat16)
    for ei in range(7):
        sel[ei, 16 * ei:16 * ei + 16] = 1.0
    C.sel_c = nc.inline_tensor(sel, "c_sel").ap()

    C._sem_guards = []
    C._semnames = []

    def S(name):
        g = nc.semaphore(name)
        h = g.__enter__()
        C._sem_guards.append(g)
        s = Sem(h)
        setattr(C, name, s)
        C._semnames.append(name)

    for name in ("LD", "LDW", "STQ", "GQ", "PL", "VS", "AS", "TS", "EXPS",
                 "AVS", "NRM", "KVE", "BAR", "ARM",
                 "LD2", "ST2", "VS2", "AS2", "T2"):
        S(name)

    C._sb_guards = {}

    def sbuf(name, shape, dtype, offset=None):
        if offset is None:
            g = nc.sbuf_tensor(name, list(shape), dtype)
            h = g.__enter__()
            C._sb_guards[name] = g
        else:
            h = nc.alloc_sbuf_tensor_at(name, list(shape), dtype, offset=offset)
        setattr(C, name, h)
        return h

    def sbfree(name):
        g = C._sb_guards.pop(name, None)
        if g is not None:
            g.__exit__(None, None, None)

    C.sbuf = sbuf
    C.sbfree = sbfree
    sbuf("rmt_sb", (1, 4), I32)
    sbuf("eps_sb", (128, 1), F32)
    sbuf("idb_sb", (128, 128), BF16)
    sbuf("idf_sb", (128, 128), F32)
    sbuf("onc_sb", (128, 1), BF16)
    sbuf("onr_sb", (1, 128), BF16)
    sbuf("masks_sb", (128, 4 * CH), BF16)
    sbuf("bqkvc_sb", (128, 16), F32)
    sbuf("bvrow_sb", (1, 1024), BF16)
    sbuf("kn_sb", (1, 16), F32)
    sbuf("tiny_sb", (128, 16), F32)
    sbuf("bnst0_sb", (128, 24), F32)
    sbuf("tinyb_sb", (1, 8), BF16)
    sbuf("gz_sb", (1, 1), I32)
    sbuf("pf_sb", (1, 16), I32)

    C.PS6 = [nc.alloc_psum_tensor(f"ps6_{i}", [128, 512], F32) for i in range(6)]
    C.PAV = [nc.alloc_psum_tensor(f"pav{i}", [128, 512], F32) for i in range(2)]
    C.PBF = [C.PS6[4].bitcast(BF16), C.PS6[5].bitcast(BF16)]
    return C


def alloc_phase1(C):
    C.sbuf("K_sb", (128, 4 * T), BF16)
    C.sbuf("V_sb", (128, 64 * 516), BF16)
    C.sbuf("xs_sb", (128, 2 * NDC * SC), BF16)
    C.sbuf("wkv_sb", (128, NDC * 1024), BF16)
    C.sbuf("Q_sb", (128, 2 * 4 * CH), BF16)
    C.sbuf("Pt_sb", (128, 4 * CH), BF16)
    C.sbuf("Pm_sb", (128, 2 * CH), BF16)
    C.sbuf("Ost_sb", (128, 2048), BF16)
    C.sbuf("negb_sb", (128, 8), F32)
    C.sbuf("wq_sb", (128, NDC * 512), BF16, offset=C.nc.lookup_mloc(C.wkv_sb).addr)
    C.sbuf("sqq_sb", (128, 2 * CH), BF16)
    kof = C.nc.lookup_mloc(C.K_sb).addr
    vof = C.nc.lookup_mloc(C.V_sb).addr
    C.sbuf("x0_sb", (128, 2 * D), F32, offset=kof)
    C.sbuf("xn_sb", (128, D), BF16, offset=kof + 16384)
    C.sbuf("sq0_sb", (128, D), BF16, offset=kof + 16384 + 4096)
    C.sbuf("xnt_sb", (128, 2 * NDC * CH), BF16, offset=kof + 24576)
    C.sbuf("stg_sb", (128, 2048), F32, offset=vof)
    C.sbuf("stgb_sb", (128, 2048), BF16, offset=vof + 8192)


def plan_prolog(C):
    P = C.P
    C.LD.inc(16)
    C.c_rmt = C.LD.n

    dma(C, P, "sync", C.LD, C.idb_sb.ap(), C.ident_bf)
    dma(C, P, "sync", C.LD, C.idf_sb.ap(), C.ident_f32)
    dma(C, P, "sync", C.LD, C.onc_sb.ap(), C.ones_col)
    dma(C, P, "sync", C.LD, C.onr_sb.ap(), C.ones_row)
    for r in range(4):
        dma(C, P, "sync", C.LD, C.masks_sb.ap()[:, r * CH:(r + 1) * CH], C.masks_c[r])

    c_msets = vop(C, P, lambda e: e.memset(C.eps_sb.ap(), EPS))

    stg = C.stg_sb.ap()
    stgb = C.stgb_sb.ap()
    stgA = stg[:, 0:NDC * PACK].rearrange("p (c k) -> p c k", k=PACK)
    for ei in range(6):
        dma(C, P, "sync", C.LD, stgA[:, :, 16 * ei:16 * ei + 16],
            C.a_r[ei].rearrange("(c p) r -> p c r", p=128))
    c1 = dma(C, P, "sync", C.LD, stgA[:, :, 96:112], C.a_s.rearrange("(c p) r -> p c r", p=128))
    wt(C, P, "scalar", C.LD, c1)
    cA = aop(C, P, lambda e: e.copy(stgb[:, 0:NDC * PACK], stg[:, 0:NDC * PACK]))
    wt(C, P, "scalar", C.AS, cA)
    dma(C, P, "scalar", C.STQ, C.D_AALL, stgb[:, 0:NDC * PACK])
    wt(C, P, "sync", C.AS, cA)
    for ei in range(6):
        dma(C, P, "sync", C.LD, stg[16 * ei:16 * ei + 16, 0:D], C.b_r[ei])
    c1 = dma(C, P, "sync", C.LD, stg[96:112, 0:D], C.b_s)
    wt(C, P, "scalar", C.LD, c1)
    cB = aop(C, P, lambda e: e.copy(stgb[0:PACK, 0:D], stg[0:PACK, 0:D]))
    wt(C, P, "scalar", C.AS, cB)
    dma(C, P, "scalar", C.STQ, C.D_BALL, stgb[0:PACK, 0:D])
    wt(C, P, "sync", C.AS, cB)
    cz = vop(C, P, lambda e: e.memset(stgb[:, 0:NDC * 8], 0.0))
    stgR = stg[:, 0:NDC * 6].rearrange("p (c r) -> p c r", r=6)
    c1 = dma(C, P, "sync", C.LD, stgR, C.wrouter.rearrange("(c p) r -> p c r", p=128))
    wt(C, P, "scalar", C.LD, c1)
    wt(C, P, "scalar", C.VS, cz)
    cR = aop(C, P, lambda e: e.copy(
        stgb[:, 0:NDC * 8].rearrange("p (c r) -> p c r", r=8)[:, :, 0:6], stgR))
    wt(C, P, "scalar", C.AS, cR)
    dma(C, P, "scalar", C.STQ, C.D_WROUT, stgb[:, 0:NDC * 8])
    wt(C, P, "sync", C.AS, cR)
    c1 = dma(C, P, "sync", C.LD, stg[0:1, 0:D], C.bproj)
    wt(C, P, "scalar", C.LD, c1)
    cb = aop(C, P, lambda e: e.copy(stgb[0:1, 0:D], stg[0:1, 0:D]))
    wt(C, P, "scalar", C.AS, cb)
    cs1 = dma(C, P, "scalar", C.STQ, C.D_BIAS[0:1, :], stgb[0:1, 0:D])
    wt(C, P, "sync", C.AS, cb)
    c1 = dma(C, P, "sync", C.LD, stg[0:1, 0:D], C.bbase)
    wt(C, P, "scalar", C.LD, c1)
    wt(C, P, "scalar", C.STQ, cs1)
    cb = aop(C, P, lambda e: e.copy(stgb[0:1, 0:D], stg[0:1, 0:D]))
    wt(C, P, "scalar", C.AS, cb)
    cs1 = dma(C, P, "scalar", C.STQ, C.D_BIAS[1:2, :], stgb[0:1, 0:D])
    wt(C, P, "sync", C.AS, cb)
    c1 = dma(C, P, "sync", C.LD, stg[0:1, 0:6], C.brouter)
    wt(C, P, "vector", C.STQ, cs1)
    cbm = vop(C, P, lambda e: e.memset(stgb[0:1, 0:8], NEG_BIG))
    wt(C, P, "scalar", C.LD, c1)
    wt(C, P, "scalar", C.VS, cbm)
    wt(C, P, "scalar", C.STQ, cs1)
    cb = aop(C, P, lambda e: e.copy(stgb[0:1, 0:6], stg[0:1, 0:6]))
    wt(C, P, "scalar", C.AS, cb)
    cs1 = dma(C, P, "scalar", C.STQ, C.D_BIAS[2:3, 0:8], stgb[0:1, 0:8])
    wt(C, P, "sync", C.AS, cb)
    c1 = dma(C, P, "sync", C.LD, stg[0:1, 0:1024], C.bqkv[0:1, 2048:3072])
    wt(C, P, "scalar", C.LD, c1)
    aop(C, P, lambda e: e.copy(C.bvrow_sb.ap(), stg[0:1, 0:1024]))
    C.c_smalls_stq = C.STQ.n
    wt(C, P, "sync", C.AS, C.AS.n)
    c1 = dma(C, P, "sync", C.LD, stg[0:1, 0:2048], C.bqkv[0:1, 0:2048])
    wt(C, P, "tensor", C.LD, c1)
    for c in range(16):
        op(C, P, "tensor", lambda e, c=c: e.transpose(
            C.PS6[0].ap()[:, c:c + 1], stg[0:1, 128 * c:128 * c + 128],
            C.idf_sb.ap()[0:1, 0:1]), C.TS if c == 15 else None)
    c_biasT = C.TS.n
    wt(C, P, "scalar", C.TS, c_biasT)
    cqb = aop(C, P, lambda e: e.copy(C.bqkvc_sb.ap(), C.PS6[0].ap()[:, 0:16]))
    wt(C, P, "vector", C.AS, cqb)
    vop(C, P, lambda e: e.tensor_scalar_mul(C.bqkvc_sb.ap()[:, 0:8], C.bqkvc_sb.ap()[:, 0:8], ATTN_MULT))
    C.c_bias_ps_free = cqb

    jobs = []
    for c in range(NDC):
        for part in range(3):
            src = C.wqkv[128 * c:128 * c + 128, 1024 * part:1024 * part + 1024]
            dst = C.MY_WQKV[128 * c:128 * c + 128, 1024 * part:1024 * part + 1024]
            jobs.append((src, dst, None))
    for r8 in range(8):
        for halfc in range(2):
            cs = slice(1024 * halfc, 1024 * halfc + 1024)
            jobs.append((C.wproj_half[128 * r8:128 * r8 + 128, cs],
                         C.SH_WP[128 * r8:128 * r8 + 128, cs],
                         C.SH_WP[1024 + 128 * r8:1024 + 128 * r8 + 128, cs]))
            jobs.append((C.wbase_half[128 * r8:128 * r8 + 128, cs],
                         C.SH_WB[128 * r8:128 * r8 + 128, cs],
                         C.SH_WB[1024 + 128 * r8:1024 + 128 * r8 + 128, cs]))

    c_stg_done = C.AS.n
    prev_store = {0: 0, 1: 0}
    for j, (src, d0, d1) in enumerate(jobs):
        sl = j % 2
        sslc = stg[:, 1024 * sl:1024 * sl + 1024]
        bslc = C.stgb_sb.ap()[:, 1024 * sl:1024 * sl + 1024]
        if j < 2:
            wt(C, P, "sync", C.AS, c_stg_done)
            wt(C, P, "sync", C.TS, c_biasT)
            wt(C, P, "sync", C.STQ, C.c_smalls_stq)
        wt(C, P, "sync", C.STQ, prev_store[sl])
        cl = dma(C, P, "sync", C.LDW, sslc, src)
        wt(C, P, "scalar", C.LDW, cl)
        wt(C, P, "scalar", C.STQ, prev_store[sl])
        if j < 2:
            wt(C, P, "scalar", C.STQ, C.c_smalls_stq)
        ccast = aop(C, P, lambda e, o=bslc, i=sslc: e.copy(o, i))
        wt(C, P, "scalar", C.AS, ccast)
        if d1 is None:
            cst = dma(C, P, "scalar", C.STQ, d0, bslc)
        else:
            cst = dma_br(C, P, "scalar", C.STQ, d0, d1, bslc)
        prev_store[sl] = cst
    C.c_wconv = C.STQ.n


def barrier(C, P, slot, gq_waits):
    for s, cval in gq_waits:
        wt(C, P, "gpsimd", s, cval)
    cf = C.GQ.inc(16)
    i0 = [("op", lambda e, s=C.GQ.h, sl=slot: e.dma_start(
        out=C.SH_FLAGS[0:1, sl, 0:1], in_=C.rmt_sb.ap()[0:1, 2:3]).then_inc(s, 16))]
    i1 = [("op", lambda e, s=C.GQ.h, sl=slot: e.dma_start(
        out=C.SH_FLAGS[1:2, sl, 0:1], in_=C.rmt_sb.ap()[0:1, 2:3]).then_inc(s, 16))]
    P.br("gpsimd", "memb", i0, i1)
    wt(C, P, "gpsimd", C.GQ, cf)

    def _poll(e, sl=slot):
        R = C.GREGS

        def cond():
            with e.If_eq(R["memb"], 0):
                e.dma_start(out=C.pf_sb.ap()[0:1, 0:1], in_=C.SH_FLAGS[1:2, sl, 0:1]).then_inc(C.PL.h, 16)
            with e.If_ne(R["memb"], 0):
                e.dma_start(out=C.pf_sb.ap()[0:1, 0:1], in_=C.SH_FLAGS[0:1, sl, 0:1]).then_inc(C.PL.h, 16)
            e.reg_alu(R["plcnt"], R["plcnt"], 16, ALU.add)
            e.wait_ge(C.PL.h, R["plcnt"])
            e.reg_load(R["pf"], C.pf_sb.ap()[0:1, 0:1])
            e.reg_alu(R["cond"], R["pf"], R["tok"], ALU.not_equal)
            return R["cond"]

        with e.While(cond):
            pass

    P.em("gpsimd", _poll)
    return op(C, P, "gpsimd", lambda e: e.memset(C.gz_sb.ap(), 0), C.BAR, 1)


def plan_phase0(C, P, role):
    nln = LN_N[role]
    nblk = nln // 128
    x0 = C.x0_sb.ap()
    xn = C.xn_sb.ap()
    sq0 = C.sq0_sb.ap()
    st = C.tiny_sb.ap()

    vlast = {0: 0, 1: 0}
    alast = {0: 0, 1: 0}
    evlast = {0: 0, 1: 0}
    stlast = {0: 0, 1: 0}
    cload = []
    tlast = 0
    c_ss_prev = 0
    xntg = C.xnt_sb.ap().rearrange("p (g c t) -> p g c t", g=2, t=CH)
    dstr = C.SH_XNT.rearrange("(c p) t -> p c t", p=128)
    for i in range(nblk):
        par = i % 2
        if i == 0:
            cload.append(dma(C, P, "sync", C.LD, x0[:, 0:D], C.x_ln[0:128, :]))
        if i + 1 < nblk:
            npar = (i + 1) % 2
            wt(C, P, "sync", C.VS, vlast[npar])
            cload.append(dma(C, P, "sync", C.LD, x0[:, npar * D:npar * D + D],
                             C.x_ln[128 * (i + 1):128 * (i + 1) + 128, :]))
        xv = x0[:, par * D:par * D + D]
        wt(C, P, "vector", C.LD, cload[i])
        bnst = C.bnst0_sb.ap()
        for q in range(4):
            vop(C, P, lambda e, xv=xv, q=q: e.bn_stats(
                bnst[:, 6 * q:6 * q + 6], xv[:, 512 * q:512 * q + 512]))
        cvar = vop(C, P, lambda e: e.bn_aggr(st[:, 0:2], bnst))
        alast[par] = 0
        wt(C, P, "scalar", C.VS, cvar)
        aop(C, P, lambda e: e.activation(st[:, 2:3], st[:, 1:2], AF.Ln, bias=C.eps_sb.ap()[:, 0:1]))
        crs = aop(C, P, lambda e: e.activation(st[:, 3:4], st[:, 2:3], AF.Exp, scale=-0.5))
        wt(C, P, "vector", C.AS, crs)
        vop(C, P, lambda e: e.tensor_tensor(st[:, 2:3], st[:, 0:1], st[:, 3:4], ALU.mult))
        wt(C, P, "vector", C.TS, tlast)
        cxn = vop(C, P, lambda e, xv=xv: e.tensor_scalar(
            xn, xv, st[:, 3:4], st[:, 2:3], ALU.mult, ALU.subtract))
        vlast[par] = cxn
        gslot = (i // 4) % 2
        toff = (i % 4) * 128
        for half in range(2):
            pb = C.PBF[half].ap()
            wt(C, P, "tensor", C.VS, cxn)
            wt(C, P, "tensor", C.AS, evlast[half])
            for cc in range(8):
                c = half * 8 + cc
                op(C, P, "tensor", lambda e, c=c, cc=cc, pb=pb: e.transpose(
                    pb[:, 128 * cc:128 * cc + 128], xn[:, 128 * c:128 * c + 128], C.idb_sb.ap()),
                   C.TS if cc == 7 else None)
            ct = C.TS.n
            wt(C, P, "scalar", C.TS, ct)
            wt(C, P, "scalar", C.STQ, stlast[gslot])
            ce = aop(C, P, lambda e, pb=pb, half=half, gslot=gslot, toff=toff: e.copy(
                xntg[:, gslot, 8 * half:8 * half + 8, toff:toff + 128],
                pb.rearrange("p (c t) -> p c t", t=128)))
            evlast[half] = ce
        tlast = C.TS.n
        if i % 4 == 3 or i == nblk - 1:
            tg0 = (i // 4) * CH
            tglen = (i % 4 + 1) * 128
            wt(C, P, "scalar", C.AS, C.AS.n)
            cst = C.STQ.inc(16)
            i0 = [("op", lambda e, tg0=tg0, tglen=tglen, gslot=gslot, s=C.STQ.h: e.dma_start(
                out=dstr[:, :, tg0:tg0 + tglen],
                in_=xntg[:, gslot, :, 0:tglen]).then_inc(s, 16))]
            i1 = [("op", lambda e, tg0=tg0, tglen=tglen, gslot=gslot, nln=nln, s=C.STQ.h: e.dma_start(
                out=dstr[:, :, nln + tg0:nln + tg0 + tglen],
                in_=xntg[:, gslot, :, 0:tglen]).then_inc(s, 16))]
            P.br("scalar", "memb", i0, i1)
            stlast[gslot] = cst


def plan_phase1(C, P, role, limit=None):
    slots = OUTER_SLOTS if role == 0 else MIDDLE_SLOTS
    kvr = KV_RANGE[role]
    nsc = kvr // SC
    xsr = C.xs_sb.ap().rearrange("p (s c t) -> p s c t", s=2, t=SC)
    shx = C.SH_XNT.rearrange("(c p) t -> p c t", p=128)
    Kap = C.K_sb.ap()
    Vap = C.V_sb.ap()
    st = C.tiny_sb.ap()
    PS = C.PS6
    PAV = C.PAV

    barrier(C, P, 0, [(C.STQ, C.STQ.n)])
    bar0 = C.BAR.n

    wt(C, P, "vector", C.BAR, bar0)
    vset = vop(C, P, lambda e: e.memset(
        Vap.rearrange("p (k h d) -> p k h d", h=4, d=129)[:, :, :, 128:129], 1.0))

    srot = [0]
    exp_of_bank = {i: 0 for i in range(6)}
    nrm_last = 0
    ost_stored = 0
    pt_used = {i: 0 for i in range(4)}
    pm_used = {0: 0, 1: 0}
    xs_free = 0
    sqq_ts = {0: 0, 1: 0, 2: 0, 3: 0}
    kv_ev = 0
    kn_vread = 0

    for g in range(2):
        wt(C, P, "sync", C.STQ, C.c_wconv)
        wt(C, P, "sync", C.BAR, bar0)
        wt(C, P, "sync", C.TS, xs_free)
        ck = 0
        for c in range(NDC):
            ck = dma(C, P, "sync", C.LD, C.wkv_sb.ap()[:, 1024 * c:1024 * c + 512],
                     C.MY_WQKV[128 * c:128 * c + 128, 1024 + 512 * g:1024 + 512 * g + 512])
            ck = dma(C, P, "sync", C.LD, C.wkv_sb.ap()[:, 1024 * c + 512:1024 * c + 1024],
                     C.MY_WQKV[128 * c:128 * c + 128, 2048 + 512 * g:2048 + 512 * g + 512])
        c_wkv = ck

        kv_ts = {0: xs_free, 1: xs_free}
        knts = {0: 0, 1: 0, 2: 0, 3: 0}
        vkn = 0
        for si in range(nsc):
            par = si % 2
            wt(C, P, "sync", C.TS, kv_ts[par])
            base = si * SC
            cl = 0
            for qq in range(4):
                cl = dma(C, P, "sync", C.LD, xsr[:, par, 4 * qq:4 * qq + 4, :],
                         shx[:, 4 * qq:4 * qq + 4, base:base + SC])
            wt(C, P, "tensor", C.LD, cl)
            if g == 0 and si == 0:
                wt(C, P, "tensor", C.AS, C.c_bias_ps_free)
            if si == 0:
                wt(C, P, "tensor", C.LD, c_wkv)
                wt(C, P, "tensor", C.VS, kn_vread)
                wt(C, P, "tensor", C.NRM, nrm_last)
                wt(C, P, "tensor", C.EXPS, max(exp_of_bank.values()))
            wt(C, P, "tensor", C.KVE, kv_ev)
            for h in range(4):
                for c in range(NDC):
                    op(C, P, "tensor", lambda e, h=h, c=c, par=par: e.matmul(
                        PS[h].ap()[:, 0:SC],
                        C.wkv_sb.ap()[:, 1024 * c + 128 * h:1024 * c + 128 * h + 128],
                        C.xs_sb.ap()[:, (par * NDC + c) * SC:(par * NDC + c) * SC + SC],
                        start=(c == 0), stop=(c == NDC - 1)), None)
            for tb in range(2):
                for c in range(NDC):
                    op(C, P, "tensor", lambda e, tb=tb, c=c, par=par: e.matmul(
                        PAV[tb].ap()[:, 0:512],
                        C.xs_sb.ap()[:, (par * NDC + c) * SC + 128 * tb:(par * NDC + c) * SC + 128 * tb + 128],
                        C.wkv_sb.ap()[:, 1024 * c + 512:1024 * c + 1024],
                        start=(c == 0), stop=False), None)
                op(C, P, "tensor", lambda e, tb=tb, g=g: e.matmul(
                    PAV[tb].ap()[:, 0:512], C.onr_sb.ap()[0:1, :],
                    C.bvrow_sb.ap()[0:1, 512 * g:512 * g + 512],
                    start=False, stop=True), C.TS if tb == 1 else None)
            kv_ts[par] = C.TS.n
            wt(C, P, "scalar", C.TS, kv_ts[par])
            for h in range(4):
                aop(C, P, lambda e, h=h, base=base, g=g: e.activation(
                    Kap[:, h * T + base:h * T + base + SC], PS[h].ap()[:, 0:SC],
                    AF.Identity, bias=C.bqkvc_sb.ap()[:, 8 + 4 * g + h:8 + 4 * g + h + 1]), chain=False)
            c_kev = C.AS.n
            for tb in range(2):
                kb = base // 128 + tb
                for h in range(4):
                    op(C, P, "scalar", lambda e, kb=kb, h=h, tb=tb: e.copy(
                        Vap[:, 516 * kb + 129 * h:516 * kb + 129 * h + 128],
                        PAV[tb].ap()[:, 128 * h:128 * h + 128]),
                        C.KVE if (tb == 1 and h == 3) else None)
            kv_ev = C.KVE.n
            wt(C, P, "vector", C.AS, c_kev)
            csq = 0
            for h in range(4):
                wt(C, P, "vector", C.TS, sqq_ts[h])
                csq = vop(C, P, lambda e, h=h, base=base: e.tensor_mul(
                    C.sqq_sb.ap()[:, h * SC:h * SC + SC],
                    Kap[:, h * T + base:h * T + base + SC],
                    Kap[:, h * T + base:h * T + base + SC]))
            wt(C, P, "tensor", C.VS, csq)
            for h in range(4):
                wt(C, P, "tensor", C.VS, knts[h])
                op(C, P, "tensor", lambda e, h=h: e.matmul(
                    PS[4 + h // 2].ap()[0:1, (h % 2) * SC:(h % 2) * SC + SC],
                    C.onc_sb.ap()[:, 0:1],
                    C.sqq_sb.ap()[:, h * SC:h * SC + SC],
                    start=True, stop=True), C.TS)
                sqq_ts[h] = C.TS.n
            wt(C, P, "vector", C.TS, C.TS.n)
            for h in range(4):
                vop(C, P, lambda e, h=h: e.reduce_max(
                    st[0:1, 4:5],
                    PS[4 + h // 2].ap()[0:1, (h % 2) * SC:(h % 2) * SC + SC], axis=AX.X))
                if si == 0:
                    vkn = vop(C, P, lambda e, h=h: e.tensor_copy(
                        C.kn_sb.ap()[0:1, h:h + 1], st[0:1, 4:5]))
                else:
                    vkn = vop(C, P, lambda e, h=h: e.tensor_tensor(
                        C.kn_sb.ap()[0:1, h:h + 1], C.kn_sb.ap()[0:1, h:h + 1],
                        st[0:1, 4:5], ALU.max))
                knts[h] = vkn
        kn_vread = vkn
        if limit in ("kvonly", "kv1g"):
            xs_free = max(kv_ts.values())
            continue
        wt(C, P, "sync", C.TS, max(kv_ts.values()))
        ck = 0
        for c in range(NDC):
            ck = dma(C, P, "sync", C.LD, C.wq_sb.ap()[:, 512 * c:512 * c + 512],
                     C.MY_WQKV[128 * c:128 * c + 128, 512 * g:512 * g + 512])
        c_wq = ck
        wt(C, P, "scalar", C.VS, vkn)
        aop(C, P, lambda e: e.activation(C.kn_sb.ap()[0:1, 8:12], C.kn_sb.ap()[0:1, 0:4], AF.Ln))
        ckn = aop(C, P, lambda e: e.activation(C.kn_sb.ap()[0:1, 4:8], C.kn_sb.ap()[0:1, 8:12], AF.Exp, scale=0.5))
        if limit in ("kv", "kn_novec", "kn_nolnexp"):
            continue

        q_consumed = max(kv_ts.values())
        avset = [(C.PS6[4], C.PS6[5]), (C.PAV[0], C.PAV[1])]
        nrm_of_set = {0: 0, 1: 0}
        ghead = [0]
        bank_extra = {i: [] for i in range(4)}

        def s_slot():
            sb = srot[0] % 4
            srot[0] += 1
            return sb

        for j in range(ATT_J):
            sl = slots[j]
            qbase = sl * CH
            nkt = 4 * (sl + 1)
            qpar = j % 2
            for half in range(2):
                wt(C, P, "sync", C.TS, q_consumed)
                for qq in range(4):
                    cl = dma(C, P, "sync", C.LD, xsr[:, half, 4 * qq:4 * qq + 4, :],
                             shx[:, 4 * qq:4 * qq + 4, qbase + SC * half:qbase + SC * half + SC])
            c_xq = cl
            wt(C, P, "tensor", C.LD, c_xq)
            if j == 0:
                wt(C, P, "tensor", C.LD, c_wq)
                wt(C, P, "tensor", C.VS, kn_vread)
                wt(C, P, "tensor", C.NRM, nrm_last)
            for h in range(4):
                wt(C, P, "tensor", C.EXPS, exp_of_bank[h])
                for sem, cnt in bank_extra[h]:
                    wt(C, P, "tensor", sem, cnt)
                bank_extra[h] = []
                for half in range(2):
                    for c in range(NDC):
                        last = (half == 1 and c == NDC - 1)
                        op(C, P, "tensor", lambda e, h=h, half=half, c=c: e.matmul(
                            C.PS6[h].ap()[:, half * SC:half * SC + SC],
                            C.wq_sb.ap()[:, 512 * c + 128 * h:512 * c + 128 * h + 128],
                            C.xs_sb.ap()[:, (half * NDC + c) * SC:(half * NDC + c) * SC + SC],
                            start=(c == 0), stop=(c == NDC - 1)),
                            C.TS if last else None)
                cqmm_h = C.TS.n
                wt(C, P, "scalar", C.TS, cqmm_h)
                aop(C, P, lambda e, h=h, qpar=qpar, g=g: e.activation(
                    C.Q_sb.ap()[:, (4 * qpar + h) * CH:(4 * qpar + h) * CH + CH],
                    C.PS6[h].ap()[:, 0:CH], AF.Identity,
                    bias=C.bqkvc_sb.ap()[:, 4 * g + h:4 * g + h + 1], scale=ATTN_MULT),
                    chain=False)
            cqmm = C.TS.n
            q_consumed = cqmm
            cqev = C.AS.n
            if g == 0 and j == 0:
                wt(C, P, "tensor", C.VS, vset)

            def emit_bounds_a(h):
                hp = h % 2
                wt(C, P, "vector", C.AS, cqev)
                wt(C, P, "vector", C.TS, max(sqq_ts[2 * hp], sqq_ts[2 * hp + 1]))
                qsl = C.Q_sb.ap()[:, (4 * qpar + h) * CH:(4 * qpar + h) * CH + CH]
                csq = vop(C, P, lambda e, qsl=qsl, hp=hp: e.tensor_mul(
                    C.sqq_sb.ap()[:, hp * CH:hp * CH + CH], qsl, qsl))
                sb = s_slot()
                wt(C, P, "tensor", C.EXPS, exp_of_bank[sb])
                for sem, cnt in bank_extra[sb]:
                    wt(C, P, "tensor", sem, cnt)
                bank_extra[sb] = []
                wt(C, P, "tensor", C.VS, csq)
                op(C, P, "tensor", lambda e, sb=sb, hp=hp: e.matmul(
                    C.PS6[sb].ap()[0:1, 0:CH], C.onc_sb.ap()[:, 0:1],
                    C.sqq_sb.ap()[:, hp * CH:hp * CH + CH],
                    start=True, stop=True), C.TS)
                cts = C.TS.n
                sqq_ts[2 * hp] = cts
                sqq_ts[2 * hp + 1] = cts
                wt(C, P, "vector", C.TS, cts)
                vop(C, P, lambda e, sb=sb: e.reduce_max(
                    st[0:1, 0:1], C.PS6[sb].ap()[0:1, 0:CH], axis=AX.X))
                wt(C, P, "scalar", C.VS)
                wt(C, P, "scalar", C.AS, ckn)
                aop(C, P, lambda e: e.activation(st[0:1, 1:2], st[0:1, 0:1], AF.Ln))
                caq = aop(C, P, lambda e: e.activation(st[0:1, 2:3], st[0:1, 1:2], AF.Exp, scale=0.5))
                wt(C, P, "vector", C.AS, caq)
                vop(C, P, lambda e, h=h: e.tensor_tensor(
                    st[0:1, 3:4], st[0:1, 2:3], C.kn_sb.ap()[0:1, 4 + h:5 + h], ALU.mult))
                vop(C, P, lambda e: e.tensor_scalar(
                    st[0:1, 0:1], st[0:1, 3:4], -1.0, -1.0, ALU.mult, ALU.add))
                vnb = vop(C, P, lambda e, h=h: e.tensor_copy(
                    C.tinyb_sb.ap()[0:1, h:h + 1], st[0:1, 0:1]))
                bank_extra[sb].append((C.VS, C.VS.n))
                return vnb

            def emit_bounds_b(h, vnb):
                sb2 = s_slot()
                wt(C, P, "tensor", C.EXPS, exp_of_bank[sb2])
                for sem, cnt in bank_extra[sb2]:
                    wt(C, P, "tensor", sem, cnt)
                bank_extra[sb2] = []
                wt(C, P, "tensor", C.VS, vnb)
                op(C, P, "tensor", lambda e, sb2=sb2, h=h: e.matmul(
                    C.PS6[sb2].ap()[:, 256 + h:256 + h + 1], C.onr_sb.ap()[0:1, :],
                    C.tinyb_sb.ap()[0:1, h:h + 1], start=True, stop=True), C.TS)
                bts = C.TS.n
                wt(C, P, "scalar", C.TS, bts)
                cnb = aop(C, P, lambda e, qpar=qpar, h=h, sb2=sb2: e.copy(
                    C.negb_sb.ap()[:, 4 * qpar + h:4 * qpar + h + 1],
                    C.PS6[sb2].ap()[:, 256 + h:256 + h + 1]))
                bank_extra[sb2].append((C.AS, cnb))
                return cnb

            cnb_of = {}
            cnb_of[0] = emit_bounds_b(0, emit_bounds_a(0))
            if limit == "tap45":
                wt(C, P, "gpsimd", C.AS, cnb_of[0])
                wt(C, P, "gpsimd", C.KVE, C.KVE.n)
                dma(C, P, "gpsimd", C.GQ, C.dbg[0:128, 0:2048], C.Q_sb.ap()[:, 0:2048])
                dma(C, P, "gpsimd", C.GQ, C.dbg[128:256, 0:16],
                    C.negb_sb.ap().bitcast(BF16)[:, 0:16])
                dma(C, P, "gpsimd", C.GQ, C.dbg[256:384, 0:2048], C.K_sb.ap()[:, 0:2048])
                dma(C, P, "gpsimd", C.GQ, C.dbg[384:512, 0:2048], C.V_sb.ap()[:, 0:2048])
                wt(C, P, "gpsimd", C.GQ, C.GQ.n)
                return

            for h in range(4):
                gh = ghead[0]
                ghead[0] += 1
                sset = gh % 2
                bk = avset[sset]
                wt(C, P, "scalar", C.AS, cnb_of[h])
                state = {}

                def emit_S(kt, h=h):
                    sb = s_slot()
                    wt(C, P, "tensor", C.EXPS, exp_of_bank[sb])
                    for sem, cnt in bank_extra[sb]:
                        wt(C, P, "tensor", sem, cnt)
                    bank_extra[sb] = []
                    if kt == 0:
                        wt(C, P, "tensor", C.NRM, nrm_of_set[sset])
                    op(C, P, "tensor", lambda e, sb=sb, h=h, kt=kt, qpar=qpar: e.matmul(
                        C.PS6[sb].ap()[:, 0:CH],
                        Kap[:, h * T + kt * 128:h * T + kt * 128 + 128],
                        C.Q_sb.ap()[:, (4 * qpar + h) * CH:(4 * qpar + h) * CH + CH],
                        start=True, stop=True), C.TS)
                    state[kt] = (sb, C.TS.n)

                def emit_tail(kt, h=h, sset=sset, bk=bk):
                    sb, cs = state.pop(kt)
                    diag = (kt >= nkt - 4)
                    rel = kt - (nkt - 4)
                    ppar = kt % 4
                    wt(C, P, "scalar", C.TS, cs)
                    wt(C, P, "scalar", C.AVS, pt_used[ppar])
                    ce = op(C, P, "scalar", lambda e, sb=sb, ppar=ppar, qpar=qpar, h=h: e.activation(
                        C.Pt_sb.ap()[:, ppar * CH:ppar * CH + CH], C.PS6[sb].ap()[:, 0:CH],
                        AF.Exp, bias=C.negb_sb.ap()[:, 4 * qpar + h:4 * qpar + h + 1]), C.EXPS)
                    exp_of_bank[sb] = ce
                    src_p = C.Pt_sb.ap()[:, ppar * CH:ppar * CH + CH]
                    if diag:
                        wt(C, P, "vector", C.EXPS, ce)
                        wt(C, P, "vector", C.AVS, pm_used[rel % 2])
                        cm = vop(C, P, lambda e, rel=rel, src_p=src_p: e.tensor_mul(
                            C.Pm_sb.ap()[:, (rel % 2) * CH:(rel % 2) * CH + CH],
                            src_p, C.masks_sb.ap()[:, rel * CH:rel * CH + CH]))
                        src_p = C.Pm_sb.ap()[:, (rel % 2) * CH:(rel % 2) * CH + CH]
                        wt(C, P, "tensor", C.VS, cm)
                    else:
                        wt(C, P, "tensor", C.EXPS, ce)
                    for b4 in range(4):
                        op(C, P, "tensor", lambda e, b4=b4, kt=kt, h=h, src_p=src_p, bk=bk: e.matmul(
                            bk[b4 // 2].ap()[:, 129 * (b4 % 2):129 * (b4 % 2) + 129],
                            src_p[:, 128 * b4:128 * b4 + 128],
                            Vap[:, 516 * kt + 129 * h:516 * kt + 129 * h + 129],
                            start=(kt == 0 and b4 % 2 == 0), stop=(kt == nkt - 1)),
                            C.AVS if b4 == 3 else None)
                    pt_used[ppar] = C.AVS.n
                    if diag:
                        pm_used[rel % 2] = C.AVS.n

                LOOK = 2
                vnb_next = [None]
                for kt in range(nkt + LOOK):
                    if kt < nkt:
                        emit_S(kt)
                    if kt == 0 and h < 3:
                        vnb_next[0] = emit_bounds_a(h + 1)
                    if kt - LOOK >= 0:
                        emit_tail(kt - LOOK)
                    if kt == 2 and h < 3:
                        cnb_of[h + 1] = emit_bounds_b(h + 1, vnb_next[0])
                cav = C.AVS.n
                wt(C, P, "vector", C.AVS, cav)
                wt(C, P, "vector", C.STQ, ost_stored)
                for b4 in range(4):
                    pbank = bk[b4 // 2].ap()[:, 129 * (b4 % 2):129 * (b4 % 2) + 129]
                    vop(C, P, lambda e, pbank=pbank: e.tensor_scalar(
                        st[:, 4:5], pbank[:, 128:129], 1e-30, None, ALU.add))
                    vop(C, P, lambda e: e.reciprocal(st[:, 5:6], st[:, 4:5]))
                    lastnrm = (b4 == 3)
                    wt(C, P, "vector", C.VS)
                    op(C, P, "vector", lambda e, b4=b4, h=h, pbank=pbank: e.tensor_scalar_mul(
                        C.Ost_sb.ap()[:, 512 * b4 + 128 * h:512 * b4 + 128 * h + 128],
                        pbank[:, 0:128], st[:, 5:6]),
                        C.NRM if lastnrm else C.VS)
                nrm_of_set[sset] = C.NRM.n
                nrm_last = C.NRM.n
                wt(C, P, "vector", C.NRM, nrm_last)
            wt(C, P, "scalar", C.NRM, C.NRM.n)
            cst = C.STQ.inc(16)
            i0 = [("op", lambda e, j=j, g=g, s=C.STQ.h: e.dma_start(
                out=C.SH_O[512 * j:512 * j + 512, 512 * g:512 * g + 512].rearrange(
                    "(b p) f -> p b f", p=128),
                in_=C.Ost_sb.ap().rearrange("p (b f) -> p b f", f=512)).then_inc(s, 16))]
            i1 = [("op", lambda e, j=j, g=g, s=C.STQ.h: e.dma_start(
                out=C.SH_O[512 * j:512 * j + 512, 1024 + 512 * g:1024 + 512 * g + 512].rearrange(
                    "(b p) f -> p b f", p=128),
                in_=C.Ost_sb.ap().rearrange("p (b f) -> p b f", f=512)).then_inc(s, 16))]
            P.br("scalar", "memb", i0, i1)
            ost_stored = cst

        xs_free = q_consumed

    wt(C, P, "tensor", C.NRM, C.NRM.n)
    op(C, P, "tensor", lambda e: e.transpose(
        C.PBF[0].ap()[:, 0:128], C.idb_sb.ap(), C.idb_sb.ap()), C.ARM, 1)
    wt(C, P, "scalar", C.AS)
    wt(C, P, "scalar", C.STQ, C.STQ.n)
    op(C, P, "scalar", lambda e: e.copy(C.tinyb_sb.ap()[0:1, 4:5], C.tinyb_sb.ap()[0:1, 5:6]), C.ARM, 1)
    wt(C, P, "vector", C.VS)
    op(C, P, "vector", lambda e: e.tensor_copy(st[0:1, 6:7], st[0:1, 7:8]), C.ARM, 1)
    barrier(C, P, 1, [(C.STQ, C.STQ.n)])


def plan_phase2(C, P):
    st = C.tiny_sb.ap()
    wproj = C.wproj_sb.ap()
    wbase = C.wbase_sb.ap()
    wt(C, P, "sync", C.ARM, 3)
    wt(C, P, "sync", C.BAR, C.BAR.n)
    wt(C, P, "sync", C.STQ, C.c_smalls_stq)
    dma(C, P, "sync", C.LD2, C.aall_sb.ap(), C.D_AALL)
    dma(C, P, "sync", C.LD2, C.ball_sb.ap(), C.D_BALL)
    dma(C, P, "sync", C.LD2, C.wrout_sb.ap(), C.D_WROUT)
    dma(C, P, "sync", C.LD2, C.bias2_sb.ap()[0:1, :], C.D_BIAS[0:1, :])
    dma(C, P, "sync", C.LD2, C.bias2_sb.ap()[64:65, :], C.D_BIAS[1:2, :])
    dma(C, P, "sync", C.LD2, C.bias2_sb.ap()[32:33, 0:8], C.D_BIAS[2:3, 0:8])
    dma(C, P, "sync", C.LD2, C.ones3_sb.ap()[32:33, :], C.ones_row)
    dma(C, P, "sync", C.LD2, C.ones3_sb.ap()[64:65, :], C.ones_row)
    dma(C, P, "sync", C.LD2, C.sel_sb.ap(), C.sel_c)
    c_smalls = dma(C, P, "sync", C.LD2, C.w7_sb.ap()[6:7, :], C.ones_row)
    cw = 0
    for c in range(NDC):
        cw = dma(C, P, "sync", C.LD2, wproj[:, 2048 * c:2048 * c + 2048],
                 C.SH_WP[128 * c:128 * c + 128, :])
    c_wp = cw
    for c in range(NDC):
        cw = dma(C, P, "sync", C.LD2, wbase[:, 2048 * c:2048 * c + 2048],
                 C.SH_WB[128 * c:128 * c + 128, :])
    c_wb = cw

    xres = C.xres_sb.ap()
    osb = C.o_sb.ap()[:, 0:D]
    ot = C.ot_sb.ap()
    big = C.big_sb.ap()
    x2 = C.x2_sb.ap()
    xn2 = C.xn2_sb.ap()
    xn2t = C.xn2t_sb.ap()
    outb = C.out_sb.ap()
    g8 = C.g8_sb.ap()

    bnst = C.bnst_sb.ap()

    def ln_stats(src, vw):
        for s, v in vw:
            wt(C, P, "vector", s, v)
        for q in range(4):
            v2op(C, P, lambda e, q=q: e.bn_stats(bnst[:, 6 * q:6 * q + 6],
                                                 src[:, 512 * q:512 * q + 512]))
        cv = v2op(C, P, lambda e: e.bn_aggr(st[:, 0:2], bnst))
        wt(C, P, "scalar", C.VS2, cv)
        a2op(C, P, lambda e: e.activation(st[:, 2:3], st[:, 1:2], AF.Ln, bias=C.eps_sb.ap()[:, 0:1]))
        crs = a2op(C, P, lambda e: e.activation(st[:, 3:4], st[:, 2:3], AF.Exp, scale=-0.5))
        wt(C, P, "vector", C.AS2, crs)
        v2op(C, P, lambda e: e.tensor_tensor(st[:, 2:3], st[:, 0:1], st[:, 3:4], ALU.mult))

    S = {
        "c_xload": {}, "c_oload": {}, "c_otr": {}, "tr_ev": {0: 0, 1: 0},
        "otr_ev": {0: 0, 1: 0}, "xn2_ev": {0: 0, 1: 0},
        "projev": {0: 0, 1: 0}, "ln4_norm": 0, "c_ffn": {},
        "cLN2": {}, "cLN4": {}, "c_add": {}, "cxn2": {}, "c_xn2tr": {},
        "c_rout": {}, "c_hmm": {}, "c_hwf_ev": 0, "cw2": 0, "cge": 0,
        "chw": {}, "c_ost": {}, "c_xntr_done": 0,
    }

    def seg_oload(i):
        if i > 0:
            wt(C, P, "sync", C.T2, S["c_otr"][i - 1])
        S["c_oload"][i] = C.LD2.inc(16)
        i0 = [("op", lambda e, i=i, s=C.LD2.h: e.dma_start(
            out=osb, in_=C.SH_O[128 * i:128 * i + 128, :]).then_inc(s, 16))]
        i1 = [("op", lambda e, i=i, s=C.LD2.h: e.dma_start(
            out=osb, in_=C.SH_O[2048 + 128 * i:2048 + 128 * i + 128, :]).then_inc(s, 16))]
        P.br("sync", "memb", i0, i1)

    def seg_xload(i):
        if i > 0:
            wt(C, P, "sync", C.VS2, S["c_add"].get(i - 1, 0))
        S["c_xload"][i] = dma(C, P, "sync", C.LD2, xres, C.x_res[128 * i:128 * i + 128, :])

    def seg_otr(i):
        wt(C, P, "tensor", C.LD2, S["c_oload"][i])
        if i == 0:
            wt(C, P, "tensor", C.LD2, c_wp)
            wt(C, P, "tensor", C.LD2, c_smalls)
        for half in range(2):
            pb = C.PBF[half].ap()
            wt(C, P, "tensor", C.AS2, S["tr_ev"][half])
            for cc in range(8):
                c = half * 8 + cc
                op(C, P, "tensor", lambda e, c=c, cc=cc, pb=pb: e.transpose(
                    pb[:, 128 * cc:128 * cc + 128], osb[:, 128 * c:128 * c + 128],
                    C.idb_sb.ap()), C.T2 if cc == 7 else None)
            ct = C.T2.n
            wt(C, P, "scalar", C.T2, ct)
            ev = a2op(C, P, lambda e, pb=pb, half=half: e.copy(
                ot[:, 1024 * half:1024 * half + 1024], pb), chain=False)
            S["tr_ev"][half] = ev
            S["otr_ev"][half] = ev
        S["c_otr"][i] = C.T2.n

    def seg_xn2tr_tr(i):
        wt(C, P, "tensor", C.VS2, S["cxn2"][i])
        for half in range(2):
            pb = C.PBF[half].ap()
            wt(C, P, "tensor", C.AS2, S["tr_ev"][half])
            for cc in range(8):
                c = half * 8 + cc
                op(C, P, "tensor", lambda e, c=c, cc=cc, pb=pb: e.transpose(
                    pb[:, 128 * cc:128 * cc + 128], xn2[:, 128 * c:128 * c + 128],
                    C.idb_sb.ap()), C.T2 if (half, cc) == (1, 7) else None)
        S["c_xntr_done"] = C.T2.n
        S["c_xn2tr"][i] = C.T2.n

    def seg_xn2tr_ev(gate):
        for half in range(2):
            pb = C.PBF[half].ap()
            wt(C, P, "scalar", C.T2, gate)
            ev = a2op(C, P, lambda e, pb=pb, half=half: e.copy(
                xn2t[:, 1024 * half:1024 * half + 1024], pb), chain=False)
            S["tr_ev"][half] = ev
            S["xn2_ev"][half] = ev

    def seg_proj(i):
        for r in range(2):
            wt(C, P, "tensor", C.AS2, S["projev"][1] if r == 0 else S["projev"][0])
            for c in range(NDC):
                if r == 0 and c in (0, 8):
                    wt(C, P, "tensor", C.AS2, S["otr_ev"][c // 8])
                for fi in range(2):
                    fc = 2 * r + fi
                    op(C, P, "tensor", lambda e, c=c, fc=fc, fi=fi: e.matmul(
                        C.PS6[fi].ap()[:, 0:512],
                        ot[:, 128 * c:128 * c + 128],
                        wproj[:, 2048 * c + 512 * fc:2048 * c + 512 * fc + 512],
                        start=(c == 0), stop=False), None)
            for fi in range(2):
                fc = 2 * r + fi
                op(C, P, "tensor", lambda e, fc=fc, fi=fi: e.matmul(
                    C.PS6[fi].ap()[:, 0:512], C.onr_sb.ap()[0:1, :],
                    C.bias2_sb.ap()[0:1, 512 * fc:512 * fc + 512],
                    start=False, stop=True), C.T2 if fi == 1 else None)
            cpr = C.T2.n
            if r == 0 and i >= 1:
                seg_xn2tr_tr(i - 1)
            wt(C, P, "scalar", C.T2, cpr)
            if r == 0:
                wt(C, P, "scalar", C.VS2, S["cLN2"].get(i - 1, 0))
            for fi in range(2):
                fc = 2 * r + fi
                ev = a2op(C, P, lambda e, fc=fc, fi=fi: e.copy(
                    big[:, 512 * fc:512 * fc + 512], C.PS6[fi].ap()[:, 0:512]), chain=False)
            S["projev"][r] = ev
        if i >= 1:
            seg_xn2tr_ev(S["c_xn2tr"][i - 1])
        if i + 1 <= 15:
            seg_otr(i + 1)
        if i + 2 <= 15:
            seg_oload(i + 2)

    def seg_ln23(i):
        sl = i % 2
        pe = S["projev"][1]
        ln_stats(big, [(C.AS2, pe)])
        wt(C, P, "vector", C.LD2, S["c_xload"][i])
        v2op(C, P, lambda e, sl=sl: e.tensor_scalar(
            x2[:, sl * D:sl * D + D], big, st[:, 3:4], st[:, 2:3], ALU.mult, ALU.subtract))
        S["cLN2"][i] = C.VS2.n
        S["c_add"][i] = v2op(C, P, lambda e, sl=sl: e.tensor_add(
            x2[:, sl * D:sl * D + D], x2[:, sl * D:sl * D + D], xres))
        ln_stats(x2[:, sl * D:sl * D + D], [])
        wt(C, P, "vector", C.T2, S["c_xn2tr"].get(i - 1, 0))
        S["cxn2"][i] = v2op(C, P, lambda e, sl=sl: e.tensor_scalar(
            xn2, x2[:, sl * D:sl * D + D], st[:, 3:4], st[:, 2:3], ALU.mult, ALU.subtract))

    def seg_rh(i):
        if i == 0:
            wt(C, P, "tensor", C.LD2, c_wb)
        if i > 0:
            wt(C, P, "tensor", C.AS2, S["c_hwf_ev"])
            wt(C, P, "tensor", C.VS2, S["ln4_norm"])
        for c in range(NDC):
            if c in (0, 8):
                wt(C, P, "tensor", C.AS2, S["xn2_ev"][c // 8])
            op(C, P, "tensor", lambda e, c=c: e.matmul(
                C.PAV[1].ap()[:, 0:8], xn2t[:, 128 * c:128 * c + 128],
                C.wrout_sb.ap()[:, 8 * c:8 * c + 8],
                start=(c == 0), stop=False), None)
        op(C, P, "tensor", lambda e: e.matmul(
            C.PAV[1].ap()[:, 0:8], C.ones3_sb.ap()[32:33, :], C.bias2_sb.ap()[32:33, 0:8],
            start=False, stop=True), C.T2)
        S["c_rout"][i] = C.T2.n
        for c in range(NDC):
            op(C, P, "tensor", lambda e, c=c: e.matmul(
                C.PAV[0].ap()[0:PACK, 0:128],
                C.aall_sb.ap()[:, PACK * c:PACK * c + PACK], xn2t[:, 128 * c:128 * c + 128],
                start=(c == 0), stop=(c == NDC - 1)), C.T2 if c == NDC - 1 else None)
        S["c_hmm"][i] = C.T2.n

    def seg_softmax_v(i):
        wt(C, P, "vector", C.T2, S["c_rout"][i])
        v2op(C, P, lambda e: e.reduce_max(st[:, 4:5], C.PAV[1].ap()[:, 0:8], axis=AX.X))
        cgm = v2op(C, P, lambda e: e.tensor_scalar_mul(st[:, 7:8], st[:, 4:5], -1.0))
        wt(C, P, "scalar", C.VS2, cgm)
        S["cge"] = a2op(C, P, lambda e: e.activation(g8, C.PAV[1].ap()[:, 0:8], AF.Exp, bias=st[:, 7:8]))
        wt(C, P, "vector", C.AS2, S["cge"])
        v2op(C, P, lambda e: e.reduce_sum(st[:, 5:6], g8, axis=AX.X))
        v2op(C, P, lambda e: e.reciprocal(st[:, 6:7], st[:, 5:6]))
        v2op(C, P, lambda e: e.tensor_scalar_mul(C.e2_sb.ap(), g8, st[:, 6:7]))
        v2op(C, P, lambda e: e.reduce_max(st[:, 4:5], C.e2_sb.ap(), axis=AX.X))
        v2op(C, P, lambda e: e.tensor_scalar(C.e1_sb.ap(), C.e2_sb.ap(), st[:, 4:5], None, ALU.is_equal))
        v2op(C, P, lambda e: e.scalar_tensor_tensor(g8, C.e1_sb.ap(), NEG_BIG, C.e2_sb.ap(), ALU.mult, ALU.add))
        v2op(C, P, lambda e: e.reduce_max(st[:, 5:6], g8, axis=AX.X))
        v2op(C, P, lambda e: e.tensor_scalar(C.w2_sb.ap(), g8, st[:, 5:6], None, ALU.is_equal))
        v2op(C, P, lambda e: e.tensor_add(g8, C.e1_sb.ap(), C.w2_sb.ap()))
        S["cw2"] = v2op(C, P, lambda e: e.tensor_mul(C.w2_sb.ap(), C.e2_sb.ap(), g8))

    def seg_ffn(i):
        FB = [C.PS6[2], C.PS6[3], C.PAV[0], C.PAV[1]]
        wt(C, P, "tensor", C.VS2, S["ln4_norm"])
        for c in range(NDC):
            for fi in range(2):
                op(C, P, "tensor", lambda e, c=c, fi=fi, bk=FB[fi]: e.matmul(
                    bk.ap()[:, 0:512],
                    xn2t[:, 128 * c:128 * c + 128],
                    wbase[:, 2048 * c + 512 * fi:2048 * c + 512 * fi + 512],
                    start=(c == 0), stop=False), None)
        wt(C, P, "tensor", C.VS2, S["cw2"])
        wt(C, P, "tensor", C.AS2, S["cge"])
        op(C, P, "tensor", lambda e: e.transpose(
            C.PAV[1].ap()[0:8, 128:256], C.w2_sb.ap(), C.idf_sb.ap()), C.T2)
        cwt = C.T2.n
        wt(C, P, "scalar", C.T2, cwt)
        cw7 = a2op(C, P, lambda e: e.copy(C.w7_sb.ap()[0:6, :], C.PAV[1].ap()[0:6, 128:256]))
        wt(C, P, "tensor", C.AS2, cw7)
        op(C, P, "tensor", lambda e: e.matmul(
            C.PAV[0].ap()[0:PACK, 128:256], C.sel_sb.ap()[0:7, :], C.w7_sb.ap()[0:7, :],
            start=True, stop=True), C.T2)
        c_wfull = C.T2.n
        wt(C, P, "scalar", C.T2, c_wfull)
        a2op(C, P, lambda e: e.copy(C.h_sb.ap(), C.PAV[0].ap()[0:PACK, 0:128]), chain=False)
        S["c_hwf_ev"] = a2op(C, P, lambda e: e.copy(
            C.wf_sb.ap(), C.PAV[0].ap()[0:PACK, 128:256]), chain=False)
        wt(C, P, "vector", C.AS2, S["c_hwf_ev"])
        S["chw"][i] = v2op(C, P, lambda e: e.tensor_mul(C.hw_sb.ap(), C.h_sb.ap(), C.wf_sb.ap()))
        if i + 1 <= 15:
            seg_ln23(i + 1)
        wt(C, P, "tensor", C.AS2, S["c_hwf_ev"])
        for c in range(NDC):
            for fi in range(2, 4):
                op(C, P, "tensor", lambda e, c=c, fi=fi, bk=FB[fi]: e.matmul(
                    bk.ap()[:, 0:512],
                    xn2t[:, 128 * c:128 * c + 128],
                    wbase[:, 2048 * c + 512 * fi:2048 * c + 512 * fi + 512],
                    start=(c == 0), stop=False), None)
        wt(C, P, "tensor", C.VS2, S["chw"][i])
        for fi in range(4):
            op(C, P, "tensor", lambda e, fi=fi, bk=FB[fi]: e.matmul(
                bk.ap()[:, 0:512], C.hw_sb.ap(),
                C.ball_sb.ap()[:, 512 * fi:512 * fi + 512],
                start=False, stop=False), None)
        for fi in range(4):
            op(C, P, "tensor", lambda e, fi=fi, bk=FB[fi]: e.matmul(
                bk.ap()[:, 0:512], C.ones3_sb.ap()[64:65, :],
                C.bias2_sb.ap()[64:65, 512 * fi:512 * fi + 512],
                start=False, stop=True), C.T2 if fi == 3 else None)
        S["c_ffn"][i] = C.T2.n

    def seg_ln4_out(i):
        FB = [C.PS6[2], C.PS6[3], C.PAV[0], C.PAV[1]]
        sl = i % 2
        cf = S["c_ffn"][i]
        wt(C, P, "vector", C.T2, cf)
        for fi in range(4):
            v2op(C, P, lambda e, fi=fi, bk=FB[fi]: e.bn_stats(
                bnst[:, 6 * fi:6 * fi + 6], bk.ap()[:, 0:512]))
        cv = v2op(C, P, lambda e: e.bn_aggr(st[:, 0:2], bnst))
        wt(C, P, "scalar", C.VS2, cv)
        a2op(C, P, lambda e: e.activation(st[:, 2:3], st[:, 1:2], AF.Ln, bias=C.eps_sb.ap()[:, 0:1]))
        crs = a2op(C, P, lambda e: e.activation(st[:, 3:4], st[:, 2:3], AF.Exp, scale=-0.5))
        wt(C, P, "vector", C.AS2, crs)
        v2op(C, P, lambda e: e.tensor_tensor(st[:, 2:3], st[:, 0:1], st[:, 3:4], ALU.mult))
        wt(C, P, "vector", C.ST2, S["c_ost"].get(i - 1, 0))
        cn = 0
        for fi in (2, 3, 0, 1):
            cn = v2op(C, P, lambda e, fi=fi, bk=FB[fi]: e.tensor_scalar(
                outb[:, 512 * fi:512 * fi + 512], bk.ap()[:, 0:512],
                st[:, 3:4], st[:, 2:3], ALU.mult, ALU.subtract))
        S["ln4_norm"] = cn
        S["cLN4"][i] = cn
        cout = v2op(C, P, lambda e, sl=sl: e.tensor_add(outb, outb, x2[:, sl * D:sl * D + D]))
        wt(C, P, "gpsimd", C.VS2, cout)
        S["c_ost"][i] = dma(C, P, "gpsimd", C.ST2, C.out[128 * i:128 * i + 128, :], outb)

    seg_xload(0)
    seg_oload(0)
    seg_otr(0)
    seg_oload(1)
    for k in range(-1, 16):
        if k + 1 <= 15:
            seg_proj(k + 1)
        if k == -1:
            seg_ln23(0)
        if k >= 0:
            if k == 15:
                seg_xn2tr_tr(k)
                seg_xn2tr_ev(S["c_xntr_done"])
            seg_rh(k)
            seg_softmax_v(k)
            seg_ffn(k)
            seg_ln4_out(k)
        if k + 2 <= 15:
            seg_xload(k + 2)

    wt(C, P, "gpsimd", C.ST2, C.ST2.n)


def finish_debug(C):
    _close_block(C)


def _close_block(C):
    with C.nc.Block() as block:
        def run(ename):
            def f(e):
                if ename == "sync":
                    e.dma_start(out=C.rmt_sb.ap(), in_=C.rmt).then_inc(C.LD.h, 16)
                e.wait_ge(C.LD.h, C.c_rmt)
                r_role = e.alloc_register()
                e.reg_load(r_role, C.rmt_sb.ap()[0:1, 0:1])
                r_memb = e.alloc_register()
                e.reg_load(r_memb, C.rmt_sb.ap()[0:1, 1:2])
                regs = {"role": r_role, "memb": r_memb}
                if ename == "gpsimd":
                    C.GREGS["memb"] = r_memb
                    r_tok = e.alloc_register()
                    e.reg_load(r_tok, C.rmt_sb.ap()[0:1, 2:3])
                    r_pf = e.alloc_register()
                    r_plcnt = e.alloc_register()
                    r_cond = e.alloc_register()
                    e.reg_mov(r_plcnt, 0)
                    C.GREGS.update(tok=r_tok, pf=r_pf, plcnt=r_plcnt, cond=r_cond)
                _play(e, C.P.q[ename], regs)
            return f
        block.sync(run("sync"))
        block.tensor(run("tensor"))
        block.scalar(run("scalar"))
        block.vector(run("vector"))
        block.gpsimd(run("gpsimd"))
    for g in C._sem_guards:
        g.__exit__(None, None, None)


def build():
    C = make_context()
    alloc_phase1(C)
    C.GREGS = {}
    plan_prolog(C)

    semnames = C._semnames
    dl = DEBUG_LEVEL
    if dl == 1:
        finish_debug(C)
        return C
    limit = {2: "phase0", 25: "bar0", 26: "kvonly", 261: "kv1g", 27: "wq",
             28: "kn_novec", 29: "kn_nolnexp", 3: "kv",
             42: "bar0", 43: "kv", 44: None, 45: "tap45"}.get(dl, None)
    snap = {k: getattr(C, k).n for k in semnames}
    Pa = Plan()
    plan_phase0(C, Pa, 0)
    if limit != "phase0":
        plan_phase1(C, Pa, 0, limit=limit)
    cnt_a = {k: getattr(C, k).n for k in semnames}
    for k in semnames:
        getattr(C, k).n = snap[k]
    Pb = Plan()
    plan_phase0(C, Pb, 1)
    if limit != "phase0":
        plan_phase1(C, Pb, 1, limit=limit)
    cnt_b = {k: getattr(C, k).n for k in semnames}
    if limit is None:
        assert cnt_a["ARM"] == cnt_b["ARM"] == snap["ARM"] + 3, (cnt_a["ARM"], cnt_b["ARM"])
        assert cnt_a["BAR"] == cnt_b["BAR"], (cnt_a["BAR"], cnt_b["BAR"])
    if dl == 45:
        C.P.splice_role(Pa, Pb)
        finish_debug(C)
        return C
    for k in semnames:
        getattr(C, k).n = max(cnt_a[k], cnt_b[k])
    C.P.splice_role(Pa, Pb)
    P = C.P
    if dl == 42:
        wt(C, P, "gpsimd", C.BAR, min(cnt_a["BAR"], cnt_b["BAR"]))
        i0 = [("op", lambda e, s=C.GQ.h: e.dma_start(
            out=C.dbg.rearrange("(c p) t -> p c t", p=128),
            in_=C.SH_XNT.rearrange("(c p) t -> p c t", p=128)[:, :, 0:2048]).then_inc(s, 16))]
        i1 = [("op", lambda e, s=C.GQ.h: e.dma_start(
            out=C.dbg.rearrange("(c p) t -> p c t", p=128),
            in_=C.SH_XNT.rearrange("(c p) t -> p c t", p=128)[:, :, 4096:4096 + 2048]).then_inc(s, 16))]
        C.GQ.inc(16)
        P.br("gpsimd", "memb", i0, i1)
        wt(C, P, "gpsimd", C.GQ, C.GQ.n)
        finish_debug(C)
        return C
    if dl == 43:
        wt(C, P, "gpsimd", C.BAR, min(cnt_a["BAR"], cnt_b["BAR"]))
        dma(C, P, "gpsimd", C.GQ, C.dbg[0:128, 0:2048], C.K_sb.ap()[:, 0:2048])
        dma(C, P, "gpsimd", C.GQ, C.dbg[128:256, 0:2048], C.V_sb.ap()[:, 0:2048])
        dma(C, P, "gpsimd", C.GQ, C.dbg[384:385, 0:16], C.kn_sb.ap().bitcast(BF16)[0:1, 0:32][0:1, 0:16])
        wt(C, P, "gpsimd", C.GQ, C.GQ.n)
        finish_debug(C)
        return C
    if dl == 44:
        wt(C, P, "gpsimd", C.BAR, min(cnt_a["BAR"], cnt_b["BAR"]))
        i0 = [("op", lambda e, s=C.GQ.h: e.dma_start(
            out=C.dbg, in_=C.SH_O[0:2048, :]).then_inc(s, 16))]
        i1 = [("op", lambda e, s=C.GQ.h: e.dma_start(
            out=C.dbg, in_=C.SH_O[2048:4096, :]).then_inc(s, 16))]
        C.GQ.inc(16)
        P.br("gpsimd", "memb", i0, i1)
        wt(C, P, "gpsimd", C.GQ, C.GQ.n)
        finish_debug(C)
        return C
    if dl in (2, 25, 26, 261, 27, 28, 29, 3, 4):
        finish_debug(C)
        return C

    for name in ("sqq_sb", "negb_sb", "Ost_sb", "Pm_sb", "Pt_sb", "Q_sb",
                 "wkv_sb", "xs_sb", "V_sb", "K_sb"):
        C.sbfree(name)
    C.sbuf("wproj_sb", (128, NDC * 2048), BF16)
    C.sbuf("wbase_sb", (128, NDC * 2048), BF16)
    C.sbuf("xres_sb", (128, D), F32)
    C.sbuf("o_sb", (128, D), BF16)
    C.sbuf("ot_sb", (128, D), BF16)
    C.sbuf("big_sb", (128, D), F32)
    C.sbuf("x2_sb", (128, 2 * D), F32)
    C.sbuf("xn2_sb", (128, D), BF16)
    C.sbuf("out_sb", (128, D), F32)
    C.sbuf("g8_sb", (128, 8), F32)
    C.sbuf("e1_sb", (128, 8), F32)
    C.sbuf("e2_sb", (128, 8), F32)
    C.sbuf("w2_sb", (128, 8), F32)
    C.sbuf("h_sb", (PACK, 128), BF16)
    C.sbuf("wf_sb", (PACK, 128), BF16)
    C.sbuf("hw_sb", (PACK, 128), BF16)
    C.sbuf("aall_sb", (128, NDC * PACK), BF16)
    C.sbuf("ball_sb", (PACK, D), BF16)
    C.sbuf("wrout_sb", (128, NDC * 8), BF16)
    C.sbuf("sel_sb", (8, PACK), BF16)
    C.sbuf("w7_sb", (8, 128), BF16)
    C.sbuf("bias2_sb", (128, D), BF16)
    C.sbuf("ones3_sb", (128, 128), BF16)
    C.sbuf("bnst_sb", (128, 24), F32)
    C.sbuf("xn2t_sb", (128, D), BF16)

    plan_phase2(C, C.P)
    _close_block(C)
    return C


_NC_CACHE = [None]
DEBUG_LEVEL = 0


def kernel(**L):
    xin = np.asarray(L["L0_attn_x"])
    w_qkv = np.asarray(L["L0_attn_w_qkv"])
    b_qkv = np.asarray(L["L0_attn_b_qkv"])
    w_proj = np.asarray(L["L0_attn_w_proj"])
    b_proj = np.asarray(L["L0_attn_b_proj"])
    W_base = np.asarray(L["L0_ffn_W_base"])
    b_base = np.asarray(L["L0_ffn_b_base"])
    A_s = np.asarray(L["L0_ffn_A_s"])
    B_s = np.asarray(L["L0_ffn_B_s"])
    A_r = np.asarray(L["L0_ffn_A_r"])
    B_r = np.asarray(L["L0_ffn_B_r"])
    W_router = np.asarray(L["L0_ffn_W_router"])
    b_router = np.asarray(L["L0_ffn_b_router"])

    if _NC_CACHE[0] is None:
        _NC_CACHE[0] = build()
    C = _NC_CACHE[0]

    token = int(np.random.randint(1, 2**31 - 1))
    in_maps = []
    rows_per_core = []
    for c in range(8):
        p, m = c // 2, c % 2
        b, role = p // 2, p % 2
        n = LN_N[role]
        xl = np.zeros((4096, D), dtype=np.float32)
        xl[:n] = xin[b, m * n:m * n + n]
        if role == 0:
            rows = np.r_[0:2048] if m == 0 else np.r_[2048:E1, E2:T]
        else:
            rows = np.r_[E1:E1 + 2048] if m == 0 else np.r_[E1 + 2048:E2]
        rows_per_core.append((b, rows))
        hcols = np.r_[1024 * m:1024 * m + 1024]
        wq = np.concatenate([w_qkv[:, hcols], w_qkv[:, 2048 + hcols], w_qkv[:, 4096 + hcols]], axis=1)
        bq = np.concatenate([b_qkv[hcols], b_qkv[2048 + hcols], b_qkv[4096 + hcols]])[None, :]
        in_maps.append({
            "x_ln": np.ascontiguousarray(xl),
            "x_res": np.ascontiguousarray(xin[b, rows]),
            "wqkv": np.ascontiguousarray(wq),
            "bqkv": np.ascontiguousarray(bq),
            "wproj_half": np.ascontiguousarray(w_proj[1024 * m:1024 * m + 1024]),
            "wbase_half": np.ascontiguousarray(W_base[1024 * m:1024 * m + 1024]),
            "bproj": np.ascontiguousarray(b_proj[None, :]),
            "bbase": np.ascontiguousarray(b_base[None, :]),
            "a_s": np.ascontiguousarray(A_s),
            "b_s": np.ascontiguousarray(B_s),
            "a_r": np.ascontiguousarray(A_r),
            "b_r": np.ascontiguousarray(B_r),
            "wrouter": np.ascontiguousarray(W_router),
            "brouter": np.ascontiguousarray(b_router[None, :]),
            "rmt": np.array([[role, m, token, 0]], dtype=np.int32),
        })

    res = run_bass_kernel_spmd(C.nc, in_maps, core_ids=list(range(8)))
    outv = np.zeros((B, T, D), dtype=np.float32)
    for c in range(8):
        b, rows = rows_per_core[c]
        outv[b, rows] = res.results[c]["out"]
    if DEBUG_LEVEL:
        kernel.dbg = [np.asarray(res.results[c]["dbg"]) for c in range(8)]
    return outv


# version 22
# speedup vs baseline: 2.5417x; 1.0019x over previous
import numpy as np
import ml_dtypes
from types import SimpleNamespace
import concourse.bass as bass
import concourse.mybir as mybir
from concourse.bass_utils import run_bass_kernel_spmd

F32 = mybir.dt.float32
BF16 = mybir.dt.bfloat16
I32 = mybir.dt.int32
AF = mybir.ActivationFunctionType
ALU = mybir.AluOpType
AX = mybir.AxisListType

B, T, D = 2, 8192, 2048
NDC = D // 128
CH = 512
SC = 256
E1, E2 = 2560, 6656
OUTER_SLOTS = [0, 1, 2, 3, 4, 13, 14, 15]
MIDDLE_SLOTS = [5, 6, 7, 8, 9, 10, 11, 12]
KV_RANGE = {0: T, 1: E2}
LN_N = {0: 4096, 1: 3328}
ATTN_MULT = 1.0 / float(np.sqrt(128.0))
NEG_BIG = -1e30
EPS = 1e-5
PACK = 112
ENGS = ("sync", "tensor", "scalar", "vector", "gpsimd")
ATT_J = 8


class Sem:
    def __init__(self, h):
        self.h = h
        self.n = 0

    def inc(self, k):
        self.n += k
        return self.n


class Plan:
    def __init__(self):
        self.q = {e: [] for e in ENGS}

    def em(self, e, fn):
        self.q[e].append(("op", fn))

    def br(self, e, key, it0, it1):
        self.q[e].append(("br", key, it0, it1))

    def splice_role(self, pa, pb):
        for e in ENGS:
            self.q[e].append(("br", "role", pa.q[e], pb.q[e]))


def _play(e, items, regs):
    for it in items:
        if it[0] == "op":
            it[1](e)
        else:
            _, key, it0, it1 = it
            with e.If_eq(regs[key], 0):
                _play(e, it0, regs)
            with e.If_ne(regs[key], 0):
                _play(e, it1, regs)


def dma(C, P, eng, sem, o, i):
    c = sem.inc(16)
    P.em(eng, lambda e, o=o, i=i, s=sem.h: e.dma_start(out=o, in_=i).then_inc(s, 16))
    return c


def dma_br(C, P, eng, sem, o0, o1, i):
    c = sem.inc(16)
    it0 = [("op", lambda e, o=o0, i=i, s=sem.h: e.dma_start(out=o, in_=i).then_inc(s, 16))]
    it1 = [("op", lambda e, o=o1, i=i, s=sem.h: e.dma_start(out=o, in_=i).then_inc(s, 16))]
    P.br(eng, "memb", it0, it1)
    return c


def wt(C, P, eng, sem, val=None):
    v = sem.n if val is None else val
    if v > 0:
        P.em(eng, lambda e, s=sem.h, v=v: e.wait_ge(s, v))


def op(C, P, eng, fn, sem=None, k=1):
    if sem is None:
        P.em(eng, lambda e, f=fn: f(e))
        return None
    c = sem.inc(k)
    P.em(eng, lambda e, f=fn, s=sem.h, k=k: f(e).then_inc(s, k))
    return c


def vop(C, P, fn, sem=None):
    wt(C, P, "vector", C.VS)
    return op(C, P, "vector", fn, sem if sem is not None else C.VS)


def aop(C, P, fn, chain=True):
    if chain:
        wt(C, P, "scalar", C.AS)
    return op(C, P, "scalar", fn, C.AS)


def v2op(C, P, fn):
    wt(C, P, "vector", C.VS2)
    return op(C, P, "vector", fn, C.VS2)


def a2op(C, P, fn, chain=True):
    if chain:
        wt(C, P, "scalar", C.AS2)
    return op(C, P, "scalar", fn, C.AS2)


def make_context():
    nc = bass.Bass("TRN2", target_bir_lowering=False)
    C = SimpleNamespace()
    C.nc = nc
    C.P = Plan()

    def din(name, shape, dtype=F32):
        a = nc.dram_tensor(name, list(shape), dtype, kind="ExternalInput").ap()
        setattr(C, name, a)

    din("x_ln", (4096, D))
    din("x_res", (2048, D))
    din("wqkv", (D, 3072))
    din("bqkv", (1, 3072))
    din("wproj_half", (1024, D))
    din("wbase_half", (1024, D))
    din("bproj", (1, D))
    din("bbase", (1, D))
    din("a_s", (D, 16))
    din("b_s", (16, D))
    din("a_r", (6, D, 16))
    din("b_r", (6, 16, D))
    din("wrouter", (D, 6))
    din("brouter", (1, 6))
    din("rmt", (1, 4), I32)

    C.out = nc.dram_tensor("out", [2048, D], F32, kind="ExternalOutput").ap()
    if DEBUG_LEVEL:
        C.dbg = nc.dram_tensor("dbg", [2048, D], BF16, kind="ExternalOutput").ap()

    C.MY_WQKV = nc.dram_tensor("MY_WQKV", [D, 3072], BF16, kind="Internal").ap()
    C.D_AALL = nc.dram_tensor("D_AALL", [128, NDC * PACK], BF16, kind="Internal").ap()
    C.D_BALL = nc.dram_tensor("D_BALL", [PACK, D], BF16, kind="Internal").ap()
    C.D_WROUT = nc.dram_tensor("D_WROUT", [128, NDC * 8], BF16, kind="Internal").ap()
    C.D_BIAS = nc.dram_tensor("D_BIAS", [4, D], BF16, kind="Internal").ap()
    C.SH_XNT = nc.dram_tensor("SH_XNT", [D, T], BF16, kind="Internal", addr_space="Shared").ap()
    C.SH_O = nc.dram_tensor("SH_O", [4096, D], BF16, kind="Internal", addr_space="Shared").ap()
    C.SH_WP = nc.dram_tensor("SH_WP", [D, D], BF16, kind="Internal", addr_space="Shared").ap()
    C.SH_WB = nc.dram_tensor("SH_WB", [D, D], BF16, kind="Internal", addr_space="Shared").ap()
    C.SH_FLAGS = nc.dram_tensor("SH_FLAGS", [2, 4, 16], I32, kind="Internal", addr_space="Shared").ap()

    C.ident_bf = nc.inline_tensor(np.eye(128, dtype=ml_dtypes.bfloat16), "c_identbf").ap()
    C.ident_f32 = nc.inline_tensor(np.eye(128, dtype=np.float32), "c_identf").ap()
    C.ones_col = nc.inline_tensor(np.ones((128, 1), dtype=ml_dtypes.bfloat16), "c_onescol").ap()
    C.ones_row = nc.inline_tensor(np.ones((1, 128), dtype=ml_dtypes.bfloat16), "c_onesrow").ap()
    mk = np.zeros((4, 128, CH), dtype=ml_dtypes.bfloat16)
    for r in range(4):
        for k in range(128):
            lo = 128 * r + k
            if lo < CH:
                mk[r, k, lo:] = 1.0
    C.masks_c = nc.inline_tensor(np.ascontiguousarray(mk), "c_masks").ap()
    sel = np.zeros((8, PACK), dtype=ml_dtypes.bfloat16)
    for ei in range(7):
        sel[ei, 16 * ei:16 * ei + 16] = 1.0
    C.sel_c = nc.inline_tensor(sel, "c_sel").ap()

    C._sem_guards = []
    C._semnames = []

    def S(name):
        g = nc.semaphore(name)
        h = g.__enter__()
        C._sem_guards.append(g)
        s = Sem(h)
        setattr(C, name, s)
        C._semnames.append(name)

    for name in ("LD", "LDW", "STQ", "GQ", "PL", "VS", "AS", "TS", "EXPS",
                 "AVS", "NRM", "KVE", "BAR", "ARM",
                 "LD2", "ST2", "VS2", "AS2", "T2"):
        S(name)

    C._sb_guards = {}

    def sbuf(name, shape, dtype, offset=None):
        if offset is None:
            g = nc.sbuf_tensor(name, list(shape), dtype)
            h = g.__enter__()
            C._sb_guards[name] = g
        else:
            h = nc.alloc_sbuf_tensor_at(name, list(shape), dtype, offset=offset)
        setattr(C, name, h)
        return h

    def sbfree(name):
        g = C._sb_guards.pop(name, None)
        if g is not None:
            g.__exit__(None, None, None)

    C.sbuf = sbuf
    C.sbfree = sbfree
    sbuf("rmt_sb", (1, 4), I32)
    sbuf("eps_sb", (128, 1), F32)
    sbuf("idb_sb", (128, 128), BF16)
    sbuf("idf_sb", (128, 128), F32)
    sbuf("onc_sb", (128, 1), BF16)
    sbuf("onr_sb", (1, 128), BF16)
    sbuf("masks_sb", (128, 4 * CH), BF16)
    sbuf("bqkvc_sb", (128, 16), F32)
    sbuf("bvrow_sb", (1, 1024), BF16)
    sbuf("kn_sb", (1, 16), F32)
    sbuf("tiny_sb", (128, 16), F32)
    sbuf("bnst0_sb", (128, 24), F32)
    sbuf("tinyb_sb", (1, 8), BF16)
    sbuf("gz_sb", (1, 1), I32)
    sbuf("pf_sb", (1, 16), I32)

    C.PS6 = [nc.alloc_psum_tensor(f"ps6_{i}", [128, 512], F32) for i in range(6)]
    C.PAV = [nc.alloc_psum_tensor(f"pav{i}", [128, 512], F32) for i in range(2)]
    C.PBF = [C.PS6[4].bitcast(BF16), C.PS6[5].bitcast(BF16)]
    return C


def alloc_phase1(C):
    C.sbuf("K_sb", (128, 4 * T), BF16)
    C.sbuf("V_sb", (128, 64 * 516), BF16)
    C.sbuf("xs_sb", (128, 2 * NDC * SC), BF16)
    C.sbuf("wkv_sb", (128, NDC * 1024), BF16)
    C.sbuf("Q_sb", (128, 2 * 4 * CH), BF16)
    C.sbuf("Pt_sb", (128, 4 * CH), BF16)
    C.sbuf("Pm_sb", (128, 2 * CH), BF16)
    C.sbuf("Ost_sb", (128, 2048), BF16)
    C.sbuf("negb_sb", (128, 8), F32)
    C.sbuf("wq_sb", (128, NDC * 512), BF16, offset=C.nc.lookup_mloc(C.wkv_sb).addr)
    C.sbuf("sqq_sb", (128, 2 * CH), BF16)
    kof = C.nc.lookup_mloc(C.K_sb).addr
    vof = C.nc.lookup_mloc(C.V_sb).addr
    C.sbuf("x0_sb", (128, 2 * D), F32, offset=kof)
    C.sbuf("xn_sb", (128, D), BF16, offset=kof + 16384)
    C.sbuf("sq0_sb", (128, D), BF16, offset=kof + 16384 + 4096)
    C.sbuf("xnt_sb", (128, 2 * NDC * CH), BF16, offset=kof + 24576)
    C.sbuf("stg_sb", (128, 2048), F32, offset=vof)
    C.sbuf("stgb_sb", (128, 2048), BF16, offset=vof + 8192)


def plan_prolog(C):
    P = C.P
    C.LD.inc(16)
    C.c_rmt = C.LD.n

    dma(C, P, "sync", C.LD, C.idb_sb.ap(), C.ident_bf)
    dma(C, P, "sync", C.LD, C.idf_sb.ap(), C.ident_f32)
    dma(C, P, "sync", C.LD, C.onc_sb.ap(), C.ones_col)
    dma(C, P, "sync", C.LD, C.onr_sb.ap(), C.ones_row)
    for r in range(4):
        dma(C, P, "sync", C.LD, C.masks_sb.ap()[:, r * CH:(r + 1) * CH], C.masks_c[r])

    c_msets = vop(C, P, lambda e: e.memset(C.eps_sb.ap(), EPS))

    stg = C.stg_sb.ap()
    stgb = C.stgb_sb.ap()
    stgA = stg[:, 0:NDC * PACK].rearrange("p (c k) -> p c k", k=PACK)
    for ei in range(6):
        dma(C, P, "sync", C.LD, stgA[:, :, 16 * ei:16 * ei + 16],
            C.a_r[ei].rearrange("(c p) r -> p c r", p=128))
    c1 = dma(C, P, "sync", C.LD, stgA[:, :, 96:112], C.a_s.rearrange("(c p) r -> p c r", p=128))
    wt(C, P, "scalar", C.LD, c1)
    cA = aop(C, P, lambda e: e.copy(stgb[:, 0:NDC * PACK], stg[:, 0:NDC * PACK]))
    wt(C, P, "scalar", C.AS, cA)
    dma(C, P, "scalar", C.STQ, C.D_AALL, stgb[:, 0:NDC * PACK])
    wt(C, P, "sync", C.AS, cA)
    for ei in range(6):
        dma(C, P, "sync", C.LD, stg[16 * ei:16 * ei + 16, 0:D], C.b_r[ei])
    c1 = dma(C, P, "sync", C.LD, stg[96:112, 0:D], C.b_s)
    wt(C, P, "scalar", C.LD, c1)
    cB = aop(C, P, lambda e: e.copy(stgb[0:PACK, 0:D], stg[0:PACK, 0:D]))
    wt(C, P, "scalar", C.AS, cB)
    dma(C, P, "scalar", C.STQ, C.D_BALL, stgb[0:PACK, 0:D])
    wt(C, P, "sync", C.AS, cB)
    cz = vop(C, P, lambda e: e.memset(stgb[:, 0:NDC * 8], 0.0))
    stgR = stg[:, 0:NDC * 6].rearrange("p (c r) -> p c r", r=6)
    c1 = dma(C, P, "sync", C.LD, stgR, C.wrouter.rearrange("(c p) r -> p c r", p=128))
    wt(C, P, "scalar", C.LD, c1)
    wt(C, P, "scalar", C.VS, cz)
    cR = aop(C, P, lambda e: e.copy(
        stgb[:, 0:NDC * 8].rearrange("p (c r) -> p c r", r=8)[:, :, 0:6], stgR))
    wt(C, P, "scalar", C.AS, cR)
    dma(C, P, "scalar", C.STQ, C.D_WROUT, stgb[:, 0:NDC * 8])
    wt(C, P, "sync", C.AS, cR)
    c1 = dma(C, P, "sync", C.LD, stg[0:1, 0:D], C.bproj)
    wt(C, P, "scalar", C.LD, c1)
    cb = aop(C, P, lambda e: e.copy(stgb[0:1, 0:D], stg[0:1, 0:D]))
    wt(C, P, "scalar", C.AS, cb)
    cs1 = dma(C, P, "scalar", C.STQ, C.D_BIAS[0:1, :], stgb[0:1, 0:D])
    wt(C, P, "sync", C.AS, cb)
    c1 = dma(C, P, "sync", C.LD, stg[0:1, 0:D], C.bbase)
    wt(C, P, "scalar", C.LD, c1)
    wt(C, P, "scalar", C.STQ, cs1)
    cb = aop(C, P, lambda e: e.copy(stgb[0:1, 0:D], stg[0:1, 0:D]))
    wt(C, P, "scalar", C.AS, cb)
    cs1 = dma(C, P, "scalar", C.STQ, C.D_BIAS[1:2, :], stgb[0:1, 0:D])
    wt(C, P, "sync", C.AS, cb)
    c1 = dma(C, P, "sync", C.LD, stg[0:1, 0:6], C.brouter)
    wt(C, P, "vector", C.STQ, cs1)
    cbm = vop(C, P, lambda e: e.memset(stgb[0:1, 0:8], NEG_BIG))
    wt(C, P, "scalar", C.LD, c1)
    wt(C, P, "scalar", C.VS, cbm)
    wt(C, P, "scalar", C.STQ, cs1)
    cb = aop(C, P, lambda e: e.copy(stgb[0:1, 0:6], stg[0:1, 0:6]))
    wt(C, P, "scalar", C.AS, cb)
    cs1 = dma(C, P, "scalar", C.STQ, C.D_BIAS[2:3, 0:8], stgb[0:1, 0:8])
    wt(C, P, "sync", C.AS, cb)
    c1 = dma(C, P, "sync", C.LD, stg[0:1, 0:1024], C.bqkv[0:1, 2048:3072])
    wt(C, P, "scalar", C.LD, c1)
    aop(C, P, lambda e: e.copy(C.bvrow_sb.ap(), stg[0:1, 0:1024]))
    C.c_smalls_stq = C.STQ.n
    wt(C, P, "sync", C.AS, C.AS.n)
    c1 = dma(C, P, "sync", C.LD, stg[0:1, 0:2048], C.bqkv[0:1, 0:2048])
    wt(C, P, "tensor", C.LD, c1)
    for c in range(16):
        op(C, P, "tensor", lambda e, c=c: e.transpose(
            C.PS6[0].ap()[:, c:c + 1], stg[0:1, 128 * c:128 * c + 128],
            C.idf_sb.ap()[0:1, 0:1]), C.TS if c == 15 else None)
    c_biasT = C.TS.n
    wt(C, P, "scalar", C.TS, c_biasT)
    cqb = aop(C, P, lambda e: e.copy(C.bqkvc_sb.ap(), C.PS6[0].ap()[:, 0:16]))
    wt(C, P, "vector", C.AS, cqb)
    vop(C, P, lambda e: e.tensor_scalar_mul(C.bqkvc_sb.ap()[:, 0:8], C.bqkvc_sb.ap()[:, 0:8], ATTN_MULT))
    C.c_bias_ps_free = cqb

    jobs = []
    for c in range(NDC):
        for part in range(3):
            src = C.wqkv[128 * c:128 * c + 128, 1024 * part:1024 * part + 1024]
            dst = C.MY_WQKV[128 * c:128 * c + 128, 1024 * part:1024 * part + 1024]
            jobs.append((src, dst, None))
    for r8 in range(8):
        for halfc in range(2):
            cs = slice(1024 * halfc, 1024 * halfc + 1024)
            jobs.append((C.wproj_half[128 * r8:128 * r8 + 128, cs],
                         C.SH_WP[128 * r8:128 * r8 + 128, cs],
                         C.SH_WP[1024 + 128 * r8:1024 + 128 * r8 + 128, cs]))
            jobs.append((C.wbase_half[128 * r8:128 * r8 + 128, cs],
                         C.SH_WB[128 * r8:128 * r8 + 128, cs],
                         C.SH_WB[1024 + 128 * r8:1024 + 128 * r8 + 128, cs]))

    c_stg_done = C.AS.n
    prev_store = {0: 0, 1: 0}
    for j, (src, d0, d1) in enumerate(jobs):
        sl = j % 2
        sslc = stg[:, 1024 * sl:1024 * sl + 1024]
        bslc = C.stgb_sb.ap()[:, 1024 * sl:1024 * sl + 1024]
        if j < 2:
            wt(C, P, "sync", C.AS, c_stg_done)
            wt(C, P, "sync", C.TS, c_biasT)
            wt(C, P, "sync", C.STQ, C.c_smalls_stq)
        wt(C, P, "sync", C.STQ, prev_store[sl])
        cl = dma(C, P, "sync", C.LDW, sslc, src)
        wt(C, P, "scalar", C.LDW, cl)
        wt(C, P, "scalar", C.STQ, prev_store[sl])
        if j < 2:
            wt(C, P, "scalar", C.STQ, C.c_smalls_stq)
        ccast = aop(C, P, lambda e, o=bslc, i=sslc: e.copy(o, i))
        wt(C, P, "scalar", C.AS, ccast)
        if d1 is None:
            cst = dma(C, P, "scalar", C.STQ, d0, bslc)
        else:
            cst = dma_br(C, P, "scalar", C.STQ, d0, d1, bslc)
        prev_store[sl] = cst
    C.c_wconv = C.STQ.n


def barrier(C, P, slot, gq_waits):
    for s, cval in gq_waits:
        wt(C, P, "gpsimd", s, cval)
    cf = C.GQ.inc(16)
    i0 = [("op", lambda e, s=C.GQ.h, sl=slot: e.dma_start(
        out=C.SH_FLAGS[0:1, sl, 0:1], in_=C.rmt_sb.ap()[0:1, 2:3]).then_inc(s, 16))]
    i1 = [("op", lambda e, s=C.GQ.h, sl=slot: e.dma_start(
        out=C.SH_FLAGS[1:2, sl, 0:1], in_=C.rmt_sb.ap()[0:1, 2:3]).then_inc(s, 16))]
    P.br("gpsimd", "memb", i0, i1)
    wt(C, P, "gpsimd", C.GQ, cf)

    def _poll(e, sl=slot):
        R = C.GREGS

        def cond():
            with e.If_eq(R["memb"], 0):
                e.dma_start(out=C.pf_sb.ap()[0:1, 0:1], in_=C.SH_FLAGS[1:2, sl, 0:1]).then_inc(C.PL.h, 16)
            with e.If_ne(R["memb"], 0):
                e.dma_start(out=C.pf_sb.ap()[0:1, 0:1], in_=C.SH_FLAGS[0:1, sl, 0:1]).then_inc(C.PL.h, 16)
            e.reg_alu(R["plcnt"], R["plcnt"], 16, ALU.add)
            e.wait_ge(C.PL.h, R["plcnt"])
            e.reg_load(R["pf"], C.pf_sb.ap()[0:1, 0:1])
            e.reg_alu(R["cond"], R["pf"], R["tok"], ALU.not_equal)
            return R["cond"]

        with e.While(cond):
            pass

    P.em("gpsimd", _poll)
    return op(C, P, "gpsimd", lambda e: e.memset(C.gz_sb.ap(), 0), C.BAR, 1)


def plan_phase0(C, P, role):
    nln = LN_N[role]
    nblk = nln // 128
    x0 = C.x0_sb.ap()
    xn = C.xn_sb.ap()
    sq0 = C.sq0_sb.ap()
    st = C.tiny_sb.ap()

    vlast = {0: 0, 1: 0}
    alast = {0: 0, 1: 0}
    evlast = {0: 0, 1: 0}
    stlast = {0: 0, 1: 0}
    cload = []
    tlast = 0
    c_ss_prev = 0
    xntg = C.xnt_sb.ap().rearrange("p (g c t) -> p g c t", g=2, t=CH)
    dstr = C.SH_XNT.rearrange("(c p) t -> p c t", p=128)
    for i in range(nblk):
        par = i % 2
        if i == 0:
            cload.append(dma(C, P, "sync", C.LD, x0[:, 0:D], C.x_ln[0:128, :]))
        if i + 1 < nblk:
            npar = (i + 1) % 2
            wt(C, P, "sync", C.VS, vlast[npar])
            cload.append(dma(C, P, "sync", C.LD, x0[:, npar * D:npar * D + D],
                             C.x_ln[128 * (i + 1):128 * (i + 1) + 128, :]))
        xv = x0[:, par * D:par * D + D]
        wt(C, P, "vector", C.LD, cload[i])
        bnst = C.bnst0_sb.ap()
        for q in range(4):
            vop(C, P, lambda e, xv=xv, q=q: e.bn_stats(
                bnst[:, 6 * q:6 * q + 6], xv[:, 512 * q:512 * q + 512]))
        cvar = vop(C, P, lambda e: e.bn_aggr(st[:, 0:2], bnst))
        alast[par] = 0
        wt(C, P, "scalar", C.VS, cvar)
        aop(C, P, lambda e: e.activation(st[:, 2:3], st[:, 1:2], AF.Ln, bias=C.eps_sb.ap()[:, 0:1]))
        crs = aop(C, P, lambda e: e.activation(st[:, 3:4], st[:, 2:3], AF.Exp, scale=-0.5))
        wt(C, P, "vector", C.AS, crs)
        vop(C, P, lambda e: e.tensor_tensor(st[:, 2:3], st[:, 0:1], st[:, 3:4], ALU.mult))
        wt(C, P, "vector", C.TS, tlast)
        cxn = vop(C, P, lambda e, xv=xv: e.tensor_scalar(
            xn, xv, st[:, 3:4], st[:, 2:3], ALU.mult, ALU.subtract))
        vlast[par] = cxn
        gslot = (i // 4) % 2
        toff = (i % 4) * 128
        for half in range(2):
            pb = C.PBF[half].ap()
            wt(C, P, "tensor", C.VS, cxn)
            wt(C, P, "tensor", C.AS, evlast[half])
            for cc in range(8):
                c = half * 8 + cc
                op(C, P, "tensor", lambda e, c=c, cc=cc, pb=pb: e.transpose(
                    pb[:, 128 * cc:128 * cc + 128], xn[:, 128 * c:128 * c + 128], C.idb_sb.ap()),
                   C.TS if cc == 7 else None)
            ct = C.TS.n
            wt(C, P, "scalar", C.TS, ct)
            wt(C, P, "scalar", C.STQ, stlast[gslot])
            ce = aop(C, P, lambda e, pb=pb, half=half, gslot=gslot, toff=toff: e.copy(
                xntg[:, gslot, 8 * half:8 * half + 8, toff:toff + 128],
                pb.rearrange("p (c t) -> p c t", t=128)))
            evlast[half] = ce
        tlast = C.TS.n
        if i % 4 == 3 or i == nblk - 1:
            tg0 = (i // 4) * CH
            tglen = (i % 4 + 1) * 128
            wt(C, P, "scalar", C.AS, C.AS.n)
            cst = C.STQ.inc(16)
            i0 = [("op", lambda e, tg0=tg0, tglen=tglen, gslot=gslot, s=C.STQ.h: e.dma_start(
                out=dstr[:, :, tg0:tg0 + tglen],
                in_=xntg[:, gslot, :, 0:tglen]).then_inc(s, 16))]
            i1 = [("op", lambda e, tg0=tg0, tglen=tglen, gslot=gslot, nln=nln, s=C.STQ.h: e.dma_start(
                out=dstr[:, :, nln + tg0:nln + tg0 + tglen],
                in_=xntg[:, gslot, :, 0:tglen]).then_inc(s, 16))]
            P.br("scalar", "memb", i0, i1)
            stlast[gslot] = cst


def plan_phase1(C, P, role, limit=None):
    slots = OUTER_SLOTS if role == 0 else MIDDLE_SLOTS
    kvr = KV_RANGE[role]
    nsc = kvr // SC
    xsr = C.xs_sb.ap().rearrange("p (s c t) -> p s c t", s=2, t=SC)
    shx = C.SH_XNT.rearrange("(c p) t -> p c t", p=128)
    Kap = C.K_sb.ap()
    Vap = C.V_sb.ap()
    st = C.tiny_sb.ap()
    PS = C.PS6
    PAV = C.PAV

    barrier(C, P, 0, [(C.STQ, C.STQ.n)])
    bar0 = C.BAR.n

    wt(C, P, "vector", C.BAR, bar0)
    vset = vop(C, P, lambda e: e.memset(
        Vap.rearrange("p (k h d) -> p k h d", h=4, d=129)[:, :, :, 128:129], 1.0))

    srot = [0]
    exp_of_bank = {i: 0 for i in range(6)}
    nrm_last = 0
    ost_stored = 0
    pt_used = {i: 0 for i in range(4)}
    pm_used = {0: 0, 1: 0}
    xs_free = 0
    sqq_ts = {0: 0, 1: 0, 2: 0, 3: 0}
    kv_ev = 0
    kn_vread = 0

    for g in range(2):
        wt(C, P, "sync", C.STQ, C.c_wconv)
        wt(C, P, "sync", C.BAR, bar0)
        wt(C, P, "sync", C.TS, xs_free)
        ck = 0
        for c in range(NDC):
            ck = dma(C, P, "sync", C.LD, C.wkv_sb.ap()[:, 1024 * c:1024 * c + 512],
                     C.MY_WQKV[128 * c:128 * c + 128, 1024 + 512 * g:1024 + 512 * g + 512])
            ck = dma(C, P, "sync", C.LD, C.wkv_sb.ap()[:, 1024 * c + 512:1024 * c + 1024],
                     C.MY_WQKV[128 * c:128 * c + 128, 2048 + 512 * g:2048 + 512 * g + 512])
        c_wkv = ck

        kv_ts = {0: xs_free, 1: xs_free}
        knts = {0: 0, 1: 0, 2: 0, 3: 0}
        vkn = 0
        for si in range(nsc):
            par = si % 2
            wt(C, P, "sync", C.TS, kv_ts[par])
            base = si * SC
            cl = 0
            for qq in range(4):
                cl = dma(C, P, "sync", C.LD, xsr[:, par, 4 * qq:4 * qq + 4, :],
                         shx[:, 4 * qq:4 * qq + 4, base:base + SC])
            wt(C, P, "tensor", C.LD, cl)
            if g == 0 and si == 0:
                wt(C, P, "tensor", C.AS, C.c_bias_ps_free)
            if si == 0:
                wt(C, P, "tensor", C.LD, c_wkv)
                wt(C, P, "tensor", C.VS, kn_vread)
                wt(C, P, "tensor", C.NRM, nrm_last)
                wt(C, P, "tensor", C.EXPS, max(exp_of_bank.values()))
            wt(C, P, "tensor", C.KVE, kv_ev)
            for h in range(4):
                for c in range(NDC):
                    op(C, P, "tensor", lambda e, h=h, c=c, par=par: e.matmul(
                        PS[h].ap()[:, 0:SC],
                        C.wkv_sb.ap()[:, 1024 * c + 128 * h:1024 * c + 128 * h + 128],
                        C.xs_sb.ap()[:, (par * NDC + c) * SC:(par * NDC + c) * SC + SC],
                        start=(c == 0), stop=(c == NDC - 1)), None)
            for tb in range(2):
                for c in range(NDC):
                    op(C, P, "tensor", lambda e, tb=tb, c=c, par=par: e.matmul(
                        PAV[tb].ap()[:, 0:512],
                        C.xs_sb.ap()[:, (par * NDC + c) * SC + 128 * tb:(par * NDC + c) * SC + 128 * tb + 128],
                        C.wkv_sb.ap()[:, 1024 * c + 512:1024 * c + 1024],
                        start=(c == 0), stop=False), None)
                op(C, P, "tensor", lambda e, tb=tb, g=g: e.matmul(
                    PAV[tb].ap()[:, 0:512], C.onr_sb.ap()[0:1, :],
                    C.bvrow_sb.ap()[0:1, 512 * g:512 * g + 512],
                    start=False, stop=True), C.TS if tb == 1 else None)
            kv_ts[par] = C.TS.n
            wt(C, P, "scalar", C.TS, kv_ts[par])
            for h in range(4):
                aop(C, P, lambda e, h=h, base=base, g=g: e.activation(
                    Kap[:, h * T + base:h * T + base + SC], PS[h].ap()[:, 0:SC],
                    AF.Identity, bias=C.bqkvc_sb.ap()[:, 8 + 4 * g + h:8 + 4 * g + h + 1]), chain=False)
            c_kev = C.AS.n
            for tb in range(2):
                kb = base // 128 + tb
                for h in range(4):
                    op(C, P, "scalar", lambda e, kb=kb, h=h, tb=tb: e.copy(
                        Vap[:, 516 * kb + 129 * h:516 * kb + 129 * h + 128],
                        PAV[tb].ap()[:, 128 * h:128 * h + 128]),
                        C.KVE if (tb == 1 and h == 3) else None)
            kv_ev = C.KVE.n
            wt(C, P, "vector", C.AS, c_kev)
            csq = 0
            for h in range(4):
                wt(C, P, "vector", C.TS, sqq_ts[h])
                csq = vop(C, P, lambda e, h=h, base=base: e.tensor_mul(
                    C.sqq_sb.ap()[:, h * SC:h * SC + SC],
                    Kap[:, h * T + base:h * T + base + SC],
                    Kap[:, h * T + base:h * T + base + SC]))
            wt(C, P, "tensor", C.VS, csq)
            for h in range(4):
                wt(C, P, "tensor", C.VS, knts[h])
                op(C, P, "tensor", lambda e, h=h: e.matmul(
                    PS[4 + h // 2].ap()[0:1, (h % 2) * SC:(h % 2) * SC + SC],
                    C.onc_sb.ap()[:, 0:1],
                    C.sqq_sb.ap()[:, h * SC:h * SC + SC],
                    start=True, stop=True), C.TS)
                sqq_ts[h] = C.TS.n
            wt(C, P, "vector", C.TS, C.TS.n)
            for h in range(4):
                vop(C, P, lambda e, h=h: e.reduce_max(
                    st[0:1, 4:5],
                    PS[4 + h // 2].ap()[0:1, (h % 2) * SC:(h % 2) * SC + SC], axis=AX.X))
                if si == 0:
                    vkn = vop(C, P, lambda e, h=h: e.tensor_copy(
                        C.kn_sb.ap()[0:1, h:h + 1], st[0:1, 4:5]))
                else:
                    vkn = vop(C, P, lambda e, h=h: e.tensor_tensor(
                        C.kn_sb.ap()[0:1, h:h + 1], C.kn_sb.ap()[0:1, h:h + 1],
                        st[0:1, 4:5], ALU.max))
                knts[h] = vkn
        kn_vread = vkn
        if limit in ("kvonly", "kv1g"):
            xs_free = max(kv_ts.values())
            continue
        wt(C, P, "sync", C.TS, max(kv_ts.values()))
        ck = 0
        for c in range(NDC):
            ck = dma(C, P, "sync", C.LD, C.wq_sb.ap()[:, 512 * c:512 * c + 512],
                     C.MY_WQKV[128 * c:128 * c + 128, 512 * g:512 * g + 512])
        c_wq = ck
        wt(C, P, "scalar", C.VS, vkn)
        aop(C, P, lambda e: e.activation(C.kn_sb.ap()[0:1, 8:12], C.kn_sb.ap()[0:1, 0:4], AF.Ln))
        ckn = aop(C, P, lambda e: e.activation(C.kn_sb.ap()[0:1, 4:8], C.kn_sb.ap()[0:1, 8:12], AF.Exp, scale=0.5))
        if limit in ("kv", "kn_novec", "kn_nolnexp"):
            continue

        q_consumed = max(kv_ts.values())
        avset = [(C.PS6[4], C.PS6[5]), (C.PAV[0], C.PAV[1])]
        nrm_of_set = {0: 0, 1: 0}
        ghead = [0]
        bank_extra = {i: [] for i in range(4)}

        def s_slot():
            sb = srot[0] % 4
            srot[0] += 1
            return sb

        for j in range(ATT_J):
            sl = slots[j]
            qbase = sl * CH
            nkt = 4 * (sl + 1)
            qpar = j % 2
            for half in range(2):
                wt(C, P, "sync", C.TS, q_consumed)
                for qq in range(4):
                    cl = dma(C, P, "sync", C.LD, xsr[:, half, 4 * qq:4 * qq + 4, :],
                             shx[:, 4 * qq:4 * qq + 4, qbase + SC * half:qbase + SC * half + SC])
            c_xq = cl
            wt(C, P, "tensor", C.LD, c_xq)
            if j == 0:
                wt(C, P, "tensor", C.LD, c_wq)
                wt(C, P, "tensor", C.VS, kn_vread)
                wt(C, P, "tensor", C.NRM, nrm_last)
            cqev_h = {}
            for h in range(4):
                wt(C, P, "tensor", C.EXPS, exp_of_bank[h])
                for sem, cnt in bank_extra[h]:
                    wt(C, P, "tensor", sem, cnt)
                bank_extra[h] = []
                for half in range(2):
                    for c in range(NDC):
                        last = (half == 1 and c == NDC - 1)
                        op(C, P, "tensor", lambda e, h=h, half=half, c=c: e.matmul(
                            C.PS6[h].ap()[:, half * SC:half * SC + SC],
                            C.wq_sb.ap()[:, 512 * c + 128 * h:512 * c + 128 * h + 128],
                            C.xs_sb.ap()[:, (half * NDC + c) * SC:(half * NDC + c) * SC + SC],
                            start=(c == 0), stop=(c == NDC - 1)),
                            C.TS if last else None)
                cqmm_h = C.TS.n
                wt(C, P, "scalar", C.TS, cqmm_h)
                aop(C, P, lambda e, h=h, qpar=qpar, g=g: e.activation(
                    C.Q_sb.ap()[:, (4 * qpar + h) * CH:(4 * qpar + h) * CH + CH],
                    C.PS6[h].ap()[:, 0:CH], AF.Identity,
                    bias=C.bqkvc_sb.ap()[:, 4 * g + h:4 * g + h + 1], scale=ATTN_MULT),
                    chain=False)
                cqev_h[h] = C.AS.n
            cqmm = C.TS.n
            q_consumed = cqmm
            cqev = C.AS.n
            if g == 0 and j == 0:
                wt(C, P, "tensor", C.VS, vset)

            def emit_bounds_a(h):
                hp = h % 2
                wt(C, P, "vector", C.AS, cqev_h[h])
                wt(C, P, "vector", C.TS, max(sqq_ts[2 * hp], sqq_ts[2 * hp + 1]))
                qsl = C.Q_sb.ap()[:, (4 * qpar + h) * CH:(4 * qpar + h) * CH + CH]
                csq = vop(C, P, lambda e, qsl=qsl, hp=hp: e.tensor_mul(
                    C.sqq_sb.ap()[:, hp * CH:hp * CH + CH], qsl, qsl))
                sb = s_slot()
                wt(C, P, "tensor", C.EXPS, exp_of_bank[sb])
                for sem, cnt in bank_extra[sb]:
                    wt(C, P, "tensor", sem, cnt)
                bank_extra[sb] = []
                wt(C, P, "tensor", C.VS, csq)
                op(C, P, "tensor", lambda e, sb=sb, hp=hp: e.matmul(
                    C.PS6[sb].ap()[0:1, 0:CH], C.onc_sb.ap()[:, 0:1],
                    C.sqq_sb.ap()[:, hp * CH:hp * CH + CH],
                    start=True, stop=True), C.TS)
                cts = C.TS.n
                sqq_ts[2 * hp] = cts
                sqq_ts[2 * hp + 1] = cts
                wt(C, P, "vector", C.TS, cts)
                vop(C, P, lambda e, sb=sb: e.reduce_max(
                    st[0:1, 0:1], C.PS6[sb].ap()[0:1, 0:CH], axis=AX.X))
                wt(C, P, "scalar", C.VS)
                wt(C, P, "scalar", C.AS, ckn)
                aop(C, P, lambda e: e.activation(st[0:1, 1:2], st[0:1, 0:1], AF.Ln))
                caq = aop(C, P, lambda e: e.activation(st[0:1, 2:3], st[0:1, 1:2], AF.Exp, scale=0.5))
                wt(C, P, "vector", C.AS, caq)
                vop(C, P, lambda e, h=h: e.tensor_tensor(
                    st[0:1, 3:4], st[0:1, 2:3], C.kn_sb.ap()[0:1, 4 + h:5 + h], ALU.mult))
                vop(C, P, lambda e: e.tensor_scalar(
                    st[0:1, 0:1], st[0:1, 3:4], -1.0, -1.0, ALU.mult, ALU.add))
                vnb = vop(C, P, lambda e, h=h: e.tensor_copy(
                    C.tinyb_sb.ap()[0:1, h:h + 1], st[0:1, 0:1]))
                bank_extra[sb].append((C.VS, C.VS.n))
                return vnb

            def emit_bounds_b(h, vnb):
                sb2 = s_slot()
                wt(C, P, "tensor", C.EXPS, exp_of_bank[sb2])
                for sem, cnt in bank_extra[sb2]:
                    wt(C, P, "tensor", sem, cnt)
                bank_extra[sb2] = []
                wt(C, P, "tensor", C.VS, vnb)
                op(C, P, "tensor", lambda e, sb2=sb2, h=h: e.matmul(
                    C.PS6[sb2].ap()[:, 256 + h:256 + h + 1], C.onr_sb.ap()[0:1, :],
                    C.tinyb_sb.ap()[0:1, h:h + 1], start=True, stop=True), C.TS)
                bts = C.TS.n
                wt(C, P, "scalar", C.TS, bts)
                cnb = aop(C, P, lambda e, qpar=qpar, h=h, sb2=sb2: e.copy(
                    C.negb_sb.ap()[:, 4 * qpar + h:4 * qpar + h + 1],
                    C.PS6[sb2].ap()[:, 256 + h:256 + h + 1]))
                bank_extra[sb2].append((C.AS, cnb))
                return cnb

            cnb_of = {}
            cnb_of[0] = emit_bounds_b(0, emit_bounds_a(0))
            if limit == "tap45":
                wt(C, P, "gpsimd", C.AS, cnb_of[0])
                wt(C, P, "gpsimd", C.KVE, C.KVE.n)
                dma(C, P, "gpsimd", C.GQ, C.dbg[0:128, 0:2048], C.Q_sb.ap()[:, 0:2048])
                dma(C, P, "gpsimd", C.GQ, C.dbg[128:256, 0:16],
                    C.negb_sb.ap().bitcast(BF16)[:, 0:16])
                dma(C, P, "gpsimd", C.GQ, C.dbg[256:384, 0:2048], C.K_sb.ap()[:, 0:2048])
                dma(C, P, "gpsimd", C.GQ, C.dbg[384:512, 0:2048], C.V_sb.ap()[:, 0:2048])
                wt(C, P, "gpsimd", C.GQ, C.GQ.n)
                return

            for h in range(4):
                gh = ghead[0]
                ghead[0] += 1
                sset = gh % 2
                bk = avset[sset]
                wt(C, P, "scalar", C.AS, cnb_of[h])
                state = {}

                def emit_S(kt, h=h):
                    sb = s_slot()
                    wt(C, P, "tensor", C.EXPS, exp_of_bank[sb])
                    for sem, cnt in bank_extra[sb]:
                        wt(C, P, "tensor", sem, cnt)
                    bank_extra[sb] = []
                    if kt == 0:
                        wt(C, P, "tensor", C.NRM, nrm_of_set[sset])
                    op(C, P, "tensor", lambda e, sb=sb, h=h, kt=kt, qpar=qpar: e.matmul(
                        C.PS6[sb].ap()[:, 0:CH],
                        Kap[:, h * T + kt * 128:h * T + kt * 128 + 128],
                        C.Q_sb.ap()[:, (4 * qpar + h) * CH:(4 * qpar + h) * CH + CH],
                        start=True, stop=True), C.TS)
                    state[kt] = (sb, C.TS.n)

                def emit_tail(kt, h=h, sset=sset, bk=bk):
                    sb, cs = state.pop(kt)
                    diag = (kt >= nkt - 4)
                    rel = kt - (nkt - 4)
                    ppar = kt % 4
                    wt(C, P, "scalar", C.TS, cs)
                    wt(C, P, "scalar", C.AVS, pt_used[ppar])
                    ce = op(C, P, "scalar", lambda e, sb=sb, ppar=ppar, qpar=qpar, h=h: e.activation(
                        C.Pt_sb.ap()[:, ppar * CH:ppar * CH + CH], C.PS6[sb].ap()[:, 0:CH],
                        AF.Exp, bias=C.negb_sb.ap()[:, 4 * qpar + h:4 * qpar + h + 1]), C.EXPS)
                    exp_of_bank[sb] = ce
                    src_p = C.Pt_sb.ap()[:, ppar * CH:ppar * CH + CH]
                    if diag:
                        wt(C, P, "vector", C.EXPS, ce)
                        wt(C, P, "vector", C.AVS, pm_used[rel % 2])
                        cm = vop(C, P, lambda e, rel=rel, src_p=src_p: e.tensor_mul(
                            C.Pm_sb.ap()[:, (rel % 2) * CH:(rel % 2) * CH + CH],
                            src_p, C.masks_sb.ap()[:, rel * CH:rel * CH + CH]))
                        src_p = C.Pm_sb.ap()[:, (rel % 2) * CH:(rel % 2) * CH + CH]
                        wt(C, P, "tensor", C.VS, cm)
                    else:
                        wt(C, P, "tensor", C.EXPS, ce)
                    for b4 in range(4):
                        op(C, P, "tensor", lambda e, b4=b4, kt=kt, h=h, src_p=src_p, bk=bk: e.matmul(
                            bk[b4 // 2].ap()[:, 129 * (b4 % 2):129 * (b4 % 2) + 129],
                            src_p[:, 128 * b4:128 * b4 + 128],
                            Vap[:, 516 * kt + 129 * h:516 * kt + 129 * h + 129],
                            start=(kt == 0 and b4 % 2 == 0), stop=(kt == nkt - 1)),
                            C.AVS if b4 == 3 else None)
                    pt_used[ppar] = C.AVS.n
                    if diag:
                        pm_used[rel % 2] = C.AVS.n

                LOOK = 2
                vnb_next = [None]
                for kt in range(nkt + LOOK):
                    if kt < nkt:
                        emit_S(kt)
                    if kt == 0 and h < 3:
                        vnb_next[0] = emit_bounds_a(h + 1)
                    if kt - LOOK >= 0:
                        emit_tail(kt - LOOK)
                    if kt == 2 and h < 3:
                        cnb_of[h + 1] = emit_bounds_b(h + 1, vnb_next[0])
                cav = C.AVS.n
                wt(C, P, "vector", C.AVS, cav)
                wt(C, P, "vector", C.STQ, ost_stored)
                for b4 in range(4):
                    pbank = bk[b4 // 2].ap()[:, 129 * (b4 % 2):129 * (b4 % 2) + 129]
                    vop(C, P, lambda e, pbank=pbank: e.tensor_scalar(
                        st[:, 4:5], pbank[:, 128:129], 1e-30, None, ALU.add))
                    vop(C, P, lambda e: e.reciprocal(st[:, 5:6], st[:, 4:5]))
                    lastnrm = (b4 == 3)
                    wt(C, P, "vector", C.VS)
                    op(C, P, "vector", lambda e, b4=b4, h=h, pbank=pbank: e.tensor_scalar_mul(
                        C.Ost_sb.ap()[:, 512 * b4 + 128 * h:512 * b4 + 128 * h + 128],
                        pbank[:, 0:128], st[:, 5:6]),
                        C.NRM if lastnrm else C.VS)
                nrm_of_set[sset] = C.NRM.n
                nrm_last = C.NRM.n
                wt(C, P, "vector", C.NRM, nrm_last)
            wt(C, P, "scalar", C.NRM, C.NRM.n)
            cst = C.STQ.inc(16)
            i0 = [("op", lambda e, j=j, g=g, s=C.STQ.h: e.dma_start(
                out=C.SH_O[512 * j:512 * j + 512, 512 * g:512 * g + 512].rearrange(
                    "(b p) f -> p b f", p=128),
                in_=C.Ost_sb.ap().rearrange("p (b f) -> p b f", f=512)).then_inc(s, 16))]
            i1 = [("op", lambda e, j=j, g=g, s=C.STQ.h: e.dma_start(
                out=C.SH_O[512 * j:512 * j + 512, 1024 + 512 * g:1024 + 512 * g + 512].rearrange(
                    "(b p) f -> p b f", p=128),
                in_=C.Ost_sb.ap().rearrange("p (b f) -> p b f", f=512)).then_inc(s, 16))]
            P.br("scalar", "memb", i0, i1)
            ost_stored = cst

        xs_free = q_consumed

    wt(C, P, "tensor", C.NRM, C.NRM.n)
    op(C, P, "tensor", lambda e: e.transpose(
        C.PBF[0].ap()[:, 0:128], C.idb_sb.ap(), C.idb_sb.ap()), C.ARM, 1)
    wt(C, P, "scalar", C.AS)
    wt(C, P, "scalar", C.STQ, C.STQ.n)
    op(C, P, "scalar", lambda e: e.copy(C.tinyb_sb.ap()[0:1, 4:5], C.tinyb_sb.ap()[0:1, 5:6]), C.ARM, 1)
    wt(C, P, "vector", C.VS)
    op(C, P, "vector", lambda e: e.tensor_copy(st[0:1, 6:7], st[0:1, 7:8]), C.ARM, 1)
    barrier(C, P, 1, [(C.STQ, C.STQ.n)])


def plan_phase2(C, P):
    st = C.tiny_sb.ap()
    wproj = C.wproj_sb.ap()
    wbase = C.wbase_sb.ap()
    wt(C, P, "sync", C.ARM, 3)
    wt(C, P, "sync", C.BAR, C.BAR.n)
    wt(C, P, "sync", C.STQ, C.c_smalls_stq)
    dma(C, P, "sync", C.LD2, C.aall_sb.ap(), C.D_AALL)
    dma(C, P, "sync", C.LD2, C.ball_sb.ap(), C.D_BALL)
    dma(C, P, "sync", C.LD2, C.wrout_sb.ap(), C.D_WROUT)
    dma(C, P, "sync", C.LD2, C.bias2_sb.ap()[0:1, :], C.D_BIAS[0:1, :])
    dma(C, P, "sync", C.LD2, C.bias2_sb.ap()[64:65, :], C.D_BIAS[1:2, :])
    dma(C, P, "sync", C.LD2, C.bias2_sb.ap()[32:33, 0:8], C.D_BIAS[2:3, 0:8])
    dma(C, P, "sync", C.LD2, C.ones3_sb.ap()[32:33, :], C.ones_row)
    dma(C, P, "sync", C.LD2, C.ones3_sb.ap()[64:65, :], C.ones_row)
    dma(C, P, "sync", C.LD2, C.sel_sb.ap(), C.sel_c)
    c_smalls = dma(C, P, "sync", C.LD2, C.w7_sb.ap()[6:7, :], C.ones_row)
    cw = 0
    for c in range(NDC):
        cw = dma(C, P, "sync", C.LD2, wproj[:, 2048 * c:2048 * c + 2048],
                 C.SH_WP[128 * c:128 * c + 128, :])
    c_wp = cw
    for c in range(NDC):
        cw = dma(C, P, "sync", C.LD2, wbase[:, 2048 * c:2048 * c + 2048],
                 C.SH_WB[128 * c:128 * c + 128, :])
    c_wb = cw

    xres = C.xres_sb.ap()
    osb = C.o_sb.ap()[:, 0:D]
    ot = C.ot_sb.ap()
    big = C.big_sb.ap()
    x2 = C.x2_sb.ap()
    xn2 = C.xn2_sb.ap()
    xn2t = C.xn2t_sb.ap()
    outb = C.out_sb.ap()
    g8 = C.g8_sb.ap()

    bnst = C.bnst_sb.ap()

    def ln_stats(src, vw):
        for s, v in vw:
            wt(C, P, "vector", s, v)
        for q in range(4):
            v2op(C, P, lambda e, q=q: e.bn_stats(bnst[:, 6 * q:6 * q + 6],
                                                 src[:, 512 * q:512 * q + 512]))
        cv = v2op(C, P, lambda e: e.bn_aggr(st[:, 0:2], bnst))
        wt(C, P, "scalar", C.VS2, cv)
        a2op(C, P, lambda e: e.activation(st[:, 2:3], st[:, 1:2], AF.Ln, bias=C.eps_sb.ap()[:, 0:1]))
        crs = a2op(C, P, lambda e: e.activation(st[:, 3:4], st[:, 2:3], AF.Exp, scale=-0.5))
        wt(C, P, "vector", C.AS2, crs)
        v2op(C, P, lambda e: e.tensor_tensor(st[:, 2:3], st[:, 0:1], st[:, 3:4], ALU.mult))

    S = {
        "c_xload": {}, "c_oload": {}, "c_otr": {}, "tr_ev": {0: 0, 1: 0},
        "otr_ev": {0: 0, 1: 0}, "xn2_ev": {0: 0, 1: 0},
        "projev": {0: 0, 1: 0}, "ln4_norm": 0, "c_ffn": {},
        "cLN2": {}, "cLN4": {}, "c_add": {}, "cxn2": {}, "c_xn2tr": {},
        "c_rout": {}, "c_hmm": {}, "c_hwf_ev": 0, "cw2": 0, "cge": 0,
        "chw": {}, "c_ost": {}, "c_xntr_done": 0,
    }

    def seg_oload(i):
        if i > 0:
            wt(C, P, "sync", C.T2, S["c_otr"][i - 1])
        S["c_oload"][i] = C.LD2.inc(16)
        i0 = [("op", lambda e, i=i, s=C.LD2.h: e.dma_start(
            out=osb, in_=C.SH_O[128 * i:128 * i + 128, :]).then_inc(s, 16))]
        i1 = [("op", lambda e, i=i, s=C.LD2.h: e.dma_start(
            out=osb, in_=C.SH_O[2048 + 128 * i:2048 + 128 * i + 128, :]).then_inc(s, 16))]
        P.br("sync", "memb", i0, i1)

    def seg_xload(i):
        if i > 0:
            wt(C, P, "sync", C.VS2, S["c_add"].get(i - 1, 0))
        S["c_xload"][i] = dma(C, P, "sync", C.LD2, xres, C.x_res[128 * i:128 * i + 128, :])

    def seg_otr(i):
        wt(C, P, "tensor", C.LD2, S["c_oload"][i])
        if i == 0:
            wt(C, P, "tensor", C.LD2, c_wp)
            wt(C, P, "tensor", C.LD2, c_smalls)
        for half in range(2):
            pb = C.PBF[half].ap()
            wt(C, P, "tensor", C.AS2, S["tr_ev"][half])
            for cc in range(8):
                c = half * 8 + cc
                op(C, P, "tensor", lambda e, c=c, cc=cc, pb=pb: e.transpose(
                    pb[:, 128 * cc:128 * cc + 128], osb[:, 128 * c:128 * c + 128],
                    C.idb_sb.ap()), C.T2 if cc == 7 else None)
            ct = C.T2.n
            wt(C, P, "scalar", C.T2, ct)
            ev = a2op(C, P, lambda e, pb=pb, half=half: e.copy(
                ot[:, 1024 * half:1024 * half + 1024], pb), chain=False)
            S["tr_ev"][half] = ev
            S["otr_ev"][half] = ev
        S["c_otr"][i] = C.T2.n

    def seg_xn2tr_tr(i):
        wt(C, P, "tensor", C.VS2, S["cxn2"][i])
        for half in range(2):
            pb = C.PBF[half].ap()
            wt(C, P, "tensor", C.AS2, S["tr_ev"][half])
            for cc in range(8):
                c = half * 8 + cc
                op(C, P, "tensor", lambda e, c=c, cc=cc, pb=pb: e.transpose(
                    pb[:, 128 * cc:128 * cc + 128], xn2[:, 128 * c:128 * c + 128],
                    C.idb_sb.ap()), C.T2 if (half, cc) == (1, 7) else None)
        S["c_xntr_done"] = C.T2.n
        S["c_xn2tr"][i] = C.T2.n

    def seg_xn2tr_ev(gate):
        for half in range(2):
            pb = C.PBF[half].ap()
            wt(C, P, "scalar", C.T2, gate)
            ev = a2op(C, P, lambda e, pb=pb, half=half: e.copy(
                xn2t[:, 1024 * half:1024 * half + 1024], pb), chain=False)
            S["tr_ev"][half] = ev
            S["xn2_ev"][half] = ev

    def seg_proj(i):
        for r in range(2):
            wt(C, P, "tensor", C.AS2, S["projev"][1] if r == 0 else S["projev"][0])
            for c in range(NDC):
                if r == 0 and c in (0, 8):
                    wt(C, P, "tensor", C.AS2, S["otr_ev"][c // 8])
                for fi in range(2):
                    fc = 2 * r + fi
                    op(C, P, "tensor", lambda e, c=c, fc=fc, fi=fi: e.matmul(
                        C.PS6[fi].ap()[:, 0:512],
                        ot[:, 128 * c:128 * c + 128],
                        wproj[:, 2048 * c + 512 * fc:2048 * c + 512 * fc + 512],
                        start=(c == 0), stop=False), None)
            for fi in range(2):
                fc = 2 * r + fi
                op(C, P, "tensor", lambda e, fc=fc, fi=fi: e.matmul(
                    C.PS6[fi].ap()[:, 0:512], C.onr_sb.ap()[0:1, :],
                    C.bias2_sb.ap()[0:1, 512 * fc:512 * fc + 512],
                    start=False, stop=True), C.T2 if fi == 1 else None)
            cpr = C.T2.n
            if r == 0 and i >= 1:
                seg_xn2tr_tr(i - 1)
            wt(C, P, "scalar", C.T2, cpr)
            if r == 0:
                wt(C, P, "scalar", C.VS2, S["cLN2"].get(i - 1, 0))
            for fi in range(2):
                fc = 2 * r + fi
                ev = a2op(C, P, lambda e, fc=fc, fi=fi: e.copy(
                    big[:, 512 * fc:512 * fc + 512], C.PS6[fi].ap()[:, 0:512]), chain=False)
            S["projev"][r] = ev
        if i >= 1:
            seg_xn2tr_ev(S["c_xn2tr"][i - 1])
        if i + 1 <= 15:
            seg_otr(i + 1)
        if i + 2 <= 15:
            seg_oload(i + 2)

    def seg_ln23(i):
        sl = i % 2
        pe = S["projev"][1]
        ln_stats(big, [(C.AS2, pe)])
        wt(C, P, "vector", C.LD2, S["c_xload"][i])
        v2op(C, P, lambda e, sl=sl: e.tensor_scalar(
            x2[:, sl * D:sl * D + D], big, st[:, 3:4], st[:, 2:3], ALU.mult, ALU.subtract))
        S["cLN2"][i] = C.VS2.n
        S["c_add"][i] = v2op(C, P, lambda e, sl=sl: e.tensor_add(
            x2[:, sl * D:sl * D + D], x2[:, sl * D:sl * D + D], xres))
        ln_stats(x2[:, sl * D:sl * D + D], [])
        wt(C, P, "vector", C.T2, S["c_xn2tr"].get(i - 1, 0))
        S["cxn2"][i] = v2op(C, P, lambda e, sl=sl: e.tensor_scalar(
            xn2, x2[:, sl * D:sl * D + D], st[:, 3:4], st[:, 2:3], ALU.mult, ALU.subtract))

    def seg_rh(i):
        if i == 0:
            wt(C, P, "tensor", C.LD2, c_wb)
        if i > 0:
            wt(C, P, "tensor", C.AS2, S["c_hwf_ev"])
            wt(C, P, "tensor", C.VS2, S["ln4_norm"])
        for c in range(NDC):
            if c in (0, 8):
                wt(C, P, "tensor", C.AS2, S["xn2_ev"][c // 8])
            op(C, P, "tensor", lambda e, c=c: e.matmul(
                C.PAV[1].ap()[:, 0:8], xn2t[:, 128 * c:128 * c + 128],
                C.wrout_sb.ap()[:, 8 * c:8 * c + 8],
                start=(c == 0), stop=False), None)
        op(C, P, "tensor", lambda e: e.matmul(
            C.PAV[1].ap()[:, 0:8], C.ones3_sb.ap()[32:33, :], C.bias2_sb.ap()[32:33, 0:8],
            start=False, stop=True), C.T2)
        S["c_rout"][i] = C.T2.n
        for c in range(NDC):
            op(C, P, "tensor", lambda e, c=c: e.matmul(
                C.PAV[0].ap()[0:PACK, 0:128],
                C.aall_sb.ap()[:, PACK * c:PACK * c + PACK], xn2t[:, 128 * c:128 * c + 128],
                start=(c == 0), stop=(c == NDC - 1)), C.T2 if c == NDC - 1 else None)
        S["c_hmm"][i] = C.T2.n

    def seg_softmax_v(i):
        wt(C, P, "vector", C.T2, S["c_rout"][i])
        v2op(C, P, lambda e: e.reduce_max(st[:, 4:5], C.PAV[1].ap()[:, 0:8], axis=AX.X))
        cgm = v2op(C, P, lambda e: e.tensor_scalar_mul(st[:, 7:8], st[:, 4:5], -1.0))
        wt(C, P, "scalar", C.VS2, cgm)
        S["cge"] = a2op(C, P, lambda e: e.activation(g8, C.PAV[1].ap()[:, 0:8], AF.Exp, bias=st[:, 7:8]))
        wt(C, P, "vector", C.AS2, S["cge"])
        v2op(C, P, lambda e: e.reduce_sum(st[:, 5:6], g8, axis=AX.X))
        v2op(C, P, lambda e: e.reciprocal(st[:, 6:7], st[:, 5:6]))
        v2op(C, P, lambda e: e.tensor_scalar_mul(C.e2_sb.ap(), g8, st[:, 6:7]))
        v2op(C, P, lambda e: e.reduce_max(st[:, 4:5], C.e2_sb.ap(), axis=AX.X))
        v2op(C, P, lambda e: e.tensor_scalar(C.e1_sb.ap(), C.e2_sb.ap(), st[:, 4:5], None, ALU.is_equal))
        v2op(C, P, lambda e: e.scalar_tensor_tensor(g8, C.e1_sb.ap(), NEG_BIG, C.e2_sb.ap(), ALU.mult, ALU.add))
        v2op(C, P, lambda e: e.reduce_max(st[:, 5:6], g8, axis=AX.X))
        v2op(C, P, lambda e: e.tensor_scalar(C.w2_sb.ap(), g8, st[:, 5:6], None, ALU.is_equal))
        v2op(C, P, lambda e: e.tensor_add(g8, C.e1_sb.ap(), C.w2_sb.ap()))
        S["cw2"] = v2op(C, P, lambda e: e.tensor_mul(C.w2_sb.ap(), C.e2_sb.ap(), g8))

    def seg_ffn(i):
        FB = [C.PS6[2], C.PS6[3], C.PAV[0], C.PAV[1]]
        wt(C, P, "tensor", C.VS2, S["ln4_norm"])
        for c in range(NDC):
            for fi in range(2):
                op(C, P, "tensor", lambda e, c=c, fi=fi, bk=FB[fi]: e.matmul(
                    bk.ap()[:, 0:512],
                    xn2t[:, 128 * c:128 * c + 128],
                    wbase[:, 2048 * c + 512 * fi:2048 * c + 512 * fi + 512],
                    start=(c == 0), stop=False), None)
        wt(C, P, "tensor", C.VS2, S["cw2"])
        wt(C, P, "tensor", C.AS2, S["cge"])
        op(C, P, "tensor", lambda e: e.transpose(
            C.PAV[1].ap()[0:8, 128:256], C.w2_sb.ap(), C.idf_sb.ap()), C.T2)
        cwt = C.T2.n
        wt(C, P, "scalar", C.T2, cwt)
        cw7 = a2op(C, P, lambda e: e.copy(C.w7_sb.ap()[0:6, :], C.PAV[1].ap()[0:6, 128:256]))
        wt(C, P, "tensor", C.AS2, cw7)
        op(C, P, "tensor", lambda e: e.matmul(
            C.PAV[0].ap()[0:PACK, 128:256], C.sel_sb.ap()[0:7, :], C.w7_sb.ap()[0:7, :],
            start=True, stop=True), C.T2)
        c_wfull = C.T2.n
        wt(C, P, "scalar", C.T2, c_wfull)
        a2op(C, P, lambda e: e.copy(C.h_sb.ap(), C.PAV[0].ap()[0:PACK, 0:128]), chain=False)
        S["c_hwf_ev"] = a2op(C, P, lambda e: e.copy(
            C.wf_sb.ap(), C.PAV[0].ap()[0:PACK, 128:256]), chain=False)
        wt(C, P, "vector", C.AS2, S["c_hwf_ev"])
        S["chw"][i] = v2op(C, P, lambda e: e.tensor_mul(C.hw_sb.ap(), C.h_sb.ap(), C.wf_sb.ap()))
        if i + 1 <= 15:
            seg_ln23(i + 1)
        wt(C, P, "tensor", C.AS2, S["c_hwf_ev"])
        for c in range(NDC):
            for fi in range(2, 4):
                op(C, P, "tensor", lambda e, c=c, fi=fi, bk=FB[fi]: e.matmul(
                    bk.ap()[:, 0:512],
                    xn2t[:, 128 * c:128 * c + 128],
                    wbase[:, 2048 * c + 512 * fi:2048 * c + 512 * fi + 512],
                    start=(c == 0), stop=False), None)
        wt(C, P, "tensor", C.VS2, S["chw"][i])
        for fi in range(4):
            op(C, P, "tensor", lambda e, fi=fi, bk=FB[fi]: e.matmul(
                bk.ap()[:, 0:512], C.hw_sb.ap(),
                C.ball_sb.ap()[:, 512 * fi:512 * fi + 512],
                start=False, stop=False), None)
        for fi in range(4):
            op(C, P, "tensor", lambda e, fi=fi, bk=FB[fi]: e.matmul(
                bk.ap()[:, 0:512], C.ones3_sb.ap()[64:65, :],
                C.bias2_sb.ap()[64:65, 512 * fi:512 * fi + 512],
                start=False, stop=True), C.T2 if fi == 3 else None)
        S["c_ffn"][i] = C.T2.n

    def seg_ln4_out(i):
        FB = [C.PS6[2], C.PS6[3], C.PAV[0], C.PAV[1]]
        sl = i % 2
        cf = S["c_ffn"][i]
        wt(C, P, "vector", C.T2, cf)
        for fi in range(4):
            v2op(C, P, lambda e, fi=fi, bk=FB[fi]: e.bn_stats(
                bnst[:, 6 * fi:6 * fi + 6], bk.ap()[:, 0:512]))
        cv = v2op(C, P, lambda e: e.bn_aggr(st[:, 0:2], bnst))
        wt(C, P, "scalar", C.VS2, cv)
        a2op(C, P, lambda e: e.activation(st[:, 2:3], st[:, 1:2], AF.Ln, bias=C.eps_sb.ap()[:, 0:1]))
        crs = a2op(C, P, lambda e: e.activation(st[:, 3:4], st[:, 2:3], AF.Exp, scale=-0.5))
        wt(C, P, "vector", C.AS2, crs)
        v2op(C, P, lambda e: e.tensor_tensor(st[:, 2:3], st[:, 0:1], st[:, 3:4], ALU.mult))
        wt(C, P, "vector", C.ST2, S["c_ost"].get(i - 1, 0))
        cn = 0
        for fi in (2, 3, 0, 1):
            cn = v2op(C, P, lambda e, fi=fi, bk=FB[fi]: e.tensor_scalar(
                outb[:, 512 * fi:512 * fi + 512], bk.ap()[:, 0:512],
                st[:, 3:4], st[:, 2:3], ALU.mult, ALU.subtract))
        S["ln4_norm"] = cn
        S["cLN4"][i] = cn
        cout = v2op(C, P, lambda e, sl=sl: e.tensor_add(outb, outb, x2[:, sl * D:sl * D + D]))
        wt(C, P, "gpsimd", C.VS2, cout)
        S["c_ost"][i] = dma(C, P, "gpsimd", C.ST2, C.out[128 * i:128 * i + 128, :], outb)

    seg_xload(0)
    seg_oload(0)
    seg_otr(0)
    seg_oload(1)
    for k in range(-1, 16):
        if k + 1 <= 15:
            seg_proj(k + 1)
        if k == -1:
            seg_ln23(0)
        if k >= 0:
            if k == 15:
                seg_xn2tr_tr(k)
                seg_xn2tr_ev(S["c_xntr_done"])
            seg_rh(k)
            seg_softmax_v(k)
            seg_ffn(k)
            seg_ln4_out(k)
        if k + 2 <= 15:
            seg_xload(k + 2)

    wt(C, P, "gpsimd", C.ST2, C.ST2.n)


def finish_debug(C):
    _close_block(C)


def _close_block(C):
    with C.nc.Block() as block:
        def run(ename):
            def f(e):
                if ename == "sync":
                    e.dma_start(out=C.rmt_sb.ap(), in_=C.rmt).then_inc(C.LD.h, 16)
                e.wait_ge(C.LD.h, C.c_rmt)
                r_role = e.alloc_register()
                e.reg_load(r_role, C.rmt_sb.ap()[0:1, 0:1])
                r_memb = e.alloc_register()
                e.reg_load(r_memb, C.rmt_sb.ap()[0:1, 1:2])
                regs = {"role": r_role, "memb": r_memb}
                if ename == "gpsimd":
                    C.GREGS["memb"] = r_memb
                    r_tok = e.alloc_register()
                    e.reg_load(r_tok, C.rmt_sb.ap()[0:1, 2:3])
                    r_pf = e.alloc_register()
                    r_plcnt = e.alloc_register()
                    r_cond = e.alloc_register()
                    e.reg_mov(r_plcnt, 0)
                    C.GREGS.update(tok=r_tok, pf=r_pf, plcnt=r_plcnt, cond=r_cond)
                _play(e, C.P.q[ename], regs)
            return f
        block.sync(run("sync"))
        block.tensor(run("tensor"))
        block.scalar(run("scalar"))
        block.vector(run("vector"))
        block.gpsimd(run("gpsimd"))
    for g in C._sem_guards:
        g.__exit__(None, None, None)


def build():
    C = make_context()
    alloc_phase1(C)
    C.GREGS = {}
    plan_prolog(C)

    semnames = C._semnames
    dl = DEBUG_LEVEL
    if dl == 1:
        finish_debug(C)
        return C
    limit = {2: "phase0", 25: "bar0", 26: "kvonly", 261: "kv1g", 27: "wq",
             28: "kn_novec", 29: "kn_nolnexp", 3: "kv",
             42: "bar0", 43: "kv", 44: None, 45: "tap45"}.get(dl, None)
    snap = {k: getattr(C, k).n for k in semnames}
    Pa = Plan()
    plan_phase0(C, Pa, 0)
    if limit != "phase0":
        plan_phase1(C, Pa, 0, limit=limit)
    cnt_a = {k: getattr(C, k).n for k in semnames}
    for k in semnames:
        getattr(C, k).n = snap[k]
    Pb = Plan()
    plan_phase0(C, Pb, 1)
    if limit != "phase0":
        plan_phase1(C, Pb, 1, limit=limit)
    cnt_b = {k: getattr(C, k).n for k in semnames}
    if limit is None:
        assert cnt_a["ARM"] == cnt_b["ARM"] == snap["ARM"] + 3, (cnt_a["ARM"], cnt_b["ARM"])
        assert cnt_a["BAR"] == cnt_b["BAR"], (cnt_a["BAR"], cnt_b["BAR"])
    if dl == 45:
        C.P.splice_role(Pa, Pb)
        finish_debug(C)
        return C
    for k in semnames:
        getattr(C, k).n = max(cnt_a[k], cnt_b[k])
    C.P.splice_role(Pa, Pb)
    P = C.P
    if dl == 42:
        wt(C, P, "gpsimd", C.BAR, min(cnt_a["BAR"], cnt_b["BAR"]))
        i0 = [("op", lambda e, s=C.GQ.h: e.dma_start(
            out=C.dbg.rearrange("(c p) t -> p c t", p=128),
            in_=C.SH_XNT.rearrange("(c p) t -> p c t", p=128)[:, :, 0:2048]).then_inc(s, 16))]
        i1 = [("op", lambda e, s=C.GQ.h: e.dma_start(
            out=C.dbg.rearrange("(c p) t -> p c t", p=128),
            in_=C.SH_XNT.rearrange("(c p) t -> p c t", p=128)[:, :, 4096:4096 + 2048]).then_inc(s, 16))]
        C.GQ.inc(16)
        P.br("gpsimd", "memb", i0, i1)
        wt(C, P, "gpsimd", C.GQ, C.GQ.n)
        finish_debug(C)
        return C
    if dl == 43:
        wt(C, P, "gpsimd", C.BAR, min(cnt_a["BAR"], cnt_b["BAR"]))
        dma(C, P, "gpsimd", C.GQ, C.dbg[0:128, 0:2048], C.K_sb.ap()[:, 0:2048])
        dma(C, P, "gpsimd", C.GQ, C.dbg[128:256, 0:2048], C.V_sb.ap()[:, 0:2048])
        dma(C, P, "gpsimd", C.GQ, C.dbg[384:385, 0:16], C.kn_sb.ap().bitcast(BF16)[0:1, 0:32][0:1, 0:16])
        wt(C, P, "gpsimd", C.GQ, C.GQ.n)
        finish_debug(C)
        return C
    if dl == 44:
        wt(C, P, "gpsimd", C.BAR, min(cnt_a["BAR"], cnt_b["BAR"]))
        i0 = [("op", lambda e, s=C.GQ.h: e.dma_start(
            out=C.dbg, in_=C.SH_O[0:2048, :]).then_inc(s, 16))]
        i1 = [("op", lambda e, s=C.GQ.h: e.dma_start(
            out=C.dbg, in_=C.SH_O[2048:4096, :]).then_inc(s, 16))]
        C.GQ.inc(16)
        P.br("gpsimd", "memb", i0, i1)
        wt(C, P, "gpsimd", C.GQ, C.GQ.n)
        finish_debug(C)
        return C
    if dl in (2, 25, 26, 261, 27, 28, 29, 3, 4):
        finish_debug(C)
        return C

    for name in ("sqq_sb", "negb_sb", "Ost_sb", "Pm_sb", "Pt_sb", "Q_sb",
                 "wkv_sb", "xs_sb", "V_sb", "K_sb"):
        C.sbfree(name)
    C.sbuf("wproj_sb", (128, NDC * 2048), BF16)
    C.sbuf("wbase_sb", (128, NDC * 2048), BF16)
    C.sbuf("xres_sb", (128, D), F32)
    C.sbuf("o_sb", (128, D), BF16)
    C.sbuf("ot_sb", (128, D), BF16)
    C.sbuf("big_sb", (128, D), F32)
    C.sbuf("x2_sb", (128, 2 * D), F32)
    C.sbuf("xn2_sb", (128, D), BF16)
    C.sbuf("out_sb", (128, D), F32)
    C.sbuf("g8_sb", (128, 8), F32)
    C.sbuf("e1_sb", (128, 8), F32)
    C.sbuf("e2_sb", (128, 8), F32)
    C.sbuf("w2_sb", (128, 8), F32)
    C.sbuf("h_sb", (PACK, 128), BF16)
    C.sbuf("wf_sb", (PACK, 128), BF16)
    C.sbuf("hw_sb", (PACK, 128), BF16)
    C.sbuf("aall_sb", (128, NDC * PACK), BF16)
    C.sbuf("ball_sb", (PACK, D), BF16)
    C.sbuf("wrout_sb", (128, NDC * 8), BF16)
    C.sbuf("sel_sb", (8, PACK), BF16)
    C.sbuf("w7_sb", (8, 128), BF16)
    C.sbuf("bias2_sb", (128, D), BF16)
    C.sbuf("ones3_sb", (128, 128), BF16)
    C.sbuf("bnst_sb", (128, 24), F32)
    C.sbuf("xn2t_sb", (128, D), BF16)

    plan_phase2(C, C.P)
    _close_block(C)
    return C


_NC_CACHE = [None]
DEBUG_LEVEL = 0


def kernel(**L):
    xin = np.asarray(L["L0_attn_x"])
    w_qkv = np.asarray(L["L0_attn_w_qkv"])
    b_qkv = np.asarray(L["L0_attn_b_qkv"])
    w_proj = np.asarray(L["L0_attn_w_proj"])
    b_proj = np.asarray(L["L0_attn_b_proj"])
    W_base = np.asarray(L["L0_ffn_W_base"])
    b_base = np.asarray(L["L0_ffn_b_base"])
    A_s = np.asarray(L["L0_ffn_A_s"])
    B_s = np.asarray(L["L0_ffn_B_s"])
    A_r = np.asarray(L["L0_ffn_A_r"])
    B_r = np.asarray(L["L0_ffn_B_r"])
    W_router = np.asarray(L["L0_ffn_W_router"])
    b_router = np.asarray(L["L0_ffn_b_router"])

    if _NC_CACHE[0] is None:
        _NC_CACHE[0] = build()
    C = _NC_CACHE[0]

    token = int(np.random.randint(1, 2**31 - 1))
    in_maps = []
    rows_per_core = []
    for c in range(8):
        p, m = c // 2, c % 2
        b, role = p // 2, p % 2
        n = LN_N[role]
        xl = np.zeros((4096, D), dtype=np.float32)
        xl[:n] = xin[b, m * n:m * n + n]
        if role == 0:
            rows = np.r_[0:2048] if m == 0 else np.r_[2048:E1, E2:T]
        else:
            rows = np.r_[E1:E1 + 2048] if m == 0 else np.r_[E1 + 2048:E2]
        rows_per_core.append((b, rows))
        hcols = np.r_[1024 * m:1024 * m + 1024]
        wq = np.concatenate([w_qkv[:, hcols], w_qkv[:, 2048 + hcols], w_qkv[:, 4096 + hcols]], axis=1)
        bq = np.concatenate([b_qkv[hcols], b_qkv[2048 + hcols], b_qkv[4096 + hcols]])[None, :]
        in_maps.append({
            "x_ln": np.ascontiguousarray(xl),
            "x_res": np.ascontiguousarray(xin[b, rows]),
            "wqkv": np.ascontiguousarray(wq),
            "bqkv": np.ascontiguousarray(bq),
            "wproj_half": np.ascontiguousarray(w_proj[1024 * m:1024 * m + 1024]),
            "wbase_half": np.ascontiguousarray(W_base[1024 * m:1024 * m + 1024]),
            "bproj": np.ascontiguousarray(b_proj[None, :]),
            "bbase": np.ascontiguousarray(b_base[None, :]),
            "a_s": np.ascontiguousarray(A_s),
            "b_s": np.ascontiguousarray(B_s),
            "a_r": np.ascontiguousarray(A_r),
            "b_r": np.ascontiguousarray(B_r),
            "wrouter": np.ascontiguousarray(W_router),
            "brouter": np.ascontiguousarray(b_router[None, :]),
            "rmt": np.array([[role, m, token, 0]], dtype=np.int32),
        })

    res = run_bass_kernel_spmd(C.nc, in_maps, core_ids=list(range(8)))
    outv = np.zeros((B, T, D), dtype=np.float32)
    for c in range(8):
        b, rows = rows_per_core[c]
        outv[b, rows] = res.results[c]["out"]
    if DEBUG_LEVEL:
        kernel.dbg = [np.asarray(res.results[c]["dbg"]) for c in range(8)]
    return outv


# version 23 (final)
# speedup vs baseline: 2.5653x; 1.0093x over previous
import numpy as np
import ml_dtypes
from types import SimpleNamespace
import concourse.bass as bass
import concourse.mybir as mybir
from concourse.bass_utils import run_bass_kernel_spmd

F32 = mybir.dt.float32
BF16 = mybir.dt.bfloat16
I32 = mybir.dt.int32
AF = mybir.ActivationFunctionType
ALU = mybir.AluOpType
AX = mybir.AxisListType

B, T, D = 2, 8192, 2048
NDC = D // 128
CH = 512
SC = 256
E1, E2 = 2560, 6656
OUTER_SLOTS = [0, 1, 2, 3, 4, 13, 14, 15]
MIDDLE_SLOTS = [5, 6, 7, 8, 9, 10, 11, 12]
KV_RANGE = {0: T, 1: E2}
LN_N = {0: 4096, 1: 3328}
ATTN_MULT = 1.0 / float(np.sqrt(128.0))
NEG_BIG = -1e30
EPS = 1e-5
PACK = 112
ENGS = ("sync", "tensor", "scalar", "vector", "gpsimd")
ATT_J = 8


class Sem:
    def __init__(self, h):
        self.h = h
        self.n = 0

    def inc(self, k):
        self.n += k
        return self.n


class Plan:
    def __init__(self):
        self.q = {e: [] for e in ENGS}

    def em(self, e, fn):
        self.q[e].append(("op", fn))

    def br(self, e, key, it0, it1):
        self.q[e].append(("br", key, it0, it1))

    def splice_role(self, pa, pb):
        for e in ENGS:
            self.q[e].append(("br", "role", pa.q[e], pb.q[e]))


def _play(e, items, regs):
    for it in items:
        if it[0] == "op":
            it[1](e)
        else:
            _, key, it0, it1 = it
            with e.If_eq(regs[key], 0):
                _play(e, it0, regs)
            with e.If_ne(regs[key], 0):
                _play(e, it1, regs)


def dma(C, P, eng, sem, o, i):
    c = sem.inc(16)
    P.em(eng, lambda e, o=o, i=i, s=sem.h: e.dma_start(out=o, in_=i).then_inc(s, 16))
    return c


def dma_br(C, P, eng, sem, o0, o1, i):
    c = sem.inc(16)
    it0 = [("op", lambda e, o=o0, i=i, s=sem.h: e.dma_start(out=o, in_=i).then_inc(s, 16))]
    it1 = [("op", lambda e, o=o1, i=i, s=sem.h: e.dma_start(out=o, in_=i).then_inc(s, 16))]
    P.br(eng, "memb", it0, it1)
    return c


def wt(C, P, eng, sem, val=None):
    v = sem.n if val is None else val
    if v > 0:
        P.em(eng, lambda e, s=sem.h, v=v: e.wait_ge(s, v))


def op(C, P, eng, fn, sem=None, k=1):
    if sem is None:
        P.em(eng, lambda e, f=fn: f(e))
        return None
    c = sem.inc(k)
    P.em(eng, lambda e, f=fn, s=sem.h, k=k: f(e).then_inc(s, k))
    return c


def vop(C, P, fn, sem=None):
    wt(C, P, "vector", C.VS)
    return op(C, P, "vector", fn, sem if sem is not None else C.VS)


def aop(C, P, fn, chain=True):
    if chain:
        wt(C, P, "scalar", C.AS)
    return op(C, P, "scalar", fn, C.AS)


def v2op(C, P, fn):
    wt(C, P, "vector", C.VS2)
    return op(C, P, "vector", fn, C.VS2)


def a2op(C, P, fn, chain=True):
    if chain:
        wt(C, P, "scalar", C.AS2)
    return op(C, P, "scalar", fn, C.AS2)


def make_context():
    nc = bass.Bass("TRN2", target_bir_lowering=False)
    C = SimpleNamespace()
    C.nc = nc
    C.P = Plan()

    def din(name, shape, dtype=F32):
        a = nc.dram_tensor(name, list(shape), dtype, kind="ExternalInput").ap()
        setattr(C, name, a)

    din("x_ln", (4096, D))
    din("x_res", (2048, D))
    din("wqkv", (D, 3072))
    din("bqkv", (1, 3072))
    din("wproj_half", (1024, D))
    din("wbase_half", (1024, D))
    din("bproj", (1, D))
    din("bbase", (1, D))
    din("a_s", (D, 16))
    din("b_s", (16, D))
    din("a_r", (6, D, 16))
    din("b_r", (6, 16, D))
    din("wrouter", (D, 6))
    din("brouter", (1, 6))
    din("rmt", (1, 4), I32)

    C.out = nc.dram_tensor("out", [2048, D], F32, kind="ExternalOutput").ap()
    if DEBUG_LEVEL:
        C.dbg = nc.dram_tensor("dbg", [2048, D], BF16, kind="ExternalOutput").ap()

    C.MY_WQKV = nc.dram_tensor("MY_WQKV", [D, 3072], BF16, kind="Internal").ap()
    C.D_AALL = nc.dram_tensor("D_AALL", [128, NDC * PACK], BF16, kind="Internal").ap()
    C.D_BALL = nc.dram_tensor("D_BALL", [PACK, D], BF16, kind="Internal").ap()
    C.D_WROUT = nc.dram_tensor("D_WROUT", [128, NDC * 8], BF16, kind="Internal").ap()
    C.D_BIAS = nc.dram_tensor("D_BIAS", [4, D], BF16, kind="Internal").ap()
    C.SH_XNT = nc.dram_tensor("SH_XNT", [D, T], BF16, kind="Internal", addr_space="Shared").ap()
    C.SH_O = nc.dram_tensor("SH_O", [4096, D], BF16, kind="Internal", addr_space="Shared").ap()
    C.SH_WP = nc.dram_tensor("SH_WP", [D, D], BF16, kind="Internal", addr_space="Shared").ap()
    C.SH_WB = nc.dram_tensor("SH_WB", [D, D], BF16, kind="Internal", addr_space="Shared").ap()
    C.SH_FLAGS = nc.dram_tensor("SH_FLAGS", [2, 4, 16], I32, kind="Internal", addr_space="Shared").ap()

    C.ident_bf = nc.inline_tensor(np.eye(128, dtype=ml_dtypes.bfloat16), "c_identbf").ap()
    C.ident_f32 = nc.inline_tensor(np.eye(128, dtype=np.float32), "c_identf").ap()
    C.ones_col = nc.inline_tensor(np.ones((128, 1), dtype=ml_dtypes.bfloat16), "c_onescol").ap()
    C.ones_row = nc.inline_tensor(np.ones((1, 128), dtype=ml_dtypes.bfloat16), "c_onesrow").ap()
    mk = np.zeros((4, 128, CH), dtype=ml_dtypes.bfloat16)
    for r in range(4):
        for k in range(128):
            lo = 128 * r + k
            if lo < CH:
                mk[r, k, lo:] = 1.0
    C.masks_c = nc.inline_tensor(np.ascontiguousarray(mk), "c_masks").ap()
    sel = np.zeros((8, PACK), dtype=ml_dtypes.bfloat16)
    for ei in range(7):
        sel[ei, 16 * ei:16 * ei + 16] = 1.0
    C.sel_c = nc.inline_tensor(sel, "c_sel").ap()

    C._sem_guards = []
    C._semnames = []

    def S(name):
        g = nc.semaphore(name)
        h = g.__enter__()
        C._sem_guards.append(g)
        s = Sem(h)
        setattr(C, name, s)
        C._semnames.append(name)

    for name in ("LD", "LDW", "STQ", "GQ", "PL", "VS", "AS", "TS", "EXPS",
                 "AVS", "NRM", "KVE", "BAR", "ARM",
                 "LD2", "ST2", "VS2", "AS2", "T2"):
        S(name)

    C._sb_guards = {}

    def sbuf(name, shape, dtype, offset=None):
        if offset is None:
            g = nc.sbuf_tensor(name, list(shape), dtype)
            h = g.__enter__()
            C._sb_guards[name] = g
        else:
            h = nc.alloc_sbuf_tensor_at(name, list(shape), dtype, offset=offset)
        setattr(C, name, h)
        return h

    def sbfree(name):
        g = C._sb_guards.pop(name, None)
        if g is not None:
            g.__exit__(None, None, None)

    C.sbuf = sbuf
    C.sbfree = sbfree
    sbuf("rmt_sb", (1, 4), I32)
    sbuf("eps_sb", (128, 1), F32)
    sbuf("idb_sb", (128, 128), BF16)
    sbuf("idf_sb", (128, 128), F32)
    sbuf("onc_sb", (128, 1), BF16)
    sbuf("onr_sb", (1, 128), BF16)
    sbuf("masks_sb", (128, 4 * CH), BF16)
    sbuf("bqkvc_sb", (128, 16), F32)
    sbuf("bvrow_sb", (1, 1024), BF16)
    sbuf("kn_sb", (1, 16), F32)
    sbuf("tiny_sb", (128, 16), F32)
    sbuf("bnst0_sb", (128, 24), F32)
    sbuf("tinyb_sb", (1, 8), BF16)
    sbuf("gz_sb", (1, 1), I32)
    sbuf("pf_sb", (1, 16), I32)

    C.PS6 = [nc.alloc_psum_tensor(f"ps6_{i}", [128, 512], F32) for i in range(6)]
    C.PAV = [nc.alloc_psum_tensor(f"pav{i}", [128, 512], F32) for i in range(2)]
    C.PBF = [C.PS6[4].bitcast(BF16), C.PS6[5].bitcast(BF16)]
    return C


def alloc_phase1(C):
    C.sbuf("K_sb", (128, 4 * T), BF16)
    C.sbuf("V_sb", (128, 64 * 516), BF16)
    C.sbuf("xs_sb", (128, 2 * NDC * SC), BF16)
    C.sbuf("wkv_sb", (128, NDC * 1024), BF16)
    C.sbuf("Q_sb", (128, 2 * 4 * CH), BF16)
    C.sbuf("Pt_sb", (128, 4 * CH), BF16)
    C.sbuf("Pm_sb", (128, 2 * CH), BF16)
    C.sbuf("Ost_sb", (128, 2048), BF16)
    C.sbuf("negb_sb", (128, 8), F32)
    C.sbuf("wq_sb", (128, NDC * 512), BF16, offset=C.nc.lookup_mloc(C.wkv_sb).addr)
    C.sbuf("sqq_sb", (128, 2 * CH), BF16)
    kof = C.nc.lookup_mloc(C.K_sb).addr
    vof = C.nc.lookup_mloc(C.V_sb).addr
    C.sbuf("x0_sb", (128, 2 * D), F32, offset=kof)
    C.sbuf("xn_sb", (128, D), BF16, offset=kof + 16384)
    C.sbuf("sq0_sb", (128, D), BF16, offset=kof + 16384 + 4096)
    C.sbuf("xnt_sb", (128, 2 * NDC * CH), BF16, offset=kof + 24576)
    C.sbuf("stg_sb", (128, 2048), F32, offset=vof)
    C.sbuf("stgb_sb", (128, 2048), BF16, offset=vof + 8192)


def plan_prolog(C):
    P = C.P
    C.LD.inc(16)
    C.c_rmt = C.LD.n

    dma(C, P, "sync", C.LD, C.idb_sb.ap(), C.ident_bf)
    dma(C, P, "sync", C.LD, C.idf_sb.ap(), C.ident_f32)
    dma(C, P, "sync", C.LD, C.onc_sb.ap(), C.ones_col)
    dma(C, P, "sync", C.LD, C.onr_sb.ap(), C.ones_row)
    for r in range(4):
        dma(C, P, "sync", C.LD, C.masks_sb.ap()[:, r * CH:(r + 1) * CH], C.masks_c[r])

    c_msets = vop(C, P, lambda e: e.memset(C.eps_sb.ap(), EPS))

    stg = C.stg_sb.ap()
    stgb = C.stgb_sb.ap()
    stgA = stg[:, 0:NDC * PACK].rearrange("p (c k) -> p c k", k=PACK)
    for ei in range(6):
        dma(C, P, "sync", C.LD, stgA[:, :, 16 * ei:16 * ei + 16],
            C.a_r[ei].rearrange("(c p) r -> p c r", p=128))
    c1 = dma(C, P, "sync", C.LD, stgA[:, :, 96:112], C.a_s.rearrange("(c p) r -> p c r", p=128))
    wt(C, P, "scalar", C.LD, c1)
    cA = aop(C, P, lambda e: e.copy(stgb[:, 0:NDC * PACK], stg[:, 0:NDC * PACK]))
    wt(C, P, "scalar", C.AS, cA)
    dma(C, P, "scalar", C.STQ, C.D_AALL, stgb[:, 0:NDC * PACK])
    wt(C, P, "sync", C.AS, cA)
    for ei in range(6):
        dma(C, P, "sync", C.LD, stg[16 * ei:16 * ei + 16, 0:D], C.b_r[ei])
    c1 = dma(C, P, "sync", C.LD, stg[96:112, 0:D], C.b_s)
    wt(C, P, "scalar", C.LD, c1)
    cB = aop(C, P, lambda e: e.copy(stgb[0:PACK, 0:D], stg[0:PACK, 0:D]))
    wt(C, P, "scalar", C.AS, cB)
    dma(C, P, "scalar", C.STQ, C.D_BALL, stgb[0:PACK, 0:D])
    wt(C, P, "sync", C.AS, cB)
    cz = vop(C, P, lambda e: e.memset(stgb[:, 0:NDC * 8], 0.0))
    stgR = stg[:, 0:NDC * 6].rearrange("p (c r) -> p c r", r=6)
    c1 = dma(C, P, "sync", C.LD, stgR, C.wrouter.rearrange("(c p) r -> p c r", p=128))
    wt(C, P, "scalar", C.LD, c1)
    wt(C, P, "scalar", C.VS, cz)
    cR = aop(C, P, lambda e: e.copy(
        stgb[:, 0:NDC * 8].rearrange("p (c r) -> p c r", r=8)[:, :, 0:6], stgR))
    wt(C, P, "scalar", C.AS, cR)
    dma(C, P, "scalar", C.STQ, C.D_WROUT, stgb[:, 0:NDC * 8])
    wt(C, P, "sync", C.AS, cR)
    c1 = dma(C, P, "sync", C.LD, stg[0:1, 0:D], C.bproj)
    wt(C, P, "scalar", C.LD, c1)
    cb = aop(C, P, lambda e: e.copy(stgb[0:1, 0:D], stg[0:1, 0:D]))
    wt(C, P, "scalar", C.AS, cb)
    cs1 = dma(C, P, "scalar", C.STQ, C.D_BIAS[0:1, :], stgb[0:1, 0:D])
    wt(C, P, "sync", C.AS, cb)
    c1 = dma(C, P, "sync", C.LD, stg[0:1, 0:D], C.bbase)
    wt(C, P, "scalar", C.LD, c1)
    wt(C, P, "scalar", C.STQ, cs1)
    cb = aop(C, P, lambda e: e.copy(stgb[0:1, 0:D], stg[0:1, 0:D]))
    wt(C, P, "scalar", C.AS, cb)
    cs1 = dma(C, P, "scalar", C.STQ, C.D_BIAS[1:2, :], stgb[0:1, 0:D])
    wt(C, P, "sync", C.AS, cb)
    c1 = dma(C, P, "sync", C.LD, stg[0:1, 0:6], C.brouter)
    wt(C, P, "vector", C.STQ, cs1)
    cbm = vop(C, P, lambda e: e.memset(stgb[0:1, 0:8], NEG_BIG))
    wt(C, P, "scalar", C.LD, c1)
    wt(C, P, "scalar", C.VS, cbm)
    wt(C, P, "scalar", C.STQ, cs1)
    cb = aop(C, P, lambda e: e.copy(stgb[0:1, 0:6], stg[0:1, 0:6]))
    wt(C, P, "scalar", C.AS, cb)
    cs1 = dma(C, P, "scalar", C.STQ, C.D_BIAS[2:3, 0:8], stgb[0:1, 0:8])
    wt(C, P, "sync", C.AS, cb)
    c1 = dma(C, P, "sync", C.LD, stg[0:1, 0:1024], C.bqkv[0:1, 2048:3072])
    wt(C, P, "scalar", C.LD, c1)
    aop(C, P, lambda e: e.copy(C.bvrow_sb.ap(), stg[0:1, 0:1024]))
    C.c_smalls_stq = C.STQ.n
    wt(C, P, "sync", C.AS, C.AS.n)
    c1 = dma(C, P, "sync", C.LD, stg[0:1, 0:2048], C.bqkv[0:1, 0:2048])
    wt(C, P, "tensor", C.LD, c1)
    for c in range(16):
        op(C, P, "tensor", lambda e, c=c: e.transpose(
            C.PS6[0].ap()[:, c:c + 1], stg[0:1, 128 * c:128 * c + 128],
            C.idf_sb.ap()[0:1, 0:1]), C.TS if c == 15 else None)
    c_biasT = C.TS.n
    wt(C, P, "scalar", C.TS, c_biasT)
    cqb = aop(C, P, lambda e: e.copy(C.bqkvc_sb.ap(), C.PS6[0].ap()[:, 0:16]))
    wt(C, P, "vector", C.AS, cqb)
    vop(C, P, lambda e: e.tensor_scalar_mul(C.bqkvc_sb.ap()[:, 0:8], C.bqkvc_sb.ap()[:, 0:8], ATTN_MULT))
    C.c_bias_ps_free = cqb

    jobs = []
    for c in range(NDC):
        for part in range(3):
            src = C.wqkv[128 * c:128 * c + 128, 1024 * part:1024 * part + 1024]
            dst = C.MY_WQKV[128 * c:128 * c + 128, 1024 * part:1024 * part + 1024]
            jobs.append((src, dst, None))
    for r8 in range(8):
        for halfc in range(2):
            cs = slice(1024 * halfc, 1024 * halfc + 1024)
            jobs.append((C.wproj_half[128 * r8:128 * r8 + 128, cs],
                         C.SH_WP[128 * r8:128 * r8 + 128, cs],
                         C.SH_WP[1024 + 128 * r8:1024 + 128 * r8 + 128, cs]))
            jobs.append((C.wbase_half[128 * r8:128 * r8 + 128, cs],
                         C.SH_WB[128 * r8:128 * r8 + 128, cs],
                         C.SH_WB[1024 + 128 * r8:1024 + 128 * r8 + 128, cs]))

    c_stg_done = C.AS.n
    prev_store = {0: 0, 1: 0}
    for j, (src, d0, d1) in enumerate(jobs):
        sl = j % 2
        sslc = stg[:, 1024 * sl:1024 * sl + 1024]
        bslc = C.stgb_sb.ap()[:, 1024 * sl:1024 * sl + 1024]
        if j < 2:
            wt(C, P, "sync", C.AS, c_stg_done)
            wt(C, P, "sync", C.TS, c_biasT)
            wt(C, P, "sync", C.STQ, C.c_smalls_stq)
        wt(C, P, "sync", C.STQ, prev_store[sl])
        cl = dma(C, P, "sync", C.LDW, sslc, src)
        wt(C, P, "scalar", C.LDW, cl)
        wt(C, P, "scalar", C.STQ, prev_store[sl])
        if j < 2:
            wt(C, P, "scalar", C.STQ, C.c_smalls_stq)
        ccast = aop(C, P, lambda e, o=bslc, i=sslc: e.copy(o, i))
        wt(C, P, "scalar", C.AS, ccast)
        if d1 is None:
            cst = dma(C, P, "scalar", C.STQ, d0, bslc)
        else:
            cst = dma_br(C, P, "scalar", C.STQ, d0, d1, bslc)
        prev_store[sl] = cst
    C.c_wconv = C.STQ.n


def barrier(C, P, slot, gq_waits):
    for s, cval in gq_waits:
        wt(C, P, "gpsimd", s, cval)
    cf = C.GQ.inc(16)
    i0 = [("op", lambda e, s=C.GQ.h, sl=slot: e.dma_start(
        out=C.SH_FLAGS[0:1, sl, 0:1], in_=C.rmt_sb.ap()[0:1, 2:3]).then_inc(s, 16))]
    i1 = [("op", lambda e, s=C.GQ.h, sl=slot: e.dma_start(
        out=C.SH_FLAGS[1:2, sl, 0:1], in_=C.rmt_sb.ap()[0:1, 2:3]).then_inc(s, 16))]
    P.br("gpsimd", "memb", i0, i1)
    wt(C, P, "gpsimd", C.GQ, cf)

    def _poll(e, sl=slot):
        R = C.GREGS

        def cond():
            with e.If_eq(R["memb"], 0):
                e.dma_start(out=C.pf_sb.ap()[0:1, 0:1], in_=C.SH_FLAGS[1:2, sl, 0:1]).then_inc(C.PL.h, 16)
            with e.If_ne(R["memb"], 0):
                e.dma_start(out=C.pf_sb.ap()[0:1, 0:1], in_=C.SH_FLAGS[0:1, sl, 0:1]).then_inc(C.PL.h, 16)
            e.reg_alu(R["plcnt"], R["plcnt"], 16, ALU.add)
            e.wait_ge(C.PL.h, R["plcnt"])
            e.reg_load(R["pf"], C.pf_sb.ap()[0:1, 0:1])
            e.reg_alu(R["cond"], R["pf"], R["tok"], ALU.not_equal)
            return R["cond"]

        with e.While(cond):
            pass

    P.em("gpsimd", _poll)
    return op(C, P, "gpsimd", lambda e: e.memset(C.gz_sb.ap(), 0), C.BAR, 1)


def plan_phase0(C, P, role):
    nln = LN_N[role]
    nblk = nln // 128
    x0 = C.x0_sb.ap()
    xn = C.xn_sb.ap()
    sq0 = C.sq0_sb.ap()
    st = C.tiny_sb.ap()

    vlast = {0: 0, 1: 0}
    alast = {0: 0, 1: 0}
    evlast = {0: 0, 1: 0}
    stlast = {0: 0, 1: 0}
    cload = []
    tlast = 0
    c_ss_prev = 0
    xntg = C.xnt_sb.ap().rearrange("p (g c t) -> p g c t", g=2, t=CH)
    dstr = C.SH_XNT.rearrange("(c p) t -> p c t", p=128)
    for i in range(nblk):
        par = i % 2
        if i == 0:
            cload.append(dma(C, P, "sync", C.LD, x0[:, 0:D], C.x_ln[0:128, :]))
        if i + 1 < nblk:
            npar = (i + 1) % 2
            wt(C, P, "sync", C.VS, vlast[npar])
            cload.append(dma(C, P, "sync", C.LD, x0[:, npar * D:npar * D + D],
                             C.x_ln[128 * (i + 1):128 * (i + 1) + 128, :]))
        xv = x0[:, par * D:par * D + D]
        wt(C, P, "vector", C.LD, cload[i])
        bnst = C.bnst0_sb.ap()
        for q in range(4):
            vop(C, P, lambda e, xv=xv, q=q: e.bn_stats(
                bnst[:, 6 * q:6 * q + 6], xv[:, 512 * q:512 * q + 512]))
        cvar = vop(C, P, lambda e: e.bn_aggr(st[:, 0:2], bnst))
        alast[par] = 0
        wt(C, P, "scalar", C.VS, cvar)
        aop(C, P, lambda e: e.activation(st[:, 2:3], st[:, 1:2], AF.Ln, bias=C.eps_sb.ap()[:, 0:1]))
        crs = aop(C, P, lambda e: e.activation(st[:, 3:4], st[:, 2:3], AF.Exp, scale=-0.5))
        wt(C, P, "vector", C.AS, crs)
        vop(C, P, lambda e: e.tensor_tensor(st[:, 2:3], st[:, 0:1], st[:, 3:4], ALU.mult))
        wt(C, P, "vector", C.TS, tlast)
        cxn = vop(C, P, lambda e, xv=xv: e.tensor_scalar(
            xn, xv, st[:, 3:4], st[:, 2:3], ALU.mult, ALU.subtract))
        vlast[par] = cxn
        gslot = (i // 4) % 2
        toff = (i % 4) * 128
        for half in range(2):
            pb = C.PBF[half].ap()
            wt(C, P, "tensor", C.VS, cxn)
            wt(C, P, "tensor", C.AS, evlast[half])
            for cc in range(8):
                c = half * 8 + cc
                op(C, P, "tensor", lambda e, c=c, cc=cc, pb=pb: e.transpose(
                    pb[:, 128 * cc:128 * cc + 128], xn[:, 128 * c:128 * c + 128], C.idb_sb.ap()),
                   C.TS if cc == 7 else None)
            ct = C.TS.n
            wt(C, P, "scalar", C.TS, ct)
            wt(C, P, "scalar", C.STQ, stlast[gslot])
            ce = aop(C, P, lambda e, pb=pb, half=half, gslot=gslot, toff=toff: e.copy(
                xntg[:, gslot, 8 * half:8 * half + 8, toff:toff + 128],
                pb.rearrange("p (c t) -> p c t", t=128)))
            evlast[half] = ce
        tlast = C.TS.n
        if i % 4 == 3 or i == nblk - 1:
            tg0 = (i // 4) * CH
            tglen = (i % 4 + 1) * 128
            wt(C, P, "scalar", C.AS, C.AS.n)
            cst = C.STQ.inc(16)
            i0 = [("op", lambda e, tg0=tg0, tglen=tglen, gslot=gslot, s=C.STQ.h: e.dma_start(
                out=dstr[:, :, tg0:tg0 + tglen],
                in_=xntg[:, gslot, :, 0:tglen]).then_inc(s, 16))]
            i1 = [("op", lambda e, tg0=tg0, tglen=tglen, gslot=gslot, nln=nln, s=C.STQ.h: e.dma_start(
                out=dstr[:, :, nln + tg0:nln + tg0 + tglen],
                in_=xntg[:, gslot, :, 0:tglen]).then_inc(s, 16))]
            P.br("scalar", "memb", i0, i1)
            stlast[gslot] = cst


def plan_phase1(C, P, role, limit=None):
    slots = OUTER_SLOTS if role == 0 else MIDDLE_SLOTS
    kvr = KV_RANGE[role]
    nsc = kvr // SC
    xsr = C.xs_sb.ap().rearrange("p (s c t) -> p s c t", s=2, t=SC)
    shx = C.SH_XNT.rearrange("(c p) t -> p c t", p=128)
    Kap = C.K_sb.ap()
    Vap = C.V_sb.ap()
    st = C.tiny_sb.ap()
    PS = C.PS6
    PAV = C.PAV

    barrier(C, P, 0, [(C.STQ, C.STQ.n)])
    bar0 = C.BAR.n

    wt(C, P, "vector", C.BAR, bar0)
    vset = vop(C, P, lambda e: e.memset(
        Vap.rearrange("p (k h d) -> p k h d", h=4, d=129)[:, :, :, 128:129], 1.0))

    srot = [0]
    exp_of_bank = {i: 0 for i in range(6)}
    nrm_last = 0
    ost_stored = 0
    pt_used = {i: 0 for i in range(4)}
    pm_used = {0: 0, 1: 0}
    xs_free = 0
    sqq_ts = {0: 0, 1: 0, 2: 0, 3: 0}
    kv_ev = 0
    kn_vread = 0

    for g in range(2):
        wt(C, P, "sync", C.STQ, C.c_wconv)
        wt(C, P, "sync", C.BAR, bar0)
        wt(C, P, "sync", C.TS, xs_free)
        ck = 0
        for c in range(NDC):
            ck = dma(C, P, "sync", C.LD, C.wkv_sb.ap()[:, 1024 * c:1024 * c + 512],
                     C.MY_WQKV[128 * c:128 * c + 128, 1024 + 512 * g:1024 + 512 * g + 512])
            ck = dma(C, P, "sync", C.LD, C.wkv_sb.ap()[:, 1024 * c + 512:1024 * c + 1024],
                     C.MY_WQKV[128 * c:128 * c + 128, 2048 + 512 * g:2048 + 512 * g + 512])
        c_wkv = ck

        kv_ts = {0: xs_free, 1: xs_free}
        knts = {0: 0, 1: 0, 2: 0, 3: 0}
        vkn = 0
        for si in range(nsc):
            par = si % 2
            wt(C, P, "sync", C.TS, kv_ts[par])
            base = si * SC
            cl = 0
            for qq in range(4):
                cl = dma(C, P, "sync", C.LD, xsr[:, par, 4 * qq:4 * qq + 4, :],
                         shx[:, 4 * qq:4 * qq + 4, base:base + SC])
            wt(C, P, "tensor", C.LD, cl)
            if g == 0 and si == 0:
                wt(C, P, "tensor", C.AS, C.c_bias_ps_free)
            if si == 0:
                wt(C, P, "tensor", C.LD, c_wkv)
                wt(C, P, "tensor", C.VS, kn_vread)
                wt(C, P, "tensor", C.NRM, nrm_last)
                wt(C, P, "tensor", C.EXPS, max(exp_of_bank.values()))
            wt(C, P, "tensor", C.KVE, kv_ev)
            for h in range(4):
                for c in range(NDC):
                    op(C, P, "tensor", lambda e, h=h, c=c, par=par: e.matmul(
                        PS[h].ap()[:, 0:SC],
                        C.wkv_sb.ap()[:, 1024 * c + 128 * h:1024 * c + 128 * h + 128],
                        C.xs_sb.ap()[:, (par * NDC + c) * SC:(par * NDC + c) * SC + SC],
                        start=(c == 0), stop=(c == NDC - 1)), None)
            for tb in range(2):
                for c in range(NDC):
                    op(C, P, "tensor", lambda e, tb=tb, c=c, par=par: e.matmul(
                        PAV[tb].ap()[:, 0:512],
                        C.xs_sb.ap()[:, (par * NDC + c) * SC + 128 * tb:(par * NDC + c) * SC + 128 * tb + 128],
                        C.wkv_sb.ap()[:, 1024 * c + 512:1024 * c + 1024],
                        start=(c == 0), stop=False), None)
                op(C, P, "tensor", lambda e, tb=tb, g=g: e.matmul(
                    PAV[tb].ap()[:, 0:512], C.onr_sb.ap()[0:1, :],
                    C.bvrow_sb.ap()[0:1, 512 * g:512 * g + 512],
                    start=False, stop=True), C.TS if tb == 1 else None)
            kv_ts[par] = C.TS.n
            wt(C, P, "scalar", C.TS, kv_ts[par])
            for h in range(4):
                aop(C, P, lambda e, h=h, base=base, g=g: e.activation(
                    Kap[:, h * T + base:h * T + base + SC], PS[h].ap()[:, 0:SC],
                    AF.Identity, bias=C.bqkvc_sb.ap()[:, 8 + 4 * g + h:8 + 4 * g + h + 1]), chain=False)
            c_kev = C.AS.n
            for tb in range(2):
                kb = base // 128 + tb
                for h in range(4):
                    op(C, P, "scalar", lambda e, kb=kb, h=h, tb=tb: e.copy(
                        Vap[:, 516 * kb + 129 * h:516 * kb + 129 * h + 128],
                        PAV[tb].ap()[:, 128 * h:128 * h + 128]),
                        C.KVE if (tb == 1 and h == 3) else None)
            kv_ev = C.KVE.n
            wt(C, P, "vector", C.AS, c_kev)
            csq = 0
            for h in range(4):
                wt(C, P, "vector", C.TS, sqq_ts[h])
                csq = vop(C, P, lambda e, h=h, base=base: e.tensor_mul(
                    C.sqq_sb.ap()[:, h * SC:h * SC + SC],
                    Kap[:, h * T + base:h * T + base + SC],
                    Kap[:, h * T + base:h * T + base + SC]))
            wt(C, P, "tensor", C.VS, csq)
            for h in range(4):
                wt(C, P, "tensor", C.VS, knts[h])
                op(C, P, "tensor", lambda e, h=h: e.matmul(
                    PS[4 + h // 2].ap()[0:1, (h % 2) * SC:(h % 2) * SC + SC],
                    C.onc_sb.ap()[:, 0:1],
                    C.sqq_sb.ap()[:, h * SC:h * SC + SC],
                    start=True, stop=True), C.TS)
                sqq_ts[h] = C.TS.n
            wt(C, P, "vector", C.TS, C.TS.n)
            for h in range(4):
                vop(C, P, lambda e, h=h: e.reduce_max(
                    st[0:1, 4:5],
                    PS[4 + h // 2].ap()[0:1, (h % 2) * SC:(h % 2) * SC + SC], axis=AX.X))
                if si == 0:
                    vkn = vop(C, P, lambda e, h=h: e.tensor_copy(
                        C.kn_sb.ap()[0:1, h:h + 1], st[0:1, 4:5]))
                else:
                    vkn = vop(C, P, lambda e, h=h: e.tensor_tensor(
                        C.kn_sb.ap()[0:1, h:h + 1], C.kn_sb.ap()[0:1, h:h + 1],
                        st[0:1, 4:5], ALU.max))
                knts[h] = vkn
        kn_vread = vkn
        if limit in ("kvonly", "kv1g"):
            xs_free = max(kv_ts.values())
            continue
        wt(C, P, "sync", C.TS, max(kv_ts.values()))
        ck = 0
        for c in range(NDC):
            ck = dma(C, P, "sync", C.LD, C.wq_sb.ap()[:, 512 * c:512 * c + 512],
                     C.MY_WQKV[128 * c:128 * c + 128, 512 * g:512 * g + 512])
        c_wq = ck
        wt(C, P, "scalar", C.VS, vkn)
        aop(C, P, lambda e: e.activation(C.kn_sb.ap()[0:1, 8:12], C.kn_sb.ap()[0:1, 0:4], AF.Ln))
        ckn = aop(C, P, lambda e: e.activation(C.kn_sb.ap()[0:1, 4:8], C.kn_sb.ap()[0:1, 8:12], AF.Exp, scale=0.5))
        if limit in ("kv", "kn_novec", "kn_nolnexp"):
            continue

        q_consumed = max(kv_ts.values())
        avset = [(C.PS6[4], C.PS6[5]), (C.PAV[0], C.PAV[1])]
        nrm_of_set = {0: 0, 1: 0}
        ghead = [0]
        bank_extra = {i: [] for i in range(4)}

        def s_slot():
            sb = srot[0] % 4
            srot[0] += 1
            return sb

        for j in range(ATT_J):
            sl = slots[j]
            qbase = sl * CH
            nkt = 4 * (sl + 1)
            qpar = j % 2
            for half in range(2):
                wt(C, P, "sync", C.TS, q_consumed)
                for qq in range(4):
                    cl = dma(C, P, "sync", C.LD, xsr[:, half, 4 * qq:4 * qq + 4, :],
                             shx[:, 4 * qq:4 * qq + 4, qbase + SC * half:qbase + SC * half + SC])
            c_xq = cl
            wt(C, P, "tensor", C.LD, c_xq)
            if j == 0:
                wt(C, P, "tensor", C.LD, c_wq)
                wt(C, P, "tensor", C.VS, kn_vread)
                wt(C, P, "tensor", C.NRM, nrm_last)
            for h in range(4):
                wt(C, P, "tensor", C.EXPS, exp_of_bank[h])
                for sem, cnt in bank_extra[h]:
                    wt(C, P, "tensor", sem, cnt)
                bank_extra[h] = []
                for half in range(2):
                    for c in range(NDC):
                        last = (half == 1 and c == NDC - 1)
                        op(C, P, "tensor", lambda e, h=h, half=half, c=c: e.matmul(
                            C.PS6[h].ap()[:, half * SC:half * SC + SC],
                            C.wq_sb.ap()[:, 512 * c + 128 * h:512 * c + 128 * h + 128],
                            C.xs_sb.ap()[:, (half * NDC + c) * SC:(half * NDC + c) * SC + SC],
                            start=(c == 0), stop=(c == NDC - 1)),
                            C.TS if last else None)
                cqmm_h = C.TS.n
                wt(C, P, "scalar", C.TS, cqmm_h)
                aop(C, P, lambda e, h=h, qpar=qpar, g=g: e.activation(
                    C.Q_sb.ap()[:, (4 * qpar + h) * CH:(4 * qpar + h) * CH + CH],
                    C.PS6[h].ap()[:, 0:CH], AF.Identity,
                    bias=C.bqkvc_sb.ap()[:, 4 * g + h:4 * g + h + 1], scale=ATTN_MULT),
                    chain=False)
            cqmm = C.TS.n
            q_consumed = cqmm
            cqev = C.AS.n
            if g == 0 and j == 0:
                wt(C, P, "tensor", C.VS, vset)

            def emit_bounds_a(h):
                hp = h % 2
                wt(C, P, "vector", C.AS, cqev)
                wt(C, P, "vector", C.TS, max(sqq_ts[2 * hp], sqq_ts[2 * hp + 1]))
                qsl = C.Q_sb.ap()[:, (4 * qpar + h) * CH:(4 * qpar + h) * CH + CH]
                csq = vop(C, P, lambda e, qsl=qsl, hp=hp: e.tensor_mul(
                    C.sqq_sb.ap()[:, hp * CH:hp * CH + CH], qsl, qsl))
                sb = s_slot()
                wt(C, P, "tensor", C.EXPS, exp_of_bank[sb])
                for sem, cnt in bank_extra[sb]:
                    wt(C, P, "tensor", sem, cnt)
                bank_extra[sb] = []
                wt(C, P, "tensor", C.VS, csq)
                op(C, P, "tensor", lambda e, sb=sb, hp=hp: e.matmul(
                    C.PS6[sb].ap()[0:1, 0:CH], C.onc_sb.ap()[:, 0:1],
                    C.sqq_sb.ap()[:, hp * CH:hp * CH + CH],
                    start=True, stop=True), C.TS)
                cts = C.TS.n
                sqq_ts[2 * hp] = cts
                sqq_ts[2 * hp + 1] = cts
                wt(C, P, "vector", C.TS, cts)
                vop(C, P, lambda e, sb=sb: e.reduce_max(
                    st[0:1, 0:1], C.PS6[sb].ap()[0:1, 0:CH], axis=AX.X))
                wt(C, P, "scalar", C.VS)
                wt(C, P, "scalar", C.AS, ckn)
                aop(C, P, lambda e: e.activation(st[0:1, 1:2], st[0:1, 0:1], AF.Ln))
                caq = aop(C, P, lambda e: e.activation(st[0:1, 2:3], st[0:1, 1:2], AF.Exp, scale=0.5))
                wt(C, P, "vector", C.AS, caq)
                vop(C, P, lambda e, h=h: e.tensor_tensor(
                    st[0:1, 3:4], st[0:1, 2:3], C.kn_sb.ap()[0:1, 4 + h:5 + h], ALU.mult))
                vop(C, P, lambda e: e.tensor_scalar(
                    st[0:1, 0:1], st[0:1, 3:4], -1.0, -1.0, ALU.mult, ALU.add))
                vnb = vop(C, P, lambda e, h=h: e.tensor_copy(
                    C.tinyb_sb.ap()[0:1, h:h + 1], st[0:1, 0:1]))
                bank_extra[sb].append((C.VS, C.VS.n))
                return vnb

            def emit_bounds_b(h, vnb):
                sb2 = s_slot()
                wt(C, P, "tensor", C.EXPS, exp_of_bank[sb2])
                for sem, cnt in bank_extra[sb2]:
                    wt(C, P, "tensor", sem, cnt)
                bank_extra[sb2] = []
                wt(C, P, "tensor", C.VS, vnb)
                op(C, P, "tensor", lambda e, sb2=sb2, h=h: e.matmul(
                    C.PS6[sb2].ap()[:, 256 + h:256 + h + 1], C.onr_sb.ap()[0:1, :],
                    C.tinyb_sb.ap()[0:1, h:h + 1], start=True, stop=True), C.TS)
                bts = C.TS.n
                wt(C, P, "scalar", C.TS, bts)
                cnb = aop(C, P, lambda e, qpar=qpar, h=h, sb2=sb2: e.copy(
                    C.negb_sb.ap()[:, 4 * qpar + h:4 * qpar + h + 1],
                    C.PS6[sb2].ap()[:, 256 + h:256 + h + 1]))
                bank_extra[sb2].append((C.AS, cnb))
                return cnb

            cnb_of = {}
            cnb_of[0] = emit_bounds_b(0, emit_bounds_a(0))
            if limit == "tap45":
                wt(C, P, "gpsimd", C.AS, cnb_of[0])
                wt(C, P, "gpsimd", C.KVE, C.KVE.n)
                dma(C, P, "gpsimd", C.GQ, C.dbg[0:128, 0:2048], C.Q_sb.ap()[:, 0:2048])
                dma(C, P, "gpsimd", C.GQ, C.dbg[128:256, 0:16],
                    C.negb_sb.ap().bitcast(BF16)[:, 0:16])
                dma(C, P, "gpsimd", C.GQ, C.dbg[256:384, 0:2048], C.K_sb.ap()[:, 0:2048])
                dma(C, P, "gpsimd", C.GQ, C.dbg[384:512, 0:2048], C.V_sb.ap()[:, 0:2048])
                wt(C, P, "gpsimd", C.GQ, C.GQ.n)
                return

            for h in range(4):
                gh = ghead[0]
                ghead[0] += 1
                sset = gh % 2
                bk = avset[sset]
                wt(C, P, "scalar", C.AS, cnb_of[h])
                state = {}

                def emit_S(kt, h=h):
                    sb = s_slot()
                    wt(C, P, "tensor", C.EXPS, exp_of_bank[sb])
                    for sem, cnt in bank_extra[sb]:
                        wt(C, P, "tensor", sem, cnt)
                    bank_extra[sb] = []
                    if kt == 0:
                        wt(C, P, "tensor", C.NRM, nrm_of_set[sset])
                    op(C, P, "tensor", lambda e, sb=sb, h=h, kt=kt, qpar=qpar: e.matmul(
                        C.PS6[sb].ap()[:, 0:CH],
                        Kap[:, h * T + kt * 128:h * T + kt * 128 + 128],
                        C.Q_sb.ap()[:, (4 * qpar + h) * CH:(4 * qpar + h) * CH + CH],
                        start=True, stop=True), C.TS)
                    state[kt] = (sb, C.TS.n)

                def emit_tail(kt, h=h, sset=sset, bk=bk):
                    sb, cs = state.pop(kt)
                    diag = (kt >= nkt - 4)
                    rel = kt - (nkt - 4)
                    ppar = kt % 4
                    wt(C, P, "scalar", C.TS, cs)
                    wt(C, P, "scalar", C.AVS, pt_used[ppar])
                    ce = op(C, P, "scalar", lambda e, sb=sb, ppar=ppar, qpar=qpar, h=h: e.activation(
                        C.Pt_sb.ap()[:, ppar * CH:ppar * CH + CH], C.PS6[sb].ap()[:, 0:CH],
                        AF.Exp, bias=C.negb_sb.ap()[:, 4 * qpar + h:4 * qpar + h + 1]), C.EXPS)
                    exp_of_bank[sb] = ce
                    src_p = C.Pt_sb.ap()[:, ppar * CH:ppar * CH + CH]
                    if diag:
                        wt(C, P, "vector", C.EXPS, ce)
                        wt(C, P, "vector", C.AVS, pm_used[rel % 2])
                        cm = vop(C, P, lambda e, rel=rel, src_p=src_p: e.tensor_mul(
                            C.Pm_sb.ap()[:, (rel % 2) * CH:(rel % 2) * CH + CH],
                            src_p, C.masks_sb.ap()[:, rel * CH:rel * CH + CH]))
                        src_p = C.Pm_sb.ap()[:, (rel % 2) * CH:(rel % 2) * CH + CH]
                        wt(C, P, "tensor", C.VS, cm)
                    else:
                        wt(C, P, "tensor", C.EXPS, ce)
                    for b4 in range(4):
                        op(C, P, "tensor", lambda e, b4=b4, kt=kt, h=h, src_p=src_p, bk=bk: e.matmul(
                            bk[b4 // 2].ap()[:, 129 * (b4 % 2):129 * (b4 % 2) + 129],
                            src_p[:, 128 * b4:128 * b4 + 128],
                            Vap[:, 516 * kt + 129 * h:516 * kt + 129 * h + 129],
                            start=(kt == 0 and b4 % 2 == 0), stop=(kt == nkt - 1)),
                            C.AVS if b4 == 3 else None)
                    pt_used[ppar] = C.AVS.n
                    if diag:
                        pm_used[rel % 2] = C.AVS.n

                LOOK = 2
                vnb_next = [None]
                for kt in range(nkt + LOOK):
                    if kt < nkt:
                        emit_S(kt)
                    if kt == 0 and h < 3:
                        vnb_next[0] = emit_bounds_a(h + 1)
                    if kt - LOOK >= 0:
                        emit_tail(kt - LOOK)
                    if kt == 2 and h < 3:
                        cnb_of[h + 1] = emit_bounds_b(h + 1, vnb_next[0])
                cav = C.AVS.n
                wt(C, P, "vector", C.AVS, cav)
                wt(C, P, "vector", C.STQ, ost_stored)
                for b4 in range(4):
                    pbank = bk[b4 // 2].ap()[:, 129 * (b4 % 2):129 * (b4 % 2) + 129]
                    vop(C, P, lambda e, pbank=pbank: e.tensor_scalar(
                        st[:, 4:5], pbank[:, 128:129], 1e-30, None, ALU.add))
                    vop(C, P, lambda e: e.reciprocal(st[:, 5:6], st[:, 4:5]))
                    lastnrm = (b4 == 3)
                    wt(C, P, "vector", C.VS)
                    op(C, P, "vector", lambda e, b4=b4, h=h, pbank=pbank: e.tensor_scalar_mul(
                        C.Ost_sb.ap()[:, 512 * b4 + 128 * h:512 * b4 + 128 * h + 128],
                        pbank[:, 0:128], st[:, 5:6]),
                        C.NRM if lastnrm else C.VS)
                nrm_of_set[sset] = C.NRM.n
                nrm_last = C.NRM.n
                wt(C, P, "vector", C.NRM, nrm_last)
            wt(C, P, "scalar", C.NRM, C.NRM.n)
            cst = C.STQ.inc(16)
            i0 = [("op", lambda e, j=j, g=g, s=C.STQ.h: e.dma_start(
                out=C.SH_O[512 * j:512 * j + 512, 512 * g:512 * g + 512].rearrange(
                    "(b p) f -> p b f", p=128),
                in_=C.Ost_sb.ap().rearrange("p (b f) -> p b f", f=512)).then_inc(s, 16))]
            i1 = [("op", lambda e, j=j, g=g, s=C.STQ.h: e.dma_start(
                out=C.SH_O[512 * j:512 * j + 512, 1024 + 512 * g:1024 + 512 * g + 512].rearrange(
                    "(b p) f -> p b f", p=128),
                in_=C.Ost_sb.ap().rearrange("p (b f) -> p b f", f=512)).then_inc(s, 16))]
            P.br("scalar", "memb", i0, i1)
            ost_stored = cst

        xs_free = q_consumed

    wt(C, P, "tensor", C.NRM, C.NRM.n)
    op(C, P, "tensor", lambda e: e.transpose(
        C.PBF[0].ap()[:, 0:128], C.idb_sb.ap(), C.idb_sb.ap()), C.ARM, 1)
    wt(C, P, "scalar", C.AS)
    wt(C, P, "scalar", C.STQ, C.STQ.n)
    op(C, P, "scalar", lambda e: e.copy(C.tinyb_sb.ap()[0:1, 4:5], C.tinyb_sb.ap()[0:1, 5:6]), C.ARM, 1)
    wt(C, P, "vector", C.VS)
    op(C, P, "vector", lambda e: e.tensor_copy(st[0:1, 6:7], st[0:1, 7:8]), C.ARM, 1)
    barrier(C, P, 1, [(C.STQ, C.STQ.n)])


def plan_phase2(C, P):
    st = C.tiny_sb.ap()
    wproj = C.wproj_sb.ap()
    wbase = C.wbase_sb.ap()
    wt(C, P, "sync", C.ARM, 3)
    wt(C, P, "sync", C.BAR, C.BAR.n)
    wt(C, P, "sync", C.STQ, C.c_smalls_stq)
    dma(C, P, "sync", C.LD2, C.aall_sb.ap(), C.D_AALL)
    dma(C, P, "sync", C.LD2, C.ball_sb.ap(), C.D_BALL)
    dma(C, P, "sync", C.LD2, C.wrout_sb.ap(), C.D_WROUT)
    dma(C, P, "sync", C.LD2, C.bias2_sb.ap()[0:1, :], C.D_BIAS[0:1, :])
    dma(C, P, "sync", C.LD2, C.bias2_sb.ap()[64:65, :], C.D_BIAS[1:2, :])
    dma(C, P, "sync", C.LD2, C.bias2_sb.ap()[32:33, 0:8], C.D_BIAS[2:3, 0:8])
    dma(C, P, "sync", C.LD2, C.ones3_sb.ap()[32:33, :], C.ones_row)
    dma(C, P, "sync", C.LD2, C.ones3_sb.ap()[64:65, :], C.ones_row)
    dma(C, P, "sync", C.LD2, C.sel_sb.ap(), C.sel_c)
    c_smalls = dma(C, P, "sync", C.LD2, C.w7_sb.ap()[6:7, :], C.ones_row)
    cw = 0
    for c in range(NDC):
        cw = dma(C, P, "sync", C.LD2, wproj[:, 2048 * c:2048 * c + 2048],
                 C.SH_WP[128 * c:128 * c + 128, :])
    c_wp = cw
    for c in range(NDC):
        cw = dma(C, P, "sync", C.LD2, wbase[:, 2048 * c:2048 * c + 2048],
                 C.SH_WB[128 * c:128 * c + 128, :])
    c_wb = cw

    xres = C.xres_sb.ap()
    osb = C.o_sb.ap()[:, 0:D]
    ot = C.ot_sb.ap()
    big = C.big_sb.ap()
    x2 = C.x2_sb.ap()
    xn2 = C.xn2_sb.ap()
    xn2t = C.xn2t_sb.ap()
    outb = C.out_sb.ap()
    g8 = C.g8_sb.ap()

    bnst = C.bnst_sb.ap()

    def ln_stats(src, vw):
        for s, v in vw:
            wt(C, P, "vector", s, v)
        for q in range(4):
            v2op(C, P, lambda e, q=q: e.bn_stats(bnst[:, 6 * q:6 * q + 6],
                                                 src[:, 512 * q:512 * q + 512]))
        cv = v2op(C, P, lambda e: e.bn_aggr(st[:, 0:2], bnst))
        wt(C, P, "scalar", C.VS2, cv)
        a2op(C, P, lambda e: e.activation(st[:, 2:3], st[:, 1:2], AF.Ln, bias=C.eps_sb.ap()[:, 0:1]))
        crs = a2op(C, P, lambda e: e.activation(st[:, 3:4], st[:, 2:3], AF.Exp, scale=-0.5))
        wt(C, P, "vector", C.AS2, crs)
        v2op(C, P, lambda e: e.tensor_tensor(st[:, 2:3], st[:, 0:1], st[:, 3:4], ALU.mult))

    S = {
        "c_xload": {}, "c_oload": {}, "c_otr": {}, "tr_ev": {0: 0, 1: 0},
        "otr_ev": {0: 0, 1: 0}, "xn2_ev": {0: 0, 1: 0},
        "projev": {0: 0, 1: 0}, "ln4_norm": 0, "c_ffn": {},
        "cLN2": {}, "cLN4": {}, "c_add": {}, "cxn2": {}, "c_xn2tr": {},
        "c_rout": {}, "c_hmm": {}, "c_hwf_ev": 0, "cw2": 0, "cge": 0,
        "chw": {}, "c_ost": {}, "c_xntr_done": 0,
    }

    def seg_oload(i):
        if i > 0:
            wt(C, P, "sync", C.T2, S["c_otr"][i - 1])
        S["c_oload"][i] = C.LD2.inc(16)
        i0 = [("op", lambda e, i=i, s=C.LD2.h: e.dma_start(
            out=osb, in_=C.SH_O[128 * i:128 * i + 128, :]).then_inc(s, 16))]
        i1 = [("op", lambda e, i=i, s=C.LD2.h: e.dma_start(
            out=osb, in_=C.SH_O[2048 + 128 * i:2048 + 128 * i + 128, :]).then_inc(s, 16))]
        P.br("sync", "memb", i0, i1)

    def seg_xload(i):
        if i > 0:
            wt(C, P, "sync", C.VS2, S["c_add"].get(i - 1, 0))
        S["c_xload"][i] = dma(C, P, "sync", C.LD2, xres, C.x_res[128 * i:128 * i + 128, :])

    def seg_otr(i):
        wt(C, P, "tensor", C.LD2, S["c_oload"][i])
        if i == 0:
            wt(C, P, "tensor", C.LD2, c_wp)
            wt(C, P, "tensor", C.LD2, c_smalls)
        for half in range(2):
            pb = C.PBF[half].ap()
            wt(C, P, "tensor", C.AS2, S["tr_ev"][half])
            for cc in range(8):
                c = half * 8 + cc
                op(C, P, "tensor", lambda e, c=c, cc=cc, pb=pb: e.transpose(
                    pb[:, 128 * cc:128 * cc + 128], osb[:, 128 * c:128 * c + 128],
                    C.idb_sb.ap()), C.T2 if cc == 7 else None)
            ct = C.T2.n
            wt(C, P, "scalar", C.T2, ct)
            ev = a2op(C, P, lambda e, pb=pb, half=half: e.copy(
                ot[:, 1024 * half:1024 * half + 1024], pb), chain=False)
            S["tr_ev"][half] = ev
            S["otr_ev"][half] = ev
        S["c_otr"][i] = C.T2.n

    def seg_xn2tr_tr(i):
        wt(C, P, "tensor", C.VS2, S["cxn2"][i])
        for half in range(2):
            pb = C.PBF[half].ap()
            wt(C, P, "tensor", C.AS2, S["tr_ev"][half])
            for cc in range(8):
                c = half * 8 + cc
                op(C, P, "tensor", lambda e, c=c, cc=cc, pb=pb: e.transpose(
                    pb[:, 128 * cc:128 * cc + 128], xn2[:, 128 * c:128 * c + 128],
                    C.idb_sb.ap()), C.T2 if (half, cc) == (1, 7) else None)
        S["c_xntr_done"] = C.T2.n
        S["c_xn2tr"][i] = C.T2.n

    def seg_xn2tr_ev(gate):
        for half in range(2):
            pb = C.PBF[half].ap()
            wt(C, P, "scalar", C.T2, gate)
            ev = a2op(C, P, lambda e, pb=pb, half=half: e.copy(
                xn2t[:, 1024 * half:1024 * half + 1024], pb), chain=False)
            S["tr_ev"][half] = ev
            S["xn2_ev"][half] = ev

    def seg_proj(i):
        for r in range(2):
            wt(C, P, "tensor", C.AS2, S["projev"][1] if r == 0 else S["projev"][0])
            for c in range(NDC):
                if r == 0 and c in (0, 8):
                    wt(C, P, "tensor", C.AS2, S["otr_ev"][c // 8])
                for fi in range(2):
                    fc = 2 * r + fi
                    op(C, P, "tensor", lambda e, c=c, fc=fc, fi=fi: e.matmul(
                        C.PS6[fi].ap()[:, 0:512],
                        ot[:, 128 * c:128 * c + 128],
                        wproj[:, 2048 * c + 512 * fc:2048 * c + 512 * fc + 512],
                        start=(c == 0), stop=False), None)
            for fi in range(2):
                fc = 2 * r + fi
                op(C, P, "tensor", lambda e, fc=fc, fi=fi: e.matmul(
                    C.PS6[fi].ap()[:, 0:512], C.onr_sb.ap()[0:1, :],
                    C.bias2_sb.ap()[0:1, 512 * fc:512 * fc + 512],
                    start=False, stop=True), C.T2 if fi == 1 else None)
            cpr = C.T2.n
            if r == 0 and i >= 1:
                seg_xn2tr_tr(i - 1)
            if r == 1:
                if i >= 1:
                    seg_xn2tr_ev(S["c_xn2tr"][i - 1])
                if i + 1 <= 15:
                    seg_otr(i + 1)
            wt(C, P, "scalar", C.T2, cpr)
            if r == 0:
                wt(C, P, "scalar", C.VS2, S["cLN2"].get(i - 1, 0))
            for fi in range(2):
                fc = 2 * r + fi
                ev = a2op(C, P, lambda e, fc=fc, fi=fi: e.copy(
                    big[:, 512 * fc:512 * fc + 512], C.PS6[fi].ap()[:, 0:512]), chain=False)
            S["projev"][r] = ev
        if i + 2 <= 15:
            seg_oload(i + 2)

    def seg_ln23(i):
        sl = i % 2
        pe = S["projev"][1]
        ln_stats(big, [(C.AS2, pe)])
        wt(C, P, "vector", C.LD2, S["c_xload"][i])
        v2op(C, P, lambda e, sl=sl: e.tensor_scalar(
            x2[:, sl * D:sl * D + D], big, st[:, 3:4], st[:, 2:3], ALU.mult, ALU.subtract))
        S["cLN2"][i] = C.VS2.n
        S["c_add"][i] = v2op(C, P, lambda e, sl=sl: e.tensor_add(
            x2[:, sl * D:sl * D + D], x2[:, sl * D:sl * D + D], xres))
        ln_stats(x2[:, sl * D:sl * D + D], [])
        wt(C, P, "vector", C.T2, S["c_xn2tr"].get(i - 1, 0))
        S["cxn2"][i] = v2op(C, P, lambda e, sl=sl: e.tensor_scalar(
            xn2, x2[:, sl * D:sl * D + D], st[:, 3:4], st[:, 2:3], ALU.mult, ALU.subtract))

    def seg_rh(i):
        if i == 0:
            wt(C, P, "tensor", C.LD2, c_wb)
        if i > 0:
            wt(C, P, "tensor", C.AS2, S["c_hwf_ev"])
            wt(C, P, "tensor", C.VS2, S["ln4_norm"])
        for c in range(NDC):
            if c in (0, 8):
                wt(C, P, "tensor", C.AS2, S["xn2_ev"][c // 8])
            op(C, P, "tensor", lambda e, c=c: e.matmul(
                C.PAV[1].ap()[:, 0:8], xn2t[:, 128 * c:128 * c + 128],
                C.wrout_sb.ap()[:, 8 * c:8 * c + 8],
                start=(c == 0), stop=False), None)
        op(C, P, "tensor", lambda e: e.matmul(
            C.PAV[1].ap()[:, 0:8], C.ones3_sb.ap()[32:33, :], C.bias2_sb.ap()[32:33, 0:8],
            start=False, stop=True), C.T2)
        S["c_rout"][i] = C.T2.n
        for c in range(NDC):
            op(C, P, "tensor", lambda e, c=c: e.matmul(
                C.PAV[0].ap()[0:PACK, 0:128],
                C.aall_sb.ap()[:, PACK * c:PACK * c + PACK], xn2t[:, 128 * c:128 * c + 128],
                start=(c == 0), stop=(c == NDC - 1)), C.T2 if c == NDC - 1 else None)
        S["c_hmm"][i] = C.T2.n

    def seg_softmax_v(i):
        wt(C, P, "vector", C.T2, S["c_rout"][i])
        v2op(C, P, lambda e: e.reduce_max(st[:, 4:5], C.PAV[1].ap()[:, 0:8], axis=AX.X))
        cgm = v2op(C, P, lambda e: e.tensor_scalar_mul(st[:, 7:8], st[:, 4:5], -1.0))
        wt(C, P, "scalar", C.VS2, cgm)
        S["cge"] = a2op(C, P, lambda e: e.activation(g8, C.PAV[1].ap()[:, 0:8], AF.Exp, bias=st[:, 7:8]))
        wt(C, P, "vector", C.AS2, S["cge"])
        v2op(C, P, lambda e: e.reduce_sum(st[:, 5:6], g8, axis=AX.X))
        v2op(C, P, lambda e: e.reciprocal(st[:, 6:7], st[:, 5:6]))
        v2op(C, P, lambda e: e.tensor_scalar_mul(C.e2_sb.ap(), g8, st[:, 6:7]))
        v2op(C, P, lambda e: e.reduce_max(st[:, 4:5], C.e2_sb.ap(), axis=AX.X))
        v2op(C, P, lambda e: e.tensor_scalar(C.e1_sb.ap(), C.e2_sb.ap(), st[:, 4:5], None, ALU.is_equal))
        v2op(C, P, lambda e: e.scalar_tensor_tensor(g8, C.e1_sb.ap(), NEG_BIG, C.e2_sb.ap(), ALU.mult, ALU.add))
        v2op(C, P, lambda e: e.reduce_max(st[:, 5:6], g8, axis=AX.X))
        v2op(C, P, lambda e: e.tensor_scalar(C.w2_sb.ap(), g8, st[:, 5:6], None, ALU.is_equal))
        v2op(C, P, lambda e: e.tensor_add(g8, C.e1_sb.ap(), C.w2_sb.ap()))
        S["cw2"] = v2op(C, P, lambda e: e.tensor_mul(C.w2_sb.ap(), C.e2_sb.ap(), g8))

    def seg_ffn(i):
        FB = [C.PS6[2], C.PS6[3], C.PAV[0], C.PAV[1]]
        wt(C, P, "tensor", C.VS2, S["ln4_norm"])
        for c in range(NDC):
            for fi in range(2):
                op(C, P, "tensor", lambda e, c=c, fi=fi, bk=FB[fi]: e.matmul(
                    bk.ap()[:, 0:512],
                    xn2t[:, 128 * c:128 * c + 128],
                    wbase[:, 2048 * c + 512 * fi:2048 * c + 512 * fi + 512],
                    start=(c == 0), stop=False), None)
        wt(C, P, "tensor", C.VS2, S["cw2"])
        wt(C, P, "tensor", C.AS2, S["cge"])
        op(C, P, "tensor", lambda e: e.transpose(
            C.PAV[1].ap()[0:8, 128:256], C.w2_sb.ap(), C.idf_sb.ap()), C.T2)
        cwt = C.T2.n
        wt(C, P, "scalar", C.T2, cwt)
        cw7 = a2op(C, P, lambda e: e.copy(C.w7_sb.ap()[0:6, :], C.PAV[1].ap()[0:6, 128:256]))
        wt(C, P, "tensor", C.AS2, cw7)
        op(C, P, "tensor", lambda e: e.matmul(
            C.PAV[0].ap()[0:PACK, 128:256], C.sel_sb.ap()[0:7, :], C.w7_sb.ap()[0:7, :],
            start=True, stop=True), C.T2)
        c_wfull = C.T2.n
        wt(C, P, "scalar", C.T2, c_wfull)
        a2op(C, P, lambda e: e.copy(C.h_sb.ap(), C.PAV[0].ap()[0:PACK, 0:128]), chain=False)
        S["c_hwf_ev"] = a2op(C, P, lambda e: e.copy(
            C.wf_sb.ap(), C.PAV[0].ap()[0:PACK, 128:256]), chain=False)
        wt(C, P, "vector", C.AS2, S["c_hwf_ev"])
        S["chw"][i] = v2op(C, P, lambda e: e.tensor_mul(C.hw_sb.ap(), C.h_sb.ap(), C.wf_sb.ap()))
        if i + 1 <= 15:
            seg_ln23(i + 1)
        wt(C, P, "tensor", C.AS2, S["c_hwf_ev"])
        for c in range(NDC):
            for fi in range(2, 4):
                op(C, P, "tensor", lambda e, c=c, fi=fi, bk=FB[fi]: e.matmul(
                    bk.ap()[:, 0:512],
                    xn2t[:, 128 * c:128 * c + 128],
                    wbase[:, 2048 * c + 512 * fi:2048 * c + 512 * fi + 512],
                    start=(c == 0), stop=False), None)
        wt(C, P, "tensor", C.VS2, S["chw"][i])
        for fi in range(4):
            op(C, P, "tensor", lambda e, fi=fi, bk=FB[fi]: e.matmul(
                bk.ap()[:, 0:512], C.hw_sb.ap(),
                C.ball_sb.ap()[:, 512 * fi:512 * fi + 512],
                start=False, stop=False), None)
        for fi in range(4):
            op(C, P, "tensor", lambda e, fi=fi, bk=FB[fi]: e.matmul(
                bk.ap()[:, 0:512], C.ones3_sb.ap()[64:65, :],
                C.bias2_sb.ap()[64:65, 512 * fi:512 * fi + 512],
                start=False, stop=True), C.T2 if fi == 3 else None)
        S["c_ffn"][i] = C.T2.n

    def seg_ln4_out(i):
        FB = [C.PS6[2], C.PS6[3], C.PAV[0], C.PAV[1]]
        sl = i % 2
        cf = S["c_ffn"][i]
        wt(C, P, "vector", C.T2, cf)
        for fi in range(4):
            v2op(C, P, lambda e, fi=fi, bk=FB[fi]: e.bn_stats(
                bnst[:, 6 * fi:6 * fi + 6], bk.ap()[:, 0:512]))
        cv = v2op(C, P, lambda e: e.bn_aggr(st[:, 0:2], bnst))
        wt(C, P, "scalar", C.VS2, cv)
        a2op(C, P, lambda e: e.activation(st[:, 2:3], st[:, 1:2], AF.Ln, bias=C.eps_sb.ap()[:, 0:1]))
        crs = a2op(C, P, lambda e: e.activation(st[:, 3:4], st[:, 2:3], AF.Exp, scale=-0.5))
        wt(C, P, "vector", C.AS2, crs)
        v2op(C, P, lambda e: e.tensor_tensor(st[:, 2:3], st[:, 0:1], st[:, 3:4], ALU.mult))
        wt(C, P, "vector", C.ST2, S["c_ost"].get(i - 1, 0))
        cn = 0
        for fi in (2, 3, 0, 1):
            cn = v2op(C, P, lambda e, fi=fi, bk=FB[fi]: e.tensor_scalar(
                outb[:, 512 * fi:512 * fi + 512], bk.ap()[:, 0:512],
                st[:, 3:4], st[:, 2:3], ALU.mult, ALU.subtract))
        S["ln4_norm"] = cn
        S["cLN4"][i] = cn
        cout = v2op(C, P, lambda e, sl=sl: e.tensor_add(outb, outb, x2[:, sl * D:sl * D + D]))
        wt(C, P, "gpsimd", C.VS2, cout)
        S["c_ost"][i] = dma(C, P, "gpsimd", C.ST2, C.out[128 * i:128 * i + 128, :], outb)

    seg_xload(0)
    seg_oload(0)
    seg_otr(0)
    seg_oload(1)
    for k in range(-1, 16):
        if k + 1 <= 15:
            seg_proj(k + 1)
        if k == -1:
            seg_ln23(0)
        if k >= 0:
            if k == 15:
                seg_xn2tr_tr(k)
                seg_xn2tr_ev(S["c_xntr_done"])
            seg_rh(k)
            seg_softmax_v(k)
            seg_ffn(k)
            seg_ln4_out(k)
        if k + 2 <= 15:
            seg_xload(k + 2)

    wt(C, P, "gpsimd", C.ST2, C.ST2.n)


def finish_debug(C):
    _close_block(C)


def _close_block(C):
    with C.nc.Block() as block:
        def run(ename):
            def f(e):
                if ename == "sync":
                    e.dma_start(out=C.rmt_sb.ap(), in_=C.rmt).then_inc(C.LD.h, 16)
                e.wait_ge(C.LD.h, C.c_rmt)
                r_role = e.alloc_register()
                e.reg_load(r_role, C.rmt_sb.ap()[0:1, 0:1])
                r_memb = e.alloc_register()
                e.reg_load(r_memb, C.rmt_sb.ap()[0:1, 1:2])
                regs = {"role": r_role, "memb": r_memb}
                if ename == "gpsimd":
                    C.GREGS["memb"] = r_memb
                    r_tok = e.alloc_register()
                    e.reg_load(r_tok, C.rmt_sb.ap()[0:1, 2:3])
                    r_pf = e.alloc_register()
                    r_plcnt = e.alloc_register()
                    r_cond = e.alloc_register()
                    e.reg_mov(r_plcnt, 0)
                    C.GREGS.update(tok=r_tok, pf=r_pf, plcnt=r_plcnt, cond=r_cond)
                _play(e, C.P.q[ename], regs)
            return f
        block.sync(run("sync"))
        block.tensor(run("tensor"))
        block.scalar(run("scalar"))
        block.vector(run("vector"))
        block.gpsimd(run("gpsimd"))
    for g in C._sem_guards:
        g.__exit__(None, None, None)


def build():
    C = make_context()
    alloc_phase1(C)
    C.GREGS = {}
    plan_prolog(C)

    semnames = C._semnames
    dl = DEBUG_LEVEL
    if dl == 1:
        finish_debug(C)
        return C
    limit = {2: "phase0", 25: "bar0", 26: "kvonly", 261: "kv1g", 27: "wq",
             28: "kn_novec", 29: "kn_nolnexp", 3: "kv",
             42: "bar0", 43: "kv", 44: None, 45: "tap45"}.get(dl, None)
    snap = {k: getattr(C, k).n for k in semnames}
    Pa = Plan()
    plan_phase0(C, Pa, 0)
    if limit != "phase0":
        plan_phase1(C, Pa, 0, limit=limit)
    cnt_a = {k: getattr(C, k).n for k in semnames}
    for k in semnames:
        getattr(C, k).n = snap[k]
    Pb = Plan()
    plan_phase0(C, Pb, 1)
    if limit != "phase0":
        plan_phase1(C, Pb, 1, limit=limit)
    cnt_b = {k: getattr(C, k).n for k in semnames}
    if limit is None:
        assert cnt_a["ARM"] == cnt_b["ARM"] == snap["ARM"] + 3, (cnt_a["ARM"], cnt_b["ARM"])
        assert cnt_a["BAR"] == cnt_b["BAR"], (cnt_a["BAR"], cnt_b["BAR"])
    if dl == 45:
        C.P.splice_role(Pa, Pb)
        finish_debug(C)
        return C
    for k in semnames:
        getattr(C, k).n = max(cnt_a[k], cnt_b[k])
    C.P.splice_role(Pa, Pb)
    P = C.P
    if dl == 42:
        wt(C, P, "gpsimd", C.BAR, min(cnt_a["BAR"], cnt_b["BAR"]))
        i0 = [("op", lambda e, s=C.GQ.h: e.dma_start(
            out=C.dbg.rearrange("(c p) t -> p c t", p=128),
            in_=C.SH_XNT.rearrange("(c p) t -> p c t", p=128)[:, :, 0:2048]).then_inc(s, 16))]
        i1 = [("op", lambda e, s=C.GQ.h: e.dma_start(
            out=C.dbg.rearrange("(c p) t -> p c t", p=128),
            in_=C.SH_XNT.rearrange("(c p) t -> p c t", p=128)[:, :, 4096:4096 + 2048]).then_inc(s, 16))]
        C.GQ.inc(16)
        P.br("gpsimd", "memb", i0, i1)
        wt(C, P, "gpsimd", C.GQ, C.GQ.n)
        finish_debug(C)
        return C
    if dl == 43:
        wt(C, P, "gpsimd", C.BAR, min(cnt_a["BAR"], cnt_b["BAR"]))
        dma(C, P, "gpsimd", C.GQ, C.dbg[0:128, 0:2048], C.K_sb.ap()[:, 0:2048])
        dma(C, P, "gpsimd", C.GQ, C.dbg[128:256, 0:2048], C.V_sb.ap()[:, 0:2048])
        dma(C, P, "gpsimd", C.GQ, C.dbg[384:385, 0:16], C.kn_sb.ap().bitcast(BF16)[0:1, 0:32][0:1, 0:16])
        wt(C, P, "gpsimd", C.GQ, C.GQ.n)
        finish_debug(C)
        return C
    if dl == 44:
        wt(C, P, "gpsimd", C.BAR, min(cnt_a["BAR"], cnt_b["BAR"]))
        i0 = [("op", lambda e, s=C.GQ.h: e.dma_start(
            out=C.dbg, in_=C.SH_O[0:2048, :]).then_inc(s, 16))]
        i1 = [("op", lambda e, s=C.GQ.h: e.dma_start(
            out=C.dbg, in_=C.SH_O[2048:4096, :]).then_inc(s, 16))]
        C.GQ.inc(16)
        P.br("gpsimd", "memb", i0, i1)
        wt(C, P, "gpsimd", C.GQ, C.GQ.n)
        finish_debug(C)
        return C
    if dl in (2, 25, 26, 261, 27, 28, 29, 3, 4):
        finish_debug(C)
        return C

    for name in ("sqq_sb", "negb_sb", "Ost_sb", "Pm_sb", "Pt_sb", "Q_sb",
                 "wkv_sb", "xs_sb", "V_sb", "K_sb"):
        C.sbfree(name)
    C.sbuf("wproj_sb", (128, NDC * 2048), BF16)
    C.sbuf("wbase_sb", (128, NDC * 2048), BF16)
    C.sbuf("xres_sb", (128, D), F32)
    C.sbuf("o_sb", (128, D), BF16)
    C.sbuf("ot_sb", (128, D), BF16)
    C.sbuf("big_sb", (128, D), F32)
    C.sbuf("x2_sb", (128, 2 * D), F32)
    C.sbuf("xn2_sb", (128, D), BF16)
    C.sbuf("out_sb", (128, D), F32)
    C.sbuf("g8_sb", (128, 8), F32)
    C.sbuf("e1_sb", (128, 8), F32)
    C.sbuf("e2_sb", (128, 8), F32)
    C.sbuf("w2_sb", (128, 8), F32)
    C.sbuf("h_sb", (PACK, 128), BF16)
    C.sbuf("wf_sb", (PACK, 128), BF16)
    C.sbuf("hw_sb", (PACK, 128), BF16)
    C.sbuf("aall_sb", (128, NDC * PACK), BF16)
    C.sbuf("ball_sb", (PACK, D), BF16)
    C.sbuf("wrout_sb", (128, NDC * 8), BF16)
    C.sbuf("sel_sb", (8, PACK), BF16)
    C.sbuf("w7_sb", (8, 128), BF16)
    C.sbuf("bias2_sb", (128, D), BF16)
    C.sbuf("ones3_sb", (128, 128), BF16)
    C.sbuf("bnst_sb", (128, 24), F32)
    C.sbuf("xn2t_sb", (128, D), BF16)

    plan_phase2(C, C.P)
    _close_block(C)
    return C


_NC_CACHE = [None]
DEBUG_LEVEL = 0


def kernel(**L):
    xin = np.asarray(L["L0_attn_x"])
    w_qkv = np.asarray(L["L0_attn_w_qkv"])
    b_qkv = np.asarray(L["L0_attn_b_qkv"])
    w_proj = np.asarray(L["L0_attn_w_proj"])
    b_proj = np.asarray(L["L0_attn_b_proj"])
    W_base = np.asarray(L["L0_ffn_W_base"])
    b_base = np.asarray(L["L0_ffn_b_base"])
    A_s = np.asarray(L["L0_ffn_A_s"])
    B_s = np.asarray(L["L0_ffn_B_s"])
    A_r = np.asarray(L["L0_ffn_A_r"])
    B_r = np.asarray(L["L0_ffn_B_r"])
    W_router = np.asarray(L["L0_ffn_W_router"])
    b_router = np.asarray(L["L0_ffn_b_router"])

    if _NC_CACHE[0] is None:
        _NC_CACHE[0] = build()
    C = _NC_CACHE[0]

    token = int(np.random.randint(1, 2**31 - 1))
    in_maps = []
    rows_per_core = []
    for c in range(8):
        p, m = c // 2, c % 2
        b, role = p // 2, p % 2
        n = LN_N[role]
        xl = np.zeros((4096, D), dtype=np.float32)
        xl[:n] = xin[b, m * n:m * n + n]
        if role == 0:
            rows = np.r_[0:2048] if m == 0 else np.r_[2048:E1, E2:T]
        else:
            rows = np.r_[E1:E1 + 2048] if m == 0 else np.r_[E1 + 2048:E2]
        rows_per_core.append((b, rows))
        hcols = np.r_[1024 * m:1024 * m + 1024]
        wq = np.concatenate([w_qkv[:, hcols], w_qkv[:, 2048 + hcols], w_qkv[:, 4096 + hcols]], axis=1)
        bq = np.concatenate([b_qkv[hcols], b_qkv[2048 + hcols], b_qkv[4096 + hcols]])[None, :]
        in_maps.append({
            "x_ln": np.ascontiguousarray(xl),
            "x_res": np.ascontiguousarray(xin[b, rows]),
            "wqkv": np.ascontiguousarray(wq),
            "bqkv": np.ascontiguousarray(bq),
            "wproj_half": np.ascontiguousarray(w_proj[1024 * m:1024 * m + 1024]),
            "wbase_half": np.ascontiguousarray(W_base[1024 * m:1024 * m + 1024]),
            "bproj": np.ascontiguousarray(b_proj[None, :]),
            "bbase": np.ascontiguousarray(b_base[None, :]),
            "a_s": np.ascontiguousarray(A_s),
            "b_s": np.ascontiguousarray(B_s),
            "a_r": np.ascontiguousarray(A_r),
            "b_r": np.ascontiguousarray(B_r),
            "wrouter": np.ascontiguousarray(W_router),
            "brouter": np.ascontiguousarray(b_router[None, :]),
            "rmt": np.array([[role, m, token, 0]], dtype=np.int32),
        })

    res = run_bass_kernel_spmd(C.nc, in_maps, core_ids=list(range(8)))
    outv = np.zeros((B, T, D), dtype=np.float32)
    for c in range(8):
        b, rows = rows_per_core[c]
        outv[b, rows] = res.results[c]["out"]
    if DEBUG_LEVEL:
        kernel.dbg = [np.asarray(res.results[c]["dbg"]) for c in range(8)]
    return outv
